# version 09 (final)
# baseline (speedup 1.0000x reference)
import math
import numpy as np
import ml_dtypes
from contextlib import ExitStack

import concourse.bass as bass
import concourse.mybir as mybir
from concourse.bass_utils import run_bass_kernel_spmd

F32 = mybir.dt.float32
BF16 = mybir.dt.bfloat16
AF = mybir.ActivationFunctionType
ALU = mybir.AluOpType
AX = mybir.AxisListType.X

NCN = 8
S, DM = 8192, 2048
TL = S // NCN
NT = TL // 128
H, KVH, Dh = 16, 4, 128
SK, TK = 640, 2048
FF, NE = 1408, 8
FD = 768
LOG_SHIFT = float(np.log(np.float32(S) / np.float32(SK)))
BF = ml_dtypes.bfloat16


ENGS = ("pe", "act", "dve", "pool", "sp")
NDQ = 8


class _Op:
    __slots__ = ("eng", "fn", "dma", "cc", "need_inc", "cum", "waits", "tag", "semkey")

    def __init__(self, eng, fn, dma, tag):
        self.eng = eng
        self.fn = fn
        self.dma = dma
        self.cc = False
        self.need_inc = dma
        self.cum = None
        self.waits = []
        self.tag = tag
        self.semkey = None


class Prog:

    def __init__(self, nc):
        self.nc = nc
        self.ops = []
        self.wr = {}
        self.rd = {}
        self.dma_count = {}
        self.dma_prev = {}
        self.pending_alias = {}

    def alias(self, new_name, old_names):
        deps = []
        for b in old_names:
            w = self.wr.get(b)
            if w is not None:
                deps.append(w)
            deps.extend(self.rd.get(b, ()))
        self.pending_alias.setdefault(new_name, []).extend(deps)

    def op(self, eng, fn, reads=(), writes=(), dma=False, tag=""):
        i = len(self.ops)
        o = _Op(eng, fn, dma, tag)
        deps = set()
        for b in writes:
            for d in self.pending_alias.pop(b, ()):
                deps.add(d)
        if dma:
            j = self.dma_count.get(eng, 0)
            self.dma_count[eng] = j + 1
            o.semkey = (eng, j % NDQ)
            o.cum = 16 * (j // NDQ + 1)
            if j >= NDQ:
                deps.add(self.dma_prev[(eng, j % NDQ)])
            self.dma_prev[(eng, j % NDQ)] = i
        else:
            o.semkey = eng
        for b in reads:
            w = self.wr.get(b)
            if w is not None:
                deps.add(w)
        for b in writes:
            for r in self.rd.get(b, ()):
                deps.add(r)
            w = self.wr.get(b)
            if w is not None:
                deps.add(w)
        for d in deps:
            po = self.ops[d]
            if po.eng == eng == "pe" and not po.dma and not dma:
                continue
            po.need_inc = True
            o.waits.append(d)
        self.ops.append(o)
        for b in reads:
            self.rd.setdefault(b, []).append(i)
        for b in writes:
            self.wr[b] = i
            self.rd[b] = []
        return i

    def pe(self, fn, reads=(), writes=(), tag=""):
        return self.op("pe", fn, reads, writes, tag=tag)

    def act(self, fn, reads=(), writes=(), tag=""):
        return self.op("act", fn, reads, writes, tag=tag)

    def dve(self, fn, reads=(), writes=(), tag=""):
        return self.op("dve", fn, reads, writes, tag=tag)

    def pool(self, fn, reads=(), writes=(), tag=""):
        return self.op("pool", fn, reads, writes, tag=tag)

    def dma(self, eng, out, in_, reads=(), writes=(), tag="", **kw):
        def fn(e):
            return e.dma_start(out=out, in_=in_, **kw)
        return self.op(eng, fn, reads, writes, dma=True, tag=tag)

    def cc(self, kind, op, in_ap, out_ap, reads=(), writes=(), tag="cc"):
        def fn(e):
            return e.collective_compute(kind, op, replica_groups=[list(range(NCN))],
                                        ins=[in_ap.opt()], outs=[out_ap.opt()])
        i = self.op("pool", fn, reads, writes, dma=False, tag=tag)
        o = self.ops[i]
        o.cc = True
        o.semkey = ("cc", i)
        o.cum = 1
        o.need_inc = True
        return i

    def mm(self, out, lhsT, rhs, start, stop, reads, writes, tag="mm", pm=None):
        return self.pe(lambda e: e.matmul(out, lhsT, rhs, start=start, stop=stop,
                                          skip_group_check=True, perf_mode=pm),
                       reads=reads, writes=writes, tag=tag)

    def tr(self, out, in_, ident, reads, writes, tag="tr"):
        return self.pe(lambda e: e.transpose(out, in_, ident),
                       reads=reads, writes=writes, tag=tag)

    def emit(self):
        cums = {e: 0 for e in ENGS}
        semkeys = set()
        for o in self.ops:
            if o.dma or o.cc:
                semkeys.add(o.semkey)
            elif o.need_inc:
                cums[o.eng] += 1
                o.cum = cums[o.eng]
                semkeys.add(o.semkey)
        nc = self.nc
        with ExitStack() as st:
            sems = {}
            for k in sorted(semkeys, key=str):
                name = k if isinstance(k, str) else f"dq_{k[0]}_{k[1]}"
                sems[k] = st.enter_context(nc.semaphore(f"sem_{name}"))
            blk = st.enter_context(nc.Block())
            prog = self
            dma_tot = {}
            for o in prog.ops:
                if o.dma:
                    dma_tot[o.semkey] = max(dma_tot.get(o.semkey, 0), o.cum)

            def emit_engine(bass_eng, eng_name):
                watermark = {}
                for o in prog.ops:
                    if o.eng != eng_name:
                        continue
                    need = {}
                    for d in o.waits:
                        po = prog.ops[d]
                        assert po.cum is not None, (po.tag, po.eng)
                        need[po.semkey] = max(need.get(po.semkey, 0), po.cum)
                    for k in sorted(need, key=str):
                        v = need[k]
                        if v > watermark.get(k, 0):
                            bass_eng.wait_ge(sems[k], v)
                            watermark[k] = v
                    r = o.fn(bass_eng)
                    if o.dma:
                        r.then_inc(sems[o.semkey], 16)
                    elif o.cc:
                        r.then_inc(sems[o.semkey], 1)
                    elif o.need_inc:
                        assert r is not None, f"op {o.tag} returned None but needs inc"
                        r.then_inc(sems[o.semkey], 1)
                for k, v in sorted(dma_tot.items(), key=lambda kv: str(kv[0])):
                    if isinstance(k, tuple) and k[0] == eng_name and v > watermark.get(k, 0):
                        bass_eng.wait_ge(sems[k], v)
                if eng_name == "pool":
                    for o in prog.ops:
                        if o.cc and watermark.get(o.semkey, 0) < 1:
                            bass_eng.wait_ge(sems[o.semkey], 1)

            blk.tensor(lambda e: emit_engine(e, "pe"))
            blk.scalar(lambda e: emit_engine(e, "act"))
            blk.vector(lambda e: emit_engine(e, "dve"))
            blk.gpsimd(lambda e: emit_engine(e, "pool"))
            blk.sync(lambda e: emit_engine(e, "sp"))


PM2 = mybir.MatmulPerfMode.DoubleRow


def pair2(ap2d, off, stride, w):
    a = ap2d.ap
    assert len(a) == 2, a
    return bass.AP(ap2d.tensor, ap2d.offset + off, [list(a[0]), [stride, 2], [1, w]])


def rep_mid(ap2d, n):
    a = ap2d.ap
    assert len(a) == 2, a
    return bass.AP(ap2d.tensor, ap2d.offset, [list(a[0]), [0, n], list(a[1])])



class Arena:

    def __init__(self, prog, tensor, total_cols):
        self.p, self.t, self.total = prog, tensor, total_cols
        self.off = 0
        self.events = []

    def mark(self):
        return self.off

    def pop(self, mark):
        self.off = mark

    def alloc(self, name, cols, aka=()):
        names = (name,) + tuple(aka)
        start, end = self.off, self.off + cols
        assert end <= self.total, (name, end, self.total)
        olds = sorted({n for (s, e, ns) in self.events for n in ns
                       if s < end and e > start and n not in names})
        if olds:
            for nm in names:
                self.p.alias(nm, olds)
        self.events.append((start, end, names))
        self.off = end
        return self.t[:, start:end]



def emit_phase_A(p, AF_, AB_, big6, psT, cst, ident_bs, ones_sb, one16_sb,
                 stat, dump, lg_col,
                 xin, wq, wk, wv, impw1, impb1, impw2, impb2,
                 qnw, knw, qnw_roll, knw_roll, sgnv, cos_l, sin_l, kbeT, kab,
                 o_logits, o_kh, o_kr, o_v, o_qsp, o_qrT):
    pb = [big6[:, i * 512:(i + 1) * 512] for i in range(6)]
    mA_f = AF_.mark()
    mA_b = AB_.mark()

    x_t = [AF_.alloc(f"xa{i}", DM) for i in range(2)]
    coss = AF_.alloc("cs", NT * Dh)
    sins = AF_.alloc("sn", NT * Dh)
    c2q = AF_.alloc("c2q", NT * Dh)
    s2q = AF_.alloc("s2q", NT * Dh)
    c2k = AF_.alloc("c2k", NT * Dh)
    s2k = AF_.alloc("s2k", NT * Dh)
    nrm_rows = AF_.alloc("nrm", 5 * Dh)[0:1, :]
    nb = AF_.alloc("nb", 2 * Dh)
    nbr = AF_.alloc("nbr", 2 * Dh)
    kbeT_sb = AF_.alloc("kbeT", 2 * SK)
    kab_sb = AF_.alloc("kab", 4 * SK)
    lg_row = AF_.alloc("lgrow", TL)
    w_col = AF_.alloc("wcola", NT)
    qh = AF_.alloc("qh", 512)
    rr = AF_.alloc("rr", 16)
    tmp_a = AF_.alloc("tmpa", 512)
    tmp_b = AF_.alloc("tmpb", 512)
    kh_f = AF_.alloc("khf", KVH * Dh)
    qsp_sb = AF_.alloc("qsp", 512)
    b2s = AF_.alloc("b2", 1)[0:1, :]
    impb1_sb = AF_.alloc("impb1", 1)[0:64, :]

    hT = AB_.alloc("hT", 16 * TL)
    nrm16 = AB_.alloc("nrm16", 4 * Dh)[0:1, :]
    h16 = AB_.alloc("h16", DM)
    impw1_sb = AB_.alloc("impw1", 16 * 64)
    impw2_sb = AB_.alloc("impw2", 1)[0:64, :]
    t1T = AB_.alloc("t1T", TL)[0:64, :]
    lg16 = AB_.alloc("lg16", TL)[0:1, :]
    PT_t = AB_.alloc("PTa", SK)
    ysk = AB_.alloc("ysk", 512)
    qr16 = AB_.alloc("qr16", 512)
    qrT16 = AB_.alloc("qrT16", 512)
    kv16 = AB_.alloc("kv16", 2 * KVH * Dh)
    v16 = AB_.alloc("v16", KVH * Dh)

    p.dma("sp", kbeT_sb.rearrange("p (a s) -> p a s", a=2),
          kbeT.rearrange("(a p) s -> p a s", p=128), writes=["kbeT"], tag="ldkbe")
    p.dma("sp", kab_sb.rearrange("p (a s) -> p a s", a=4),
          kab.rearrange("a p s -> p a s"), writes=["kab"], tag="ldkab")
    for i, srcv in enumerate([qnw, knw, qnw_roll, knw_roll, sgnv]):
        p.dma("sp", nrm_rows[:, i * Dh:(i + 1) * Dh], srcv, writes=["nrm"], tag=f"ldnr{i}")
    p.dma("sp", coss.rearrange("p (t d) -> p t d", t=NT),
          cos_l.rearrange("(t p) d -> p t d", p=128), writes=["cs"], tag="ldcos")
    p.dma("sp", sins.rearrange("p (t d) -> p t d", t=NT),
          sin_l.rearrange("(t p) d -> p t d", p=128), writes=["sn"], tag="ldsin")
    p.dma("sp", impb1_sb, impb1, writes=["impb1"], tag="ldib1")
    p.dma("sp", b2s, impb2, writes=["b2"], tag="ldib2")
    p.dma("pool", impw2_sb, impw2, writes=["impw2"], tag="ldiw2")
    p.dma("pool", impw1_sb.rearrange("p (a o) -> p a o", a=16),
          impw1.rearrange("(a p) o -> p a o", p=128), writes=["impw1"], tag="ldiw1")

    p.dve(lambda e: e.tensor_mul(nrm_rows[:, 2 * Dh:3 * Dh], nrm_rows[:, 2 * Dh:3 * Dh],
                                 nrm_rows[:, 4 * Dh:5 * Dh]), reads=["nrm"], writes=["nrm"], tag="sgnq")
    p.dve(lambda e: e.tensor_mul(nrm_rows[:, 3 * Dh:4 * Dh], nrm_rows[:, 3 * Dh:4 * Dh],
                                 nrm_rows[:, 4 * Dh:5 * Dh]), reads=["nrm"], writes=["nrm"], tag="sgnk")
    p.dve(lambda e: e.tensor_copy(nrm16, nrm_rows[0:1, 0:4 * Dh]),
          reads=["nrm"], writes=["nrm16"], tag="nrmcast")
    p.mm(pb[0][:, 0:2 * Dh], ones_sb, nrm16[:, 0:2 * Dh], True, True,
         reads=["onesr", "nrm16"], writes=["pb0"], tag="bcast0")
    p.dve(lambda e: e.tensor_copy(nb, pb[0][:, 0:2 * Dh]),
          reads=["pb0"], writes=["nb"], tag="nbcopy")
    p.mm(pb[0][:, 0:2 * Dh], ones_sb, nrm16[:, 2 * Dh:4 * Dh], True, True,
         reads=["onesr", "nrm16"], writes=["pb0"], tag="bcast1")
    p.dve(lambda e: e.tensor_copy(nbr, pb[0][:, 0:2 * Dh]),
          reads=["pb0"], writes=["nbr"], tag="nbrcopy")

    for tt in range(NT):
        cs_t = coss[:, tt * Dh:(tt + 1) * Dh]
        sn_t = sins[:, tt * Dh:(tt + 1) * Dh]
        p.dve(lambda e, o=c2q[:, tt * Dh:(tt + 1) * Dh], a=cs_t:
              e.tensor_mul(o, a, nb[:, 0:Dh]), reads=["cs", "nb"], writes=["c2q"], tag="c2q")
        p.dve(lambda e, o=c2k[:, tt * Dh:(tt + 1) * Dh], a=cs_t:
              e.tensor_mul(o, a, nb[:, Dh:2 * Dh]), reads=["cs", "nb"], writes=["c2k"], tag="c2k")
        p.dve(lambda e, o=s2q[:, tt * Dh:(tt + 1) * Dh], a=sn_t:
              e.tensor_mul(o, a, nbr[:, 0:Dh]), reads=["sn", "nbr"], writes=["s2q"], tag="s2q")
        p.dve(lambda e, o=s2k[:, tt * Dh:(tt + 1) * Dh], a=sn_t:
              e.tensor_mul(o, a, nbr[:, Dh:2 * Dh]), reads=["sn", "nbr"], writes=["s2k"], tag="s2k")

    s1 = stat[:, 0:1]; s2 = stat[:, 1:2]; m_t = stat[:, 2:3]; r_t = stat[:, 3:4]
    for tt in range(NT):
        xb = x_t[tt % 2]
        bx = f"xa{tt % 2}"
        p.dma("sp", xb, xin[tt * 128:(tt + 1) * 128, :], writes=[bx], tag=f"ldx{tt}")
        p.act(lambda e, x=xb: e.activation(dump, x, AF.Square, accum_out=s2),
              reads=[bx], writes=["dump", "stat"], tag=f"lnsq{tt}")
        p.dve(lambda e, x=xb: e.reduce_sum(s1, x, axis=AX),
              reads=[bx], writes=["stat"], tag=f"lns1{tt}")
        p.dve(lambda e: e.tensor_scalar_mul(m_t, s1, 1.0 / DM), reads=["stat"], writes=["stat"], tag="lnm")
        p.dve(lambda e: e.tensor_scalar_mul(s2, s2, 1.0 / DM), reads=["stat"], writes=["stat"], tag="lnv1")
        p.dve(lambda e: e.tensor_tensor(s1, m_t, m_t, ALU.mult), reads=["stat"], writes=["stat"], tag="lnv2")
        p.dve(lambda e: e.tensor_sub(s2, s2, s1), reads=["stat"], writes=["stat"], tag="lnv3")
        p.act(lambda e: e.activation(s2, s2, AF.Sqrt, bias=cst[:, 1:2]), reads=["stat", "cst"], writes=["stat"], tag="lnsd")
        p.dve(lambda e: e.reciprocal(r_t, s2), reads=["stat"], writes=["stat"], tag="lnr")
        p.dve(lambda e, x=xb: e.tensor_scalar(x, x, m_t, r_t, ALU.subtract, ALU.mult),
              reads=[bx, "stat"], writes=[bx], tag=f"lnn{tt}")
        p.dve(lambda e, x=xb: e.tensor_copy(h16, x),
              reads=[bx], writes=["h16"], tag=f"hcast{tt}")
        for g in range(2):
            for d8 in range(8):
                dt_i = g * 8 + d8
                p.tr(psT[:, d8 * 128:(d8 + 1) * 128],
                     h16[:, dt_i * 128:(dt_i + 1) * 128], ident_bs,
                     reads=["h16", "identb"], writes=["psT"], tag=f"trh{tt}")
            dst = hT.rearrange("p (a t) -> p a t", a=16)[:, g * 8:(g + 1) * 8,
                                                         tt * 128:(tt + 1) * 128]
            srcv = psT[:, 0:1024].rearrange("p (a t) -> p a t", a=8)
            p.dve(lambda e, o=dst, i_=srcv: e.tensor_copy(o, i_),
                  reads=["psT"], writes=["hT"], tag=f"evh{tt}")

    for half in range(2):
        hb = pb[half]
        for dt_i in range(16):
            p.mm(hb[0:64, :], impw1_sb[:, dt_i * 64:(dt_i + 1) * 64],
                 hT[:, dt_i * TL + half * 512: dt_i * TL + (half + 1) * 512],
                 start=(dt_i == 0), stop=(dt_i == 15),
                 reads=["impw1", "hT"], writes=[f"pb{half}"], tag="immm")
        p.act(lambda e, o=t1T[:, half * 512:(half + 1) * 512], i_=hb[0:64, :]:
              e.activation(o, i_, AF.Tanh, bias=impb1_sb),
              reads=[f"pb{half}", "impb1"], writes=["t1T"], tag="tanh")
    p.dve(lambda e: e.tensor_scalar_add(b2s, b2s, -LOG_SHIFT),
          reads=["b2"], writes=["b2"], tag="b2p")
    for half in range(2):
        p.mm(pb[2][0:1, :], impw2_sb, t1T[:, half * 512:(half + 1) * 512],
             True, True, reads=["impw2", "t1T"], writes=["pb2"], tag="lgmm")
        p.act(lambda e, o=lg_row[0:1, half * 512:(half + 1) * 512], i_=pb[2][0:1, :]:
              e.activation(o, i_, AF.Identity, bias=b2s),
              reads=["pb2", "b2"], writes=["lgrow"], tag="lgev")
    p.dma("sp", o_logits, lg_row[0:1, :], reads=["lgrow"], writes=["blg"], tag="stlg")
    p.dve(lambda e: e.tensor_copy(lg16, lg_row[0:1, :]), reads=["lgrow"], writes=["lg16"], tag="lgc")
    for tt in range(NT):
        p.mm(pb[3][0:128, 0:1], lg16[:, tt * 128:(tt + 1) * 128], one16_sb,
             True, True, reads=["lg16", "one16"], writes=["pb3"], tag="lgtr")
        p.dve(lambda e, o=lg_col[:, tt:tt + 1]: e.tensor_copy(o, pb[3][0:128, 0:1]),
              reads=["pb3"], writes=["lgcol"], tag="lgcv")
    p.act(lambda e: e.activation(w_col, lg_col, AF.Sigmoid),
          reads=["lgcol"], writes=["wcola"], tag="sigw")

    m_wq = AB_.mark()
    wq_ch = [AB_.alloc(f"wqa{i}", 16 * 512) for i in range(2)]
    for ch in range(4):
        wbuf = wq_ch[ch % 2]
        bw = f"wqa{ch % 2}"
        p.dma("pool", wbuf.rearrange("p (a o) -> p a o", a=16),
              wq.rearrange("(a p) o -> p a o", p=128)[:, :, ch * 512:(ch + 1) * 512],
              writes=[bw], tag=f"ldwq{ch}")
        for tt in range(NT):
            i_loc, jh = tt // 2, tt % 2
            p.dve(lambda e, a=kbeT_sb[:, jh * SK:(jh + 1) * SK],
                  b=kab_sb[:, i_loc * SK:(i_loc + 1) * SK]:
                  e.tensor_mul(PT_t, a, b), reads=["kbeT", "kab"], writes=["PTa"], tag=f"PTb{ch}_{tt}")
            for dt_i in range(16):
                p.mm(pb[5][:, :],
                     hT[:, dt_i * TL + tt * 128: dt_i * TL + (tt + 1) * 128],
                     wbuf[:, dt_i * 512:(dt_i + 1) * 512],
                     start=(dt_i == 0), stop=(dt_i == 15),
                     reads=["hT", bw], writes=["pb5"], tag=f"qmm{ch}_{tt}")
            for hh in range(4):
                p.act(lambda e, i_=pb[5][:, hh * 128:(hh + 1) * 128],
                      o=dump[:, hh * 128:(hh + 1) * 128], a=rr[:, hh:hh + 1]:
                      e.activation(o, i_, AF.Square, accum_out=a),
                      reads=["pb5"], writes=["dump", "rr"], tag=f"qsq{ch}_{tt}")
            p.dve(lambda e: e.tensor_scalar_mul(rr[:, 0:4], rr[:, 0:4], 1.0 / Dh),
                  reads=["rr"], writes=["rr"], tag="rrm")
            p.act(lambda e: e.activation(rr[:, 0:4], rr[:, 0:4], AF.Sqrt, bias=cst[:, 2:3]),
                  reads=["rr", "cst"], writes=["rr"], tag="rrs")
            p.dve(lambda e: e.reciprocal(rr[:, 0:4], rr[:, 0:4]),
                  reads=["rr"], writes=["rr"], tag="rrr")
            for hh in range(4):
                p.dve(lambda e, o=qh[:, hh * 128:(hh + 1) * 128],
                      i_=pb[5][:, hh * 128:(hh + 1) * 128], sc=rr[:, hh:hh + 1]:
                      e.tensor_scalar_mul(o, i_, sc),
                      reads=["pb5", "rr"], writes=["qh"], tag=f"qsc{ch}_{tt}")
            p.dve(lambda e, sc=w_col[:, tt:tt + 1]: e.tensor_scalar_mul(ysk, qh, sc),
                  reads=["qh", "wcola"], writes=["ysk"], tag=f"ysk{ch}_{tt}")
            for sp in range(5):
                p.mm(pb[sp][:, :], PT_t[:, sp * 128:(sp + 1) * 128],
                     ysk, start=(tt == 0), stop=(tt == NT - 1),
                     reads=["PTa", "ysk"], writes=[f"pb{sp}"], tag=f"skmm{ch}_{tt}")
            c2t = c2q[:, tt * Dh:(tt + 1) * Dh]
            s2t0 = s2q[:, tt * Dh: tt * Dh + 64]
            s2t1 = s2q[:, tt * Dh + 64: (tt + 1) * Dh]
            p.dve(lambda e, o=tmp_a.rearrange("p (h d) -> p h d", h=4),
                  a=qh.rearrange("p (h d) -> p h d", h=4), b=c2t:
                  e.tensor_tensor(o, a, rep_mid(b, 4), ALU.mult),
                  reads=["qh", "c2q"], writes=["tmpa"], tag=f"rpc{ch}_{tt}")
            p.dve(lambda e, o=tmp_b.rearrange("p (h d) -> p h d", h=4)[:, :, 0:64],
                  a=qh.rearrange("p (h d) -> p h d", h=4)[:, :, 64:128], b=s2t0:
                  e.tensor_tensor(o, a, rep_mid(b, 4), ALU.mult),
                  reads=["qh", "s2q"], writes=["tmpb"], tag=f"rps0{ch}_{tt}")
            p.dve(lambda e, o=tmp_b.rearrange("p (h d) -> p h d", h=4)[:, :, 64:128],
                  a=qh.rearrange("p (h d) -> p h d", h=4)[:, :, 0:64], b=s2t1:
                  e.tensor_tensor(o, a, rep_mid(b, 4), ALU.mult),
                  reads=["qh", "s2q"], writes=["tmpb"], tag=f"rps1{ch}_{tt}")
            p.dve(lambda e: e.tensor_add(qr16, tmp_a, tmp_b),
                  reads=["tmpa", "tmpb"], writes=["qr16"], tag=f"rpa{ch}_{tt}")
            for j in range(4):
                p.tr(psT[:, j * 128:(j + 1) * 128], qr16[:, j * 128:(j + 1) * 128],
                     ident_bs, reads=["qr16", "identb"], writes=["psT"], tag=f"qtr{ch}_{tt}_{j}")
            p.dve(lambda e: e.tensor_copy(qrT16, psT[:, 0:512]),
                  reads=["psT"], writes=["qrT16"], tag=f"qte{ch}_{tt}")
            p.dma("sp", o_qrT.rearrange("(a p) t -> p a t", p=128)[:, ch * 4:(ch + 1) * 4,
                                                                   tt * 128:(tt + 1) * 128],
                  qrT16.rearrange("p (a t) -> p a t", a=4),
                  reads=["qrT16"], writes=["qrT_dram"], tag=f"stqrT{ch}_{tt}")
        for sp in range(5):
            p.dve(lambda e, i_=pb[sp][:, :]: e.tensor_copy(qsp_sb, i_),
                  reads=[f"pb{sp}"], writes=["qsp"], tag=f"skev{ch}_{sp}")
            p.dma("sp", o_qsp[sp * 128:(sp + 1) * 128, ch * 512:(ch + 1) * 512], qsp_sb,
                  reads=["qsp"], writes=["bqsp"], tag=f"stqsp{ch}_{sp}")

    AB_.pop(m_wq)
    wkv_sb = AB_.alloc("wkv", 16 * 1024)
    p.dma("pool", wkv_sb.rearrange("p (a o) -> p a o", a=16)[:, :, 0:512],
          wk.rearrange("(a p) o -> p a o", p=128), writes=["wkv"], tag="ldwk")
    p.dma("pool", wkv_sb.rearrange("p (a o) -> p a o", a=16)[:, :, 512:1024],
          wv.rearrange("(a p) o -> p a o", p=128), writes=["wkv"], tag="ldwv")
    for tt in range(NT):
        for dt_i in range(16):
            p.mm(pb[0][:, :], hT[:, dt_i * TL + tt * 128: dt_i * TL + (tt + 1) * 128],
                 wkv_sb[:, dt_i * 1024: dt_i * 1024 + 512],
                 start=(dt_i == 0), stop=(dt_i == 15),
                 reads=["hT", "wkv"], writes=["pb0"], tag=f"kmm{tt}")
        for dt_i in range(16):
            p.mm(pb[1][:, :], hT[:, dt_i * TL + tt * 128: dt_i * TL + (tt + 1) * 128],
                 wkv_sb[:, dt_i * 1024 + 512: (dt_i + 1) * 1024],
                 start=(dt_i == 0), stop=(dt_i == 15),
                 reads=["hT", "wkv"], writes=["pb1"], tag=f"vmm{tt}")
        p.dve(lambda e: e.tensor_copy(v16, pb[1][:, :]),
              reads=["pb1"], writes=["v16"], tag=f"vc{tt}")
        p.dma("sp", o_v[tt * 128:(tt + 1) * 128, :], v16, reads=["v16"], writes=["bv"], tag=f"stv{tt}")
        for hh in range(KVH):
            p.act(lambda e, i_=pb[0][:, hh * 128:(hh + 1) * 128],
                  o=dump[:, hh * 128:(hh + 1) * 128], a=rr[:, hh:hh + 1]:
                  e.activation(o, i_, AF.Square, accum_out=a),
                  reads=["pb0"], writes=["dump", "rr"], tag=f"ksq{tt}")
        p.dve(lambda e: e.tensor_scalar_mul(rr[:, 0:KVH], rr[:, 0:KVH], 1.0 / Dh),
              reads=["rr"], writes=["rr"], tag="krm")
        p.act(lambda e: e.activation(rr[:, 0:KVH], rr[:, 0:KVH], AF.Sqrt, bias=cst[:, 2:3]),
              reads=["rr", "cst"], writes=["rr"], tag="krs")
        p.dve(lambda e: e.reciprocal(rr[:, 0:KVH], rr[:, 0:KVH]),
              reads=["rr"], writes=["rr"], tag="krr")
        for hh in range(KVH):
            p.dve(lambda e, o=kh_f[:, hh * 128:(hh + 1) * 128],
                  i_=pb[0][:, hh * 128:(hh + 1) * 128], sc=rr[:, hh:hh + 1]:
                  e.tensor_scalar_mul(o, i_, sc),
                  reads=["pb0", "rr"], writes=["khf"], tag=f"ksc{tt}")
        p.dve(lambda e: e.tensor_copy(kv16[:, 0:512], kh_f),
              reads=["khf"], writes=["kv16"], tag=f"khc{tt}")
        p.dma("sp", o_kh[tt * 128:(tt + 1) * 128, :], kv16[:, 0:512],
              reads=["kv16"], writes=["bkh"], tag=f"stkh{tt}")
        c2t = c2k[:, tt * Dh:(tt + 1) * Dh]
        s2t0 = s2k[:, tt * Dh: tt * Dh + 64]
        s2t1 = s2k[:, tt * Dh + 64: (tt + 1) * Dh]
        p.dve(lambda e, o=tmp_a.rearrange("p (h d) -> p h d", h=KVH),
              a=kh_f.rearrange("p (h d) -> p h d", h=KVH), b=c2t:
              e.tensor_tensor(o, a, rep_mid(b, KVH), ALU.mult),
              reads=["khf", "c2k"], writes=["tmpa"], tag=f"krc{tt}")
        p.dve(lambda e, o=tmp_b.rearrange("p (h d) -> p h d", h=KVH)[:, :, 0:64],
              a=kh_f.rearrange("p (h d) -> p h d", h=KVH)[:, :, 64:128], b=s2t0:
              e.tensor_tensor(o, a, rep_mid(b, KVH), ALU.mult),
              reads=["khf", "s2k"], writes=["tmpb"], tag=f"krs0{tt}")
        p.dve(lambda e, o=tmp_b.rearrange("p (h d) -> p h d", h=KVH)[:, :, 64:128],
              a=kh_f.rearrange("p (h d) -> p h d", h=KVH)[:, :, 0:64], b=s2t1:
              e.tensor_tensor(o, a, rep_mid(b, KVH), ALU.mult),
              reads=["khf", "s2k"], writes=["tmpb"], tag=f"krs1{tt}")
        p.dve(lambda e: e.tensor_add(kv16[:, 512:1024], tmp_a, tmp_b),
              reads=["tmpa", "tmpb"], writes=["kv16"], tag=f"kra{tt}")
        p.dma("sp", o_kr[tt * 128:(tt + 1) * 128, :], kv16[:, 512:1024],
              reads=["kv16"], writes=["bkr"], tag=f"stkr{tt}")

    AF_.pop(mA_f)
    AB_.pop(mA_b)
    p.alias("big6", [f"pb{i}" for i in range(6)])


def make_shared_consts(p, AF_, AB_, ident_b=None, ones_row=None, ones_col=None,
                       one16=None, ltri=None, utri64=None, sel8=None, rep8=None,
                       iota_row=None, iota_r1=None):
    C = {}
    cst = AF_.alloc("cst", 8)
    p.dve(lambda e: e.memset(cst[:, 0:1], 0.0), writes=["cst"], tag="c0")
    p.dve(lambda e: e.memset(cst[:, 1:2], 1e-5), writes=["cst"], tag="c1")
    p.dve(lambda e: e.memset(cst[:, 2:3], 1e-6), writes=["cst"], tag="c2")
    C["cst"] = cst
    C["stat"] = AF_.alloc("stat", 8)
    C["dump"] = AF_.alloc("dump", DM)
    C["tw_all"] = AF_.alloc("tw", NT * NE)
    C["rl_f"] = AF_.alloc("rl", 16)
    C["lg_col"] = AF_.alloc("lgcol", NT)
    C["ident_bs"] = AB_.alloc("identb", 128)
    p.dma("sp", C["ident_bs"], ident_b, writes=["identb"], tag="ldid")
    onesr = AB_.alloc("onesr", 128)
    p.dma("sp", onesr[0:1, :], ones_row, writes=["onesr"], tag="ldor")
    C["onesr_sb"] = onesr[0:1, :]
    if ones_col is not None:
        oc = AB_.alloc("ones", 1)
        p.dma("sp", oc, ones_col, writes=["ones"], tag="ldoc")
        C["ones_sb"] = oc
    o16 = AB_.alloc("one16", 1)
    if one16 is not None:
        p.dma("sp", o16[0:1, :], one16, writes=["one16"], tag="ld116")
    else:
        p.dve(lambda e: e.memset(o16[0:1, :], 1.0), writes=["one16"], tag="one16m")
    C["one16_sb"] = o16[0:1, :]
    if ltri is not None:
        lt = AB_.alloc("ltri", 128)
        p.dma("sp", lt, ltri, writes=["ltri"], tag="ldlt")
        C["ltri_sb"] = lt
    if utri64 is not None:
        ut = AB_.alloc("utri", 64)
        p.dma("sp", ut[0:64, :], utri64, writes=["utri"], tag="ldut")
        C["utri_sb"] = ut[0:64, :]
    if sel8 is not None:
        s8 = AB_.alloc("sel8", NSB)
        p.dma("sp", s8[0:64, :], sel8, writes=["sel8"], tag="ldsel")
        C["sel8_sb"] = s8[0:64, :]
    if rep8 is not None:
        r8 = AB_.alloc("rep8", GT)
        p.dma("sp", r8[0:64, :], rep8, writes=["rep8"], tag="ldrep")
        C["rep8_sb"] = r8[0:64, :]
    if iota_row is not None:
        io = AF_.alloc("iota", 128)
        p.dma("sp", io, iota_row, writes=["iota"], tag="ldio")
        C["iota_sb"] = io
    if iota_r1 is not None:
        io1 = AF_.alloc("iota1", 512)
        p.dma("sp", io1[0:1, :], iota_r1, writes=["iota1"], tag="ldio1")
        C["iota1_sb"] = io1[0:1, :]
    return C


def build_launch_A():
    nc = bass.Bass("TRN2", target_bir_lowering=False, debug=False)
    I = lambda n, s, d=F32: nc.dram_tensor(n, s, d, kind="ExternalInput").ap()
    O = lambda n, s, d=F32: nc.dram_tensor(n, s, d, kind="ExternalOutput").ap()

    xin = I("xin", [TL, DM])
    wq = I("wq", [DM, DM])
    wk = I("wk", [DM, KVH * Dh])
    wv = I("wv", [DM, KVH * Dh])
    impw1 = I("impw1", [DM, 64])
    impb1 = I("impb1", [64, 1])
    impw2 = I("impw2", [64, 1])
    impb2 = I("impb2", [1, 1])
    qnw = I("qnw", [1, Dh])
    knw = I("knw", [1, Dh])
    qnw_roll = I("qnw_roll", [1, Dh])
    knw_roll = I("knw_roll", [1, Dh])
    sgnv = I("sgnv", [1, Dh])
    cos_l = I("cos_l", [TL, Dh])
    sin_l = I("sin_l", [TL, Dh])
    kbeT = I("kbeT", [256, SK])
    kab = I("kab", [4, 128, SK])
    ident_b = I("ident_b", [128, 128], BF16)
    ones_row = I("ones_row", [1, 128], BF16)

    o_logits = O("o_logits", [1, TL])
    o_kh = O("o_kh", [TL, KVH * Dh], BF16)
    o_kr = O("o_kr", [TL, KVH * Dh], BF16)
    o_v = O("o_v", [TL, KVH * Dh], BF16)
    o_qrT = O("o_qrT", [DM, TL], BF16)
    o_qsp = O("o_qsp", [SK, DM])

    p = Prog(nc)
    with ExitStack() as st:
        big6 = st.enter_context(nc.psum_tensor("big6", [128, 3584], F32))
        psT = st.enter_context(nc.psum_tensor("psT", [128, 1024], BF16))
        af_t = st.enter_context(nc.sbuf_tensor("af", [128, 26000], F32))
        ab_t = st.enter_context(nc.sbuf_tensor("ab", [128, 49500], BF16))
        AF_ = Arena(p, af_t, 26000)
        AB_ = Arena(p, ab_t, 49500)
        C = make_shared_consts(p, AF_, AB_, ident_b=ident_b, ones_row=ones_row)
        emit_phase_A(p, AF_, AB_, big6, psT, C["cst"], C["ident_bs"], C["onesr_sb"],
                     C["one16_sb"], C["stat"], C["dump"], C["lg_col"],
                     xin, wq, wk, wv, impw1, impb1, impw2, impb2,
                     qnw, knw, qnw_roll, knw_roll, sgnv, cos_l, sin_l, kbeT, kab,
                     o_logits, o_kh, o_kr, o_v, o_qsp, o_qrT)
        p.emit()
    return nc



def _c(a):
    return np.ascontiguousarray(a)


def host_prep_A(inputs):
    x = np.asarray(inputs["L0_attn_a_hidden_states"])[0]
    cos = np.asarray(inputs["L0_attn_a_cos"])[0, :, 0, :]
    sin = np.asarray(inputs["L0_attn_a_sin"])[0, :, 0, :]
    ka = np.asarray(inputs["L0_attn_a_kron_a"])
    kb = np.asarray(inputs["L0_attn_a_kron_b"])
    ika = np.arange(SK) % 20
    ikb = np.arange(SK) // 20
    ka_expand = _c(ka.T[:, ika])
    kb_expandT = _c(kb[ikb].T)
    qnw = np.asarray(inputs["L0_attn_a_q_norm_w"])
    knw = np.asarray(inputs["L0_attn_a_k_norm_w"])
    sgn = np.concatenate([-np.ones(64, np.float32), np.ones(64, np.float32)])
    ident = np.eye(128, dtype=BF)
    ones_row = np.ones((1, 128), BF)

    maps = []
    for c in range(NCN):
        t0 = c * TL
        kab = np.broadcast_to(ka_expand.T[None, :, :], (1, SK, 128))
        kab = np.broadcast_to(ka_expand[4 * c:4 * c + 4, None, :], (4, 128, SK))
        maps.append({
            "xin": _c(x[t0:t0 + TL]),
            "wq": np.asarray(inputs["L0_attn_a_wq"]),
            "wk": np.asarray(inputs["L0_attn_a_wk"]),
            "wv": np.asarray(inputs["L0_attn_a_wv"]),
            "impw1": np.asarray(inputs["L0_attn_a_imp_w1"]),
            "impb1": _c(np.asarray(inputs["L0_attn_a_imp_b1"]).reshape(64, 1)),
            "impw2": np.asarray(inputs["L0_attn_a_imp_w2"]),
            "impb2": _c(np.asarray(inputs["L0_attn_a_imp_b2"]).reshape(1, 1)),
            "qnw": _c(qnw.reshape(1, Dh)),
            "knw": _c(knw.reshape(1, Dh)),
            "qnw_roll": _c(np.roll(qnw, 64).reshape(1, Dh)),
            "knw_roll": _c(np.roll(knw, 64).reshape(1, Dh)),
            "sgnv": _c(sgn.reshape(1, Dh)),
            "cos_l": _c(cos[t0:t0 + TL]),
            "sin_l": _c(sin[t0:t0 + TL]),
            "kbeT": kb_expandT,
            "kab": _c(kab),
            "ident_b": ident,
            "ones_row": ones_row,
        })
    return maps


def run_A(inputs):
    nc = build_launch_A()
    maps = host_prep_A(inputs)
    res = run_bass_kernel_spmd(nc, maps, core_ids=list(range(NCN)))
    return res.results



INV_SQRT_D = float(1.0 / np.sqrt(np.float32(Dh)))
GT = S // 128
SUP = [(0, 9, 384), (9, 9, 384), (18, 9, 384), (27, 9, 384),
       (36, 9, 384), (45, 9, 384), (54, 10, 512)]
NSB = len(SUP)
SUPOFF = [0]
for _s in SUP:
    SUPOFF.append(SUPOFF[-1] + _s[2])
GK = SUPOFF[-1]
KEYS = GK + SK
NKT = KEYS // 128
NKC = [512] * 6 + [384]
NQ = TL + SK



def emit_phase_B(p, AF_, AB_, big6, psT, C, debug,
                 xin, kr_all, kh_all, v_all, qrT,
                 wo, gate_wT, gateT, upT, downT, WkT, WvT, WmbT, bk_r, bv_r, bmb_r,
                 memory, normT, kbeT, kab_full, ka_eL, kb_e, qnw_col, knw_t4, ss_c,
                 o_x2, o_h3T, o_dmem, o_dnorm,
                 lgT_load, lgl_load, qspT=None, g_qsp=None, O=None):
    nc = p.nc
    ps1 = big6[:, 3072:3584]
    cst = C["cst"]; ident_bs = C["ident_bs"]; onesr_sb = C["onesr_sb"]
    ones_sb = C["ones_sb"]; one16_sb = C["one16_sb"]; ltri_sb = C["ltri_sb"]
    utri_sb = C["utri_sb"]; sel8_sb = C["sel8_sb"]; rep8_sb = C["rep8_sb"]
    iota_sb = C["iota_sb"]; iota1_sb = C["iota1_sb"]
    stat = C["stat"]; dump = C["dump"]; tw_all = C["tw_all"]; rl_f = C["rl_f"]
    qsT_dram = nc.dram_tensor("qsT_dram", [DM, SK], BF16).ap()
    osk_dram = nc.dram_tensor("osk_dram", [5 * 128, DM], BF16).ap()
    fullT_dram = nc.dram_tensor("fullT_dram", [DM, TL], BF16).ap()
    mb_dram = nc.dram_tensor("mb_dram", [TL, DM], BF16).ap()
    if debug:
        d_stats = O("d_stats", [1, 8])
        d_ks = O("d_ks", [SK, KVH * Dh])
        d_vs = O("d_vs", [SK, KVH * Dh])
        d_kdT = O("d_kdT", [128, KVH * KEYS], BF16)
        d_vall = O("d_vall", [128, NKT * 512], BF16)
        d_pen = O("d_pen", [1, KEYS])
        d_x1 = O("d_x1", [TL, DM])
        d_tw = O("d_tw", [TL, NE])
        d_osk = O("d_osk", [5 * 128, DM], BF16)
        d_mb = O("d_mb", [TL, DM], BF16)
        d_num = O("d_num", [128, DM])
        d_wmv = O("d_wmv", [128, DM], BF16)
        d_den = O("d_den", [128, 2 * H])
        d_mk16 = O("d_mk16", [128, KVH * FD], BF16)
        d_h3b = O("d_h3b", [128, 16 * 128], BF16)
        d_mb0 = O("d_mb0", [128, 1024], BF16)
        d_wmbh = O("d_wmbh", [128, 2048], BF16)
        d_mbz = O("d_mbz", [128, 1024])
    lg_sb = AF_.alloc("lg", GT)
    lgT_load(lg_sb)
    wcol = AF_.alloc("wcol", GT)
    p.act(lambda e: e.activation(wcol, lg_sb, AF.Sigmoid, bias=cst[:, 0:1]),
          reads=["lg", "cst"], writes=["wcol"], tag="sigw")
    red = AF_.alloc("red", 4)
    red16 = AB_.alloc("red16", 4)
    p.dve(lambda e: e.reduce_max(red[:, 0:1], lg_sb, axis=AX), reads=["lg"], writes=["red"], tag="gmx")
    p.dve(lambda e: e.reduce_max(red[:, 1:2], lg_sb, axis=AX, negate=True),
          reads=["lg"], writes=["red"], tag="gmn")
    p.dve(lambda e: e.tensor_copy(red16[:, 0:2], red[:, 0:2]), reads=["red"], writes=["red16"], tag="redc")
    p.tr(psT[0:2, 0:128], red16[:, 0:2], ident_bs, reads=["red16", "identb"], writes=["psT"], tag="redtr")
    rr_row = AF_.alloc("rrrow", 256)
    p.dve(lambda e: e.tensor_copy(rr_row[0:2, 0:128], psT[0:2, 0:128]),
          reads=["psT"], writes=["rrrow"], tag="redcp")
    th = AF_.alloc("th", 4)
    p.dve(lambda e: e.reduce_max(th[0:2, 0:1], rr_row[0:2, 0:128], axis=AX),
          reads=["rrrow"], writes=["th"], tag="redmx")
    th16 = AB_.alloc("th16", 2)
    p.dve(lambda e: e.tensor_copy(th16[0:2, 0:1], th[0:2, 0:1]), reads=["th"], writes=["th16"], tag="thc")
    p.tr(psT[0:1, 128:130], th16[0:2, 0:1], ident_bs[0:2, 0:2],
         reads=["th16", "identb"], writes=["psT"], tag="thtr")
    sc_ = AF_.alloc("sc", 16)
    mm16 = AB_.alloc("mm16", 2)
    p.dve(lambda e: e.tensor_copy(mm16[0:1, 0:2], psT[0:1, 128:130]),
          reads=["psT"], writes=["mm16"], tag="scini0")
    p.mm(ps1[:, 0:2], onesr_sb[0:1, :], mm16[0:1, 0:2], True, True,
         reads=["onesr", "mm16"], writes=["big6"], tag="scbc")
    p.dve(lambda e: e.tensor_copy(sc_[:, 0:2], ps1[:, 0:2]), reads=["big6"], writes=["sc"], tag="scini")
    p.dve(lambda e: e.tensor_scalar(sc_[:, 1:2], sc_[:, 0:1], 1.0, None, ALU.add),
          reads=["sc"], writes=["sc"], tag="hii")
    p.dve(lambda e: e.tensor_scalar(sc_[:, 0:1], ps1[:, 1:2], -1.0, -1.0, ALU.mult, ALU.add),
          reads=["big6"], writes=["sc"], tag="loi")
    cmp_sb = AF_.alloc("cmp", GT)
    cnt16 = AB_.alloc("cnt16", 2)
    NBIS = 44
    for it in range(NBIS):
        p.dve(lambda e: e.tensor_scalar(sc_[:, 2:3], sc_[:, 0:1], sc_[:, 1:2], 0.5,
                                        ALU.add, ALU.mult), reads=["sc"], writes=["sc"], tag=f"mid{it}")
        p.dve(lambda e: e.tensor_scalar(cmp_sb, lg_sb, sc_[:, 2:3], None, ALU.is_ge),
              reads=["lg", "sc"], writes=["cmp"], tag=f"cmp{it}")
        p.dve(lambda e: e.reduce_sum(red[:, 2:3], cmp_sb, axis=AX),
              reads=["cmp"], writes=["red"], tag=f"cr{it}")
        p.dve(lambda e: e.tensor_copy(cnt16[:, 0:1], red[:, 2:3]),
              reads=["red"], writes=["cnt16"], tag=f"cc{it}")
        p.mm(ps1[0:1, 0:1], cnt16[:, 0:1], ones_sb, True, True,
             reads=["cnt16", "ones"], writes=["big6"], tag=f"cm{it}")
        p.dve(lambda e: e.tensor_copy(cnt16[0:1, 1:2], ps1[0:1, 0:1]),
              reads=["big6"], writes=["cnt16"], tag=f"cc2{it}")
        p.mm(ps1[:, 1:2], onesr_sb[0:1, :], cnt16[0:1, 1:2], True, True,
             reads=["onesr", "cnt16"], writes=["big6"], tag=f"cb{it}")
        p.dve(lambda e: e.tensor_single_scalar(sc_[:, 4:5], ps1[:, 1:2], float(TK), ALU.is_ge),
              reads=["big6"], writes=["sc"], tag=f"pred{it}")
        p.dve(lambda e: e.tensor_tensor(sc_[:, 5:6], sc_[:, 2:3], sc_[:, 0:1], ALU.subtract),
              reads=["sc"], writes=["sc"], tag=f"d1{it}")
        p.dve(lambda e: e.tensor_tensor(sc_[:, 5:6], sc_[:, 5:6], sc_[:, 4:5], ALU.mult),
              reads=["sc"], writes=["sc"], tag=f"d2{it}")
        p.dve(lambda e: e.tensor_tensor(sc_[:, 0:1], sc_[:, 0:1], sc_[:, 5:6], ALU.add),
              reads=["sc"], writes=["sc"], tag=f"d3{it}")
        p.dve(lambda e: e.tensor_tensor(sc_[:, 6:7], sc_[:, 2:3], sc_[:, 1:2], ALU.subtract),
              reads=["sc"], writes=["sc"], tag=f"d4{it}")
        p.dve(lambda e: e.tensor_scalar(sc_[:, 7:8], sc_[:, 4:5], -1.0, 1.0, ALU.mult, ALU.add),
              reads=["sc"], writes=["sc"], tag=f"d5{it}")
        p.dve(lambda e: e.tensor_tensor(sc_[:, 6:7], sc_[:, 6:7], sc_[:, 7:8], ALU.mult),
              reads=["sc"], writes=["sc"], tag=f"d7{it}")
        p.dve(lambda e: e.tensor_tensor(sc_[:, 1:2], sc_[:, 1:2], sc_[:, 6:7], ALU.add),
              reads=["sc"], writes=["sc"], tag=f"d8{it}")
    mask_sb = AF_.alloc("mask", GT)
    p.dve(lambda e: e.tensor_scalar(mask_sb, lg_sb, sc_[:, 0:1], None, ALU.is_ge),
          reads=["lg", "sc"], writes=["mask"], tag="mask")
    restw = AF_.alloc("restw", GT)
    p.dve(lambda e: e.tensor_scalar(restw, mask_sb, -1.0, 1.0, ALU.mult, ALU.add),
          reads=["mask"], writes=["restw"], tag="rw1")
    p.dve(lambda e: e.tensor_tensor(restw, restw, wcol, ALU.mult),
          reads=["restw", "wcol"], writes=["restw"], tag="rw3")
    if debug:
        stat_o = AF_.alloc("stat_o", 8)
        p.dve(lambda e: e.tensor_copy(stat_o[0:1, 0:8], sc_[0:1, 0:8]),
              reads=["sc"], writes=["stat_o"], tag="sto")
        p.dma("sp", d_stats, stat_o[0:1, 0:8], reads=["stat_o"], tag="std")

    mask16 = AB_.alloc("mask16", GT)
    p.dve(lambda e: e.tensor_copy(mask16, mask_sb), reads=["mask"], writes=["mask16"], tag="m16")
    prefx = AF_.alloc("prefx", GT)
    for gt in range(GT):
        p.mm(ps1[:, 0:1], ltri_sb, mask16[:, gt:gt + 1], True, True,
             reads=["ltri", "mask16"], writes=["big6"], tag=f"pf{gt}")
        p.dve(lambda e, o=prefx[:, gt:gt + 1]: e.tensor_copy(o, ps1[:, 0:1]),
              reads=["big6"], writes=["prefx"], tag=f"pfc{gt}")
    p.mm(ps1[0:1, 0:GT], ones_sb, mask16, True, True,
         reads=["ones", "mask16"], writes=["big6"], tag="tot")
    tot16 = AB_.alloc("tot16", GT)
    p.dve(lambda e: e.tensor_copy(tot16[0:1, :], ps1[0:1, 0:GT]), reads=["big6"], writes=["tot16"], tag="totc")
    p.mm(ps1[0:64, 1:2], tot16[0:1, :], one16_sb[0:1, :], True, True,
         reads=["tot16", "one16"], writes=["big6"], tag="totT")
    totc16 = AB_.alloc("totc16", 1)
    p.dve(lambda e: e.tensor_copy(totc16[0:64, :], ps1[0:64, 1:2]), reads=["big6"], writes=["totc16"], tag="totcc")
    csup = AF_.alloc("csup", NSB)
    p.mm(ps1[0:1, 8:8 + NSB], totc16[0:64, :], sel8_sb[0:64, :], True, True,
         reads=["totc16", "sel8"], writes=["big6"], tag="csup")
    p.dve(lambda e: e.tensor_copy(csup[0:1, :], ps1[0:1, 8:8 + NSB]),
          reads=["big6"], writes=["csup"], tag="csupc")
    p.mm(ps1[0:1, 32:32 + GT], totc16[0:64, :], utri_sb[0:64, :], True, True,
         reads=["totc16", "utri"], writes=["big6"], tag="offs")
    offs = AF_.alloc("offs", GT)
    p.dve(lambda e: e.tensor_copy(offs[0:1, :], ps1[0:1, 32:32 + GT]), reads=["big6"], writes=["offs"], tag="offc")
    offs16r = AB_.alloc("offs16r", GT)
    p.dve(lambda e: e.tensor_copy(offs16r[0:1, :], offs[0:1, :]), reads=["offs"], writes=["offs16r"], tag="of16")
    p.mm(ps1[0:64, 2:3], offs16r[0:1, :], one16_sb[0:1, :], True, True,
         reads=["offs16r", "one16"], writes=["big6"], tag="offT")
    offc16 = AB_.alloc("offc16", 1)
    p.dve(lambda e: e.tensor_copy(offc16[0:64, :], ps1[0:64, 2:3]), reads=["big6"], writes=["offc16"], tag="offcc")
    p.mm(ps1[0:1, 96:96 + GT], offc16[0:64, :], rep8_sb[0:64, :], True, True,
         reads=["offc16", "rep8"], writes=["big6"], tag="strt")
    p.dve(lambda e: e.tensor_tensor(offs[0:1, :], offs[0:1, :], ps1[0:1, 96:96 + GT], ALU.subtract),
          reads=["offs", "big6"], writes=["offs"], tag="orel")
    offs16 = AB_.alloc("offs16", GT)
    p.dve(lambda e: e.tensor_copy(offs16[0:1, :], offs[0:1, :]), reads=["offs"], writes=["offs16"], tag="orc")
    p.mm(ps1[:, 0:GT], onesr_sb[0:1, :], offs16[0:1, :], True, True,
         reads=["onesr", "offs16"], writes=["big6"], tag="orb")
    rkb = AF_.alloc("rkb", GT)
    p.dve(lambda e: e.tensor_scalar(rkb, mask_sb, -1000.0, 999.0, ALU.mult, ALU.add),
          reads=["mask"], writes=["rkb"], tag="rk1")
    p.dve(lambda e: e.tensor_tensor(rkb, rkb, prefx, ALU.add),
          reads=["rkb", "prefx"], writes=["rkb"], tag="rk2")
    p.dve(lambda e: e.tensor_tensor(rkb, rkb, ps1[:, 0:GT], ALU.add),
          reads=["rkb", "big6"], writes=["rkb"], tag="rk3")

    pen16 = AB_.alloc("pen16", KEYS)
    pen_f = AF_.alloc("pen_f", 512)
    p.dve(lambda e: e.memset(pen16[0:1, :], 0.0), writes=["pen16"], tag="penz")
    for sb, (st0, ntl, cap) in enumerate(SUP):
        p.dve(lambda e, sci=csup[0:1, sb:sb + 1], o=pen_f[0:1, 0:cap], i_=iota1_sb[0:1, 0:cap]:
              e.tensor_single_scalar(o, i_, sci, ALU.is_ge),
              reads=["iota1", "csup"], writes=["pen_f"], tag=f"pen{sb}")
        p.dve(lambda e, o=pen16[0:1, SUPOFF[sb]:SUPOFF[sb + 1]], i_=pen_f[0:1, 0:cap]:
              e.tensor_scalar_mul(o, i_, -30000.0),
              reads=["pen_f"], writes=["pen16"], tag=f"pen2{sb}")
    if debug:
        pen_o = AF_.alloc("pen_o", KEYS)
        p.dve(lambda e: e.tensor_copy(pen_o[0:1, :], pen16[0:1, :]), reads=["pen16"], writes=["pen_o"], tag="peno")
        p.dma("sp", d_pen, pen_o[0:1, :], reads=["pen_o"], tag="dpen")

    m_attn = AB_.mark()
    m_attnf = AF_.mark()
    kdT = AB_.alloc("kdT", KVH * KEYS)
    vall = AB_.alloc("vall", NKT * 512)
    m_gat = AB_.mark()
    m_gaf = AF_.mark()
    G16 = AB_.alloc("G16", 128)
    kv_in = AB_.alloc("kvin", 1024)
    gat_o = AB_.alloc("gato", 1024)
    shc = AF_.alloc("shc", 1)
    for sb, (st0, ntl, cap) in enumerate(SUP):
        nblk = cap // 128
        blk_groups = [list(range(g, min(g + 3, nblk))) for g in range(0, nblk, 3)]
        for bg in blk_groups:
            for tt8 in range(ntl):
                gt = st0 + tt8
                p.dma("sp", kv_in[:, 0:512], kr_all[gt * 128:(gt + 1) * 128, :],
                      reads=["gkr"], writes=["kvin"], tag=f"gld{gt}")
                p.dma("sp", kv_in[:, 512:1024], v_all[gt * 128:(gt + 1) * 128, :],
                      reads=["gv"], writes=["kvin"], tag=f"gldv{gt}")
                for bi, blk in enumerate(bg):
                    p.dve(lambda e, sci=rkb[:, gt:gt + 1], b=-(blk * 128.0):
                          e.tensor_scalar(shc, sci, b, None, ALU.add),
                          reads=["rkb"], writes=["shc"], tag=f"sh{gt}_{blk}")
                    p.dve(lambda e: e.tensor_scalar(G16, iota_sb, shc[:, 0:1], None, ALU.is_equal),
                          reads=["iota", "shc"], writes=["G16"], tag=f"G{gt}_{blk}")
                    for hf in range(2):
                        p.mm(big6[:, bi * 1024 + hf * 512: bi * 1024 + (hf + 1) * 512],
                             G16, kv_in[:, hf * 512:(hf + 1) * 512],
                             start=(tt8 == 0), stop=(tt8 == ntl - 1),
                             reads=["G16", "kvin"], writes=["big6"], tag=f"gm{gt}_{blk}_{hf}")
            for bi, blk in enumerate(bg):
                ktile = SUPOFF[sb] // 128 + blk
                p.dve(lambda e, o=gat_o[:, 0:1024], i_=big6[:, bi * 1024:(bi + 1) * 1024]:
                      e.tensor_copy(o, i_), reads=["big6"], writes=["gato"], tag=f"ge{sb}_{blk}")
                for kv in range(KVH):
                    p.tr(psT[:, kv * 128:(kv + 1) * 128], gat_o[:, kv * 128:(kv + 1) * 128],
                         ident_bs, reads=["gato", "identb"], writes=["psT"], tag=f"gt{sb}_{blk}_{kv}")
                for kv in range(KVH):
                    p.dve(lambda e, o=kdT[:, kv * KEYS + ktile * 128: kv * KEYS + (ktile + 1) * 128],
                          i_=psT[:, kv * 128:(kv + 1) * 128]:
                          e.tensor_copy(o, i_), reads=["psT"], writes=["kdT"], tag=f"ge2{sb}_{blk}_{kv}")
                p.dve(lambda e, o=vall[:, ktile * 512:(ktile + 1) * 512], i_=gat_o[:, 512:1024]:
                      e.tensor_copy(o, i_), reads=["gato"], writes=["vall"], tag=f"gv{sb}_{blk}")
    AB_.pop(m_gat)
    AF_.pop(m_gaf)

    m_afsk = AF_.mark()
    m_absk = AB_.mark()
    kbe_sb = AF_.alloc("kbe", 2 * SK)
    p.dma("sp", kbe_sb.rearrange("p (a s) -> p a s", a=2),
          kbeT.rearrange("(a p) s -> p a s", p=128), writes=["kbe"], tag="ldkbeB")
    ss_sb = AF_.alloc("ss", 2)
    ss16t = AB_.alloc("ss16t", 1)
    p.dma("sp", ss_sb[0:1, 0:1], ss_c, writes=["ss"], tag="ldss")
    p.dve(lambda e: e.tensor_copy(ss16t[0:1, :], ss_sb[0:1, 0:1]),
          reads=["ss"], writes=["ss16t"], tag="ssc")
    p.mm(ps1[:, 2:3], onesr_sb[0:1, :], ss16t[0:1, :], True, True,
         reads=["onesr", "ss16t"], writes=["big6"], tag="ssbc")
    p.dve(lambda e: e.tensor_copy(ss_sb[:, 1:2], ps1[:, 2:3]), reads=["big6"], writes=["ss"], tag="ssbe")
    knwss = AF_.alloc("knwss", 512)
    knw4 = AF_.alloc("knw4", 512)
    p.dma("sp", knw4[0:1, :], knw_t4, writes=["knw4"], tag="ldknw4")
    p.dve(lambda e: e.tensor_single_scalar(knw4[0:1, :], knw4[0:1, :], ss_sb[0:1, 0:1], ALU.mult),
          reads=["knw4", "ss"], writes=["knw4"], tag="knwss1")
    knw416 = AB_.alloc("knw416", 512)
    p.dve(lambda e: e.tensor_copy(knw416[0:1, :], knw4[0:1, :]), reads=["knw4"], writes=["knw416"], tag="knwssc")
    p.mm(ps1[:, 0:512], onesr_sb[0:1, :], knw416[0:1, :], True, True,
         reads=["onesr", "knw416"], writes=["big6"], tag="knwbc")
    p.dve(lambda e: e.tensor_copy(knwss, ps1[:, 0:512]), reads=["big6"], writes=["knwss"], tag="knwbe")

    kabf = AF_.alloc("kabf0", 2 * SK, aka=("kabf1",))
    PTt = AB_.alloc("PTt", SK)
    ysk_b = AB_.alloc("yskb", 512)
    kin_b = AB_.alloc("kinb", 512)
    ksf = AF_.alloc("ksf", 5 * 512)
    ks16 = AB_.alloc("ks16", 5 * 512)
    for pas in range(2):
        src_all = kh_all if pas == 0 else v_all
        src_nm = "gkh" if pas == 0 else "gv"
        for gt in range(GT):
            i_glob, jh = gt // 2, gt % 2
            if jh == 0:
                p.dma("sp", kabf[:, (i_glob % 2) * SK:(i_glob % 2 + 1) * SK],
                      kab_full[i_glob], writes=[f"kabf{i_glob % 2}"], tag=f"kab{pas}_{i_glob}")
            p.dma("sp", kin_b, src_all[gt * 128:(gt + 1) * 128, :],
                  reads=[src_nm], writes=["kinb"], tag=f"skld{pas}_{gt}")
            p.dve(lambda e, sci=restw[:, gt:gt + 1]: e.tensor_scalar_mul(ysk_b, kin_b, sci),
                  reads=["kinb", "restw"], writes=["yskb"], tag=f"sky{pas}_{gt}")
            p.dve(lambda e, a=kbe_sb[:, jh * SK:(jh + 1) * SK],
                  b=kabf[:, (i_glob % 2) * SK:(i_glob % 2 + 1) * SK]:
                  e.tensor_mul(PTt, a, b),
                  reads=["kbe", f"kabf{i_glob % 2}"], writes=["PTt"], tag=f"skp{pas}_{gt}")
            for sp in range(5):
                p.mm(big6[:, sp * 512:(sp + 1) * 512], PTt[:, sp * 128:(sp + 1) * 128],
                     ysk_b, start=(gt == 0), stop=(gt == GT - 1),
                     reads=["PTt", "yskb"], writes=["big6"], tag=f"skm{pas}_{gt}_{sp}")
        for sp in range(5):
            p.dve(lambda e, o=ksf[:, sp * 512:(sp + 1) * 512], i_=big6[:, sp * 512:(sp + 1) * 512]:
                  e.tensor_copy(o, i_), reads=["big6"], writes=["ksf"], tag=f"ske{pas}_{sp}")
        if pas == 0:
            for sp in range(5):
                p.dve(lambda e, o=ksf[:, sp * 512:(sp + 1) * 512]:
                      e.tensor_tensor(o, o, knwss, ALU.mult),
                      reads=["ksf", "knwss"], writes=["ksf"], tag=f"skw{sp}")
        else:
            for sp in range(5):
                p.dve(lambda e, o=ksf[:, sp * 512:(sp + 1) * 512]:
                      e.tensor_scalar_mul(o, o, ss_sb[:, 1:2]),
                      reads=["ksf", "ss"], writes=["ksf"], tag=f"svw{sp}")
        p.dve(lambda e: e.tensor_copy(ks16, ksf), reads=["ksf"], writes=["ks16"], tag=f"skc{pas}")
        if debug:
            dd = d_ks if pas == 0 else d_vs
            p.dma("pool", dd.rearrange("(a p) c -> p a c", p=128),
                  ks16.rearrange("p (a c) -> p a c", a=5), reads=["ks16"], tag=f"dks{pas}")
        if pas == 0:
            for sp in range(5):
                for kv in range(KVH):
                    p.tr(psT[:, (sp % 2) * 128:(sp % 2) * 128 + 128],
                         ks16[:, sp * 512 + kv * 128: sp * 512 + (kv + 1) * 128],
                         ident_bs, reads=["ks16", "identb"], writes=["psT"], tag=f"ktr{sp}_{kv}")
                    p.dve(lambda e, o=kdT[:, kv * KEYS + GK + sp * 128:
                                          kv * KEYS + GK + (sp + 1) * 128],
                          i_=psT[:, (sp % 2) * 128:(sp % 2) * 128 + 128]:
                          e.tensor_copy(o, i_),
                          reads=["psT"], writes=["kdT"], tag=f"kte{sp}_{kv}")
        else:
            for sp in range(5):
                p.dve(lambda e, o=vall[:, (22 + sp) * 512:(23 + sp) * 512],
                      i_=ks16[:, sp * 512:(sp + 1) * 512]:
                      e.tensor_copy(o, i_), reads=["ks16"], writes=["vall"], tag=f"vse{sp}")
    AB_.pop(m_absk)
    AF_.pop(m_afsk)
    if debug:
        p.dma("sp", d_kdT, kdT, reads=["kdT"], tag="dkdT")
        p.dma("sp", d_vall, vall, reads=["vall"], tag="dvall")

    m_q3f = AF_.mark()
    m_q3b = AB_.mark()
    ss_sb2 = AF_.alloc("ss2", 2)
    ss16b = AB_.alloc("ss16b", 1)
    p.dma("sp", ss_sb2[0:1, 0:1], ss_c, writes=["ss2"], tag="ldss2")
    p.dve(lambda e: e.tensor_copy(ss16b[0:1, :], ss_sb2[0:1, 0:1]), reads=["ss2"], writes=["ss16b"], tag="ss2c")
    p.mm(ps1[:, 2:3], onesr_sb[0:1, :], ss16b[0:1, :], True, True,
         reads=["onesr", "ss16b"], writes=["big6"], tag="ss2bc")
    p.dve(lambda e: e.tensor_copy(ss_sb2[:, 1:2], ps1[:, 2:3]), reads=["big6"], writes=["ss2"], tag="ss2be")
    qnwc = AF_.alloc("qnwc", 1)
    p.dma("sp", qnwc, qnw_col, writes=["qnwc"], tag="ldqnwc")
    p.dve(lambda e: e.tensor_scalar_mul(qnwc, qnwc, ss_sb2[:, 1:2]),
          reads=["qnwc", "ss2"], writes=["qnwc"], tag="qnwss")
    qsT16 = AB_.alloc("qsT16", 2 * SK)
    HSK = SK // 2
    qs_acc = AF_.alloc("qs_acc", 16 * HSK)
    qs_in = AF_.alloc("qs_in", 16 * HSK)
    if qspT is not None:
        for hs in range(2):
            p.dma("sp", qs_acc.rearrange("p (a s) -> p a s", a=16),
                  qspT[0].rearrange("(a p) s -> p a s", p=128)[:, :, hs * HSK:(hs + 1) * HSK],
                  writes=["qs_acc"], tag=f"qsp0_{hs}")
            for c in range(1, NCN):
                p.dma("sp", qs_in.rearrange("p (a s) -> p a s", a=16),
                      qspT[c].rearrange("(a p) s -> p a s", p=128)[:, :, hs * HSK:(hs + 1) * HSK],
                      writes=["qs_in"], tag=f"qsp{c}_{hs}")
                p.dve(lambda e: e.tensor_add(qs_acc, qs_acc, qs_in),
                      reads=["qs_acc", "qs_in"], writes=["qs_acc"], tag=f"qsa{c}_{hs}")
            for a in range(16):
                p.dve(lambda e, o=qsT16[:, (a % 2) * SK + hs * HSK:(a % 2) * SK + (hs + 1) * HSK],
                      i_=qs_acc[:, a * HSK:(a + 1) * HSK]:
                      e.tensor_scalar_mul(o, i_, qnwc[:, 0:1]),
                      reads=["qs_acc", "qnwc"], writes=["qsT16"], tag=f"qsc{a}_{hs}")
                p.dma("sp", qsT_dram[a * 128:(a + 1) * 128, hs * HSK:(hs + 1) * HSK],
                      qsT16[:, (a % 2) * SK + hs * HSK:(a % 2) * SK + (hs + 1) * HSK],
                      reads=["qsT16"], writes=["qsT_dram"], tag=f"qst{a}_{hs}")

    else:
        qs16i = AB_.alloc("qs16i", 128)
        qsTf = AF_.alloc("qsTf", SK)
        qs_inb = AF_.alloc("qs_inb", 128)
        for a in range(16):
            for sp in range(5):
                p.dma("sp", qs_inb, g_qsp[sp * 128:(sp + 1) * 128, a * 128:(a + 1) * 128],
                      reads=["gqsp"], writes=["qs_inb"], tag=f"qgl{a}_{sp}")
                p.dve(lambda e: e.tensor_copy(qs16i, qs_inb),
                      reads=["qs_inb"], writes=["qs16i"], tag=f"qgc{a}_{sp}")
                p.tr(psT[:, sp * 128:(sp + 1) * 128], qs16i, ident_bs,
                     reads=["qs16i", "identb"], writes=["psT"], tag=f"qgt{a}_{sp}")
            p.dve(lambda e: e.tensor_copy(qsTf, psT[:, 0:SK]),
                  reads=["psT"], writes=["qsTf"], tag=f"qge{a}")
            p.dve(lambda e: e.tensor_scalar_mul(qsTf, qsTf, qnwc[:, 0:1]),
                  reads=["qsTf", "qnwc"], writes=["qsTf"], tag=f"qgs{a}")
            p.dve(lambda e: e.tensor_copy(qsT16[:, 0:SK], qsTf),
                  reads=["qsTf"], writes=["qsT16"], tag=f"qgb{a}")
            p.dma("sp", qsT_dram[a * 128:(a + 1) * 128, :], qsT16[:, 0:SK],
                  reads=["qsT16"], writes=["qsT_dram"], tag=f"qgw{a}")
    AF_.pop(m_q3f)
    AB_.pop(m_q3b)

    p_sb = AB_.alloc("p_sb", 2 * KEYS)
    pstg = AB_.alloc("pstg", 1024)
    att_st = AF_.alloc("attst", 8)
    qcur = AB_.alloc("qcur", TL)

    kaeL_sb = AF_.alloc("kaeL", 5 * 4)
    kbe2_sb = AF_.alloc("kbe2", 5 * 256)
    p.dma("sp", kaeL_sb.rearrange("p (a i) -> p a i", a=5),
          ka_eL.rearrange("(a p) i -> p a i", p=128), writes=["kaeL"], tag="ldkaeL")
    p.dma("sp", kbe2_sb.rearrange("p (a j) -> p a j", a=5),
          kb_e.rearrange("(a p) j -> p a j", p=128), writes=["kbe2"], tag="ldkbe2")
    ploc = AB_.alloc("ploc", 5 * 512)

    def build_ploc(half, tagx):
        for sp in range(5):
            for i2 in range(2):
                il = half * 2 + i2
                p.dve(lambda e, o=ploc[:, sp * 512 + i2 * 256: sp * 512 + (i2 + 1) * 256],
                      a=kbe2_sb[:, sp * 256:(sp + 1) * 256],
                      sci=kaeL_sb[:, sp * 4 + il: sp * 4 + il + 1]:
                      e.tensor_scalar_mul(o, a, sci),
                      reads=["kbe2", "kaeL"], writes=["ploc"], tag=f"plc{tagx}_{sp}_{il}")
    lgl_sb = AF_.alloc("lgl", NT)
    lgl_load(lgl_sb)
    maskb = AB_.alloc("maskb", TL)
    m_mrow = AB_.mark()
    mloc16 = AB_.alloc("mloc16", NT)
    p.dve(lambda e: e.tensor_scalar(mloc16, lgl_sb, sc_[:, 0:1], None, ALU.is_ge),
          reads=["lgl", "sc"], writes=["mloc16"], tag="mloc")
    mrow16 = AB_.alloc("mrow16", TL)
    for tt in range(NT):
        p.tr(psT[0:1, 256:384], mloc16[:, tt:tt + 1], ident_bs,
             reads=["mloc16", "identb"], writes=["psT"], tag=f"mtr{tt}")
        p.dve(lambda e, o=mrow16[0:1, tt * 128:(tt + 1) * 128]:
              e.tensor_copy(o, psT[0:1, 256:384]), reads=["psT"], writes=["mrow16"], tag=f"mte{tt}")
    for hf in range(2):
        p.mm(ps1[:, 0:512], onesr_sb[0:1, :], mrow16[0:1, hf * 512:(hf + 1) * 512],
             True, True, reads=["onesr", "mrow16"], writes=["big6"], tag=f"mbc{hf}")
        p.dve(lambda e, o=maskb[:, hf * 512:(hf + 1) * 512]: e.tensor_copy(o, ps1[:, 0:512]),
              reads=["big6"], writes=["maskb"], tag=f"mbe{hf}")
    AB_.pop(m_mrow)

    p.alias("b6s", ["big6"])
    p.alias("b6pv", ["big6"])
    pend = [None]
    nstep = [0]

    def attn_scores(hh, qsrc, qname, tagp, buf):
        kbase = (hh // 4) * KEYS
        off = 0
        for ci, w in enumerate(NKC):
            p.mm(big6[:, off:off + w], qsrc, kdT[:, kbase + off: kbase + off + w],
                 start=True, stop=True, reads=[qname, "kdT"], writes=["b6s"], tag=f"sc{tagp}_{ci}")
            off += w
        st_ = att_st[:, buf * 4:(buf + 1) * 4]
        p.dve(lambda e: e.reduce_max(st_[:, 0:1], big6[:, 0:KEYS], axis=AX),
              reads=["b6s"], writes=["attst"], tag=f"mx{tagp}")
        p.dve(lambda e: e.tensor_scalar_mul(st_[:, 1:2], st_[:, 0:1], -INV_SQRT_D),
              reads=["attst"], writes=["attst"], tag=f"mb{tagp}")
        p.act(lambda e, o=p_sb[:, buf * KEYS:(buf + 1) * KEYS]:
              e.activation(o, big6[:, 0:KEYS], AF.Exp,
                           bias=st_[:, 1:2], scale=INV_SQRT_D, accum_out=st_[:, 2:3]),
              reads=["b6s", "attst"], writes=["p_sb", "attst"], tag=f"exp{tagp}")
        p.act(lambda e: e.activation(st_[:, 0:1], st_[:, 1:2], AF.Exp, bias=cst[:, 0:1]),
              reads=["attst", "cst"], writes=["attst"], tag=f"pz{tagp}")
        p.dve(lambda e: e.tensor_scalar_mul(st_[:, 0:1], st_[:, 0:1], -float(GK - TK)),
              reads=["attst"], writes=["attst"], tag=f"pz2{tagp}")
        p.dve(lambda e: e.tensor_add(st_[:, 2:3], st_[:, 2:3], st_[:, 0:1]),
              reads=["attst"], writes=["attst"], tag=f"pz3{tagp}")
        p.dve(lambda e: e.reciprocal(st_[:, 3:4], st_[:, 2:3]),
              reads=["attst"], writes=["attst"], tag=f"li{tagp}")

    def attn_pv(hh, tagp, buf, evac_fn):
        kv = hh // 4
        pbase = buf * KEYS
        groups = [list(range(g, min(g + 4, NKT))) for g in range(0, NKT, 4)]
        for gi in range(len(groups) + 1):
            if gi < len(groups):
                grp = groups[gi]
                half = gi % 2
                for j, kt in enumerate(grp):
                    p.tr(psT[:, half * 512 + j * 128: half * 512 + (j + 1) * 128],
                         p_sb[:, pbase + kt * 128: pbase + (kt + 1) * 128],
                         ident_bs, reads=["p_sb", "identb"], writes=["psT"], tag=f"pt{tagp}_{kt}")
            if gi >= 1:
                pg = groups[gi - 1]
                ph = (gi - 1) % 2
                p.dve(lambda e, o=pstg[:, ph * 512: ph * 512 + len(pg) * 128],
                      i_=psT[:, ph * 512: ph * 512 + len(pg) * 128]:
                      e.tensor_copy(o, i_), reads=["psT"], writes=["pstg"], tag=f"pe{tagp}_{gi}")
                for j, kt in enumerate(pg):
                    p.mm(big6[:, 3456:3584],
                         vall[:, kt * 512 + kv * 128: kt * 512 + (kv + 1) * 128],
                         pstg[:, ph * 512 + j * 128: ph * 512 + (j + 1) * 128],
                         start=(kt == 0), stop=(kt == NKT - 1),
                         reads=["vall", "pstg"], writes=["b6pv"], tag=f"pv{tagp}_{kt}")
        evac_fn(att_st[:, buf * 4 + 3: buf * 4 + 4])

    def flush_pv():
        if pend[0] is not None:
            hh0, tag0, buf0, ev0 = pend[0]
            pend[0] = None
            attn_pv(hh0, tag0, buf0, ev0)

    def attn_step(hh, qsrc, qname, tagp, evac_fn):
        buf = nstep[0] % 2
        nstep[0] += 1
        attn_scores(hh, qsrc, qname, tagp, buf)
        flush_pv()
        pend[0] = (hh, tagp, buf, evac_fn)

    for hh in range(H):
        p.dma("sp", qcur[:, 0:SK], qsT_dram[hh * 128:(hh + 1) * 128, :],
              reads=["qsT_dram"], writes=["qcur"], tag=f"qsl{hh}")
        for sq in range(5):
            def evac(rc, hh=hh, sq=sq):
                p.dve(lambda e, o=pstg[:, 0:128]:
                      e.tensor_scalar_mul(o, big6[:, 3456:3584], rc),
                      reads=["b6pv", "attst"], writes=["pstg"], tag=f"ske{hh}_{sq}")
                p.tr(psT[:, 896:1024], pstg[:, 0:128], ident_bs,
                     reads=["pstg", "identb"], writes=["psT"], tag=f"str{hh}_{sq}")
                p.dve(lambda e, o=pstg[:, 128:256], i_=psT[:, 896:1024]:
                      e.tensor_copy(o, i_), reads=["psT"], writes=["pstg"], tag=f"ske2{hh}_{sq}")
                p.dma("sp", osk_dram[sq * 128:(sq + 1) * 128, hh * 128:(hh + 1) * 128],
                      pstg[:, 128:256], reads=["pstg"], writes=["osk_dram"], tag=f"osks{hh}_{sq}")
            attn_step(hh, qcur[:, sq * 128:(sq + 1) * 128], "qcur", f"s{hh}_{sq}", evac)
    if debug:
        p.dma("sp", d_osk, osk_dram, reads=["osk_dram"], tag="dosk")

    exp16 = AB_.alloc("exp16", TL)
    det16 = AB_.alloc("det16", TL)
    osk_h = AB_.alloc("osk_h", 5 * 128)
    for hh in range(H):
        p.dma("sp", qcur, qrT[hh * 128:(hh + 1) * 128, :], reads=["qrT_dram"], writes=["qcur"], tag=f"qld{hh}")
        p.dma("sp", osk_h.rearrange("p (a c) -> p a c", a=5),
              osk_dram.rearrange("(a p) c -> p a c", p=128)[:, :, hh * 128:(hh + 1) * 128],
              reads=["osk_dram"], writes=["osk_h"], tag=f"oskl{hh}")
        for half in range(2):
            build_ploc(half, f"{hh}_{half}")
            for sp in range(5):
                p.mm(big6[:, 2048 + half * 512: 2048 + (half + 1) * 512],
                     osk_h[:, sp * 128:(sp + 1) * 128],
                     ploc[:, sp * 512:(sp + 1) * 512],
                     start=(sp == 0), stop=(sp == 4),
                     reads=["osk_h", "ploc"], writes=["b6s"], tag=f"ex{hh}_{half}_{sp}")
            p.dve(lambda e, o=exp16[:, half * 512:(half + 1) * 512],
                  i_=big6[:, 2048 + half * 512: 2048 + (half + 1) * 512]:
                  e.tensor_copy(o, i_), reads=["b6s"], writes=["exp16"], tag=f"exe{hh}_{half}")
        for qt in range(NT):
            def evac(rc, hh=hh, qt=qt):
                p.dve(lambda e, o=det16[:, qt * 128:(qt + 1) * 128]:
                      e.tensor_scalar_mul(o, big6[:, 3456:3584], rc),
                      reads=["b6pv", "attst"], writes=["det16"],
                      tag=f"dte{hh}_{qt}")
            attn_step(hh, qcur[:, qt * 128:(qt + 1) * 128], "qcur", f"d{hh}_{qt}", evac)
        flush_pv()
        p.dve(lambda e: e.tensor_sub(det16, det16, exp16),
              reads=["det16", "exp16"], writes=["det16"], tag=f"bl1_{hh}")
        p.dve(lambda e: e.tensor_mul(det16, det16, maskb),
              reads=["det16", "maskb"], writes=["det16"], tag=f"bl2_{hh}")
        p.dve(lambda e: e.tensor_add(det16, det16, exp16),
              reads=["det16", "exp16"], writes=["det16"], tag=f"bl3_{hh}")
        p.dma("sp", fullT_dram[hh * 128:(hh + 1) * 128, :], det16,
              reads=["det16"], writes=["fullT_dram"], tag=f"fst{hh}")
    p.alias("big6", ["b6s", "b6pv"])

    AB_.pop(m_attn)
    AF_.pop(m_attnf)
    m_wo = AB_.mark()
    fullT_sb = AB_.alloc("fullT_sb", 16 * TL)
    p.dma("sp", fullT_sb.rearrange("p (a t) -> p a t", a=16),
          fullT_dram.rearrange("(a p) t -> p a t", p=128),
          reads=["fullT_dram"], writes=["fullT_sb"], tag="flld")
    m_x1 = AF_.mark()
    x1 = AF_.alloc("x1", NT * DM)
    for tt in range(NT):
        p.dma("sp", x1[:, tt * DM:(tt + 1) * DM], xin[tt * 128:(tt + 1) * 128, :],
              writes=["x1"], tag=f"xld{tt}")
    wo_q = AB_.alloc("wo_q", 16 * 512)
    for chq in range(4):
        p.dma("pool", wo_q.rearrange("p (a o) -> p a o", a=16),
              wo.rearrange("(a p) o -> p a o", p=128)[:, :, chq * 512:(chq + 1) * 512],
              writes=["wo_q"], tag=f"wold{chq}")
        for tt in range(NT):
            for dt_i in range(16):
                p.mm(ps1[:, 0:512],
                     fullT_sb[:, dt_i * TL + tt * 128: dt_i * TL + (tt + 1) * 128],
                     wo_q[:, dt_i * 512:(dt_i + 1) * 512],
                     start=(dt_i == 0), stop=(dt_i == 15),
                     reads=["fullT_sb", "wo_q"], writes=["big6"], tag=f"wom{chq}_{tt}")
            p.dve(lambda e, o=x1[:, tt * DM + chq * 512: tt * DM + (chq + 1) * 512]:
                  e.tensor_add(o, o, ps1[:, 0:512]),
                  reads=["big6", "x1"], writes=["x1"], tag=f"woe{chq}_{tt}")
    if debug:
        for tt in range(NT):
            p.dma("sp", d_x1[tt * 128:(tt + 1) * 128, :], x1[:, tt * DM:(tt + 1) * DM],
                  reads=["x1"], tag=f"dx1{tt}")
    AB_.pop(m_wo)

    h2t = AB_.alloc("h2t", DM)
    gwT_sb = AB_.alloc("gwT", 16 * NE)
    s1 = stat[:, 0:1]; s2 = stat[:, 1:2]; m_t = stat[:, 2:3]; r_t = stat[:, 3:4]
    m_ffn = AB_.mark()
    hT2 = AB_.alloc("hT2", 16 * TL)
    p.dma("pool", gwT_sb.rearrange("p (a o) -> p a o", a=16),
          gate_wT.rearrange("(a p) o -> p a o", p=128), writes=["gwT"], tag="ldgw")

    def ln_tile_ops(xap, bx, tag):
        p.act(lambda e: e.activation(dump, xap, AF.Square, accum_out=s2, bias=cst[:, 0:1]),
              reads=[bx, "cst"], writes=["dump", "stat"], tag=f"{tag}sq")
        p.dve(lambda e: e.reduce_sum(s1, xap, axis=AX), reads=[bx], writes=["stat"], tag=f"{tag}s1")
        p.dve(lambda e: e.tensor_scalar_mul(m_t, s1, 1.0 / DM), reads=["stat"], writes=["stat"], tag=f"{tag}m")
        p.dve(lambda e: e.tensor_scalar_mul(s2, s2, 1.0 / DM), reads=["stat"], writes=["stat"], tag=f"{tag}v1")
        p.dve(lambda e: e.tensor_tensor(s1, m_t, m_t, ALU.mult), reads=["stat"], writes=["stat"], tag=f"{tag}v2")
        p.dve(lambda e: e.tensor_sub(s2, s2, s1), reads=["stat"], writes=["stat"], tag=f"{tag}v3")
        p.act(lambda e: e.activation(s2, s2, AF.Sqrt, bias=cst[:, 1:2]),
              reads=["stat", "cst"], writes=["stat"], tag=f"{tag}sd")
        p.dve(lambda e: e.reciprocal(r_t, s2), reads=["stat"], writes=["stat"], tag=f"{tag}r")

    for tt in range(NT):
        xap = x1[:, tt * DM:(tt + 1) * DM]
        ln_tile_ops(xap, "x1", f"ln2_{tt}")
        p.dve(lambda e, x=xap: e.tensor_scalar(h2t, x, m_t, r_t, ALU.subtract, ALU.mult),
              reads=["x1", "stat"], writes=["h2t"], tag=f"ln2n{tt}")
        for g in range(2):
            for d8 in range(8):
                dt_i = g * 8 + d8
                p.tr(psT[:, d8 * 128:(d8 + 1) * 128], h2t[:, dt_i * 128:(dt_i + 1) * 128],
                     ident_bs, reads=["h2t", "identb"], writes=["psT"], tag=f"h2tr{tt}")
            dst = hT2.rearrange("p (a t) -> p a t", a=16)[:, g * 8:(g + 1) * 8,
                                                         tt * 128:(tt + 1) * 128]
            p.dve(lambda e, o=dst, i_=psT[:, 0:1024].rearrange("p (a t) -> p a t", a=8):
                  e.tensor_copy(o, i_), reads=["psT"], writes=["hT2"], tag=f"h2ev{tt}")
        for dt_i in range(16):
            p.mm(ps1[:, 0:NE], hT2.rearrange("p (a t) -> p a t", a=16)[:, dt_i,
                                                                      tt * 128:(tt + 1) * 128],
                 gwT_sb[:, dt_i * NE:(dt_i + 1) * NE],
                 start=(dt_i == 0), stop=(dt_i == 15),
                 reads=["hT2", "gwT"], writes=["big6"], tag=f"rt{tt}")
        p.dve(lambda e: e.tensor_copy(rl_f[:, 0:NE], ps1[:, 0:NE]),
              reads=["big6"], writes=["rl"], tag=f"rte{tt}")
        p.dve(lambda e: e.reduce_max(rl_f[:, 8:9], rl_f[:, 0:NE], axis=AX),
              reads=["rl"], writes=["rl"], tag=f"rm1{tt}")
        p.dve(lambda e: e.tensor_scalar(dump[:, 0:NE], rl_f[:, 0:NE], rl_f[:, 8:9], None,
                                        ALU.is_equal),
              reads=["rl"], writes=["dump"], tag=f"re1{tt}")
        p.dve(lambda e: e.scalar_tensor_tensor(dump[:, 8:16], dump[:, 0:NE], -1e9,
                                               rl_f[:, 0:NE], ALU.mult, ALU.add),
              reads=["dump", "rl"], writes=["dump"], tag=f"rl2{tt}")
        p.dve(lambda e: e.reduce_max(rl_f[:, 9:10], dump[:, 8:16], axis=AX),
              reads=["dump"], writes=["rl"], tag=f"rm2{tt}")
        p.dve(lambda e: e.tensor_scalar(dump[:, 16:24], dump[:, 8:16], rl_f[:, 9:10], None,
                                        ALU.is_equal),
              reads=["dump", "rl"], writes=["dump"], tag=f"re2{tt}")
        p.dve(lambda e: e.tensor_tensor(rl_f[:, 10:11], rl_f[:, 8:9], rl_f[:, 9:10], ALU.subtract),
              reads=["rl"], writes=["rl"], tag=f"rd{tt}")
        p.act(lambda e: e.activation(rl_f[:, 11:12], rl_f[:, 10:11], AF.Sigmoid, bias=cst[:, 0:1]),
              reads=["rl", "cst"], writes=["rl"], tag=f"rp1{tt}")
        p.dve(lambda e: e.tensor_scalar(rl_f[:, 12:13], rl_f[:, 11:12], -1.0, 1.0,
                                        ALU.mult, ALU.add),
              reads=["rl"], writes=["rl"], tag=f"rp2{tt}")
        p.dve(lambda e, o=tw_all[:, tt * NE:(tt + 1) * NE]:
              e.tensor_scalar_mul(o, dump[:, 0:NE], rl_f[:, 11:12]),
              reads=["dump", "rl"], writes=["tw"], tag=f"tw1{tt}")
        p.dve(lambda e: e.tensor_scalar_mul(dump[:, 16:24], dump[:, 16:24], rl_f[:, 12:13]),
              reads=["dump", "rl"], writes=["dump"], tag=f"tw2{tt}")
        p.dve(lambda e, o=tw_all[:, tt * NE:(tt + 1) * NE]:
              e.tensor_add(o, o, dump[:, 16:24]),
              reads=["tw", "dump"], writes=["tw"], tag=f"tw3{tt}")
    if debug:
        p.dma("sp", d_tw.rearrange("(t p) e -> p t e", p=128),
              tw_all.rearrange("p (t e) -> p t e", t=NT), reads=["tw"], tag="dtw")

    m_ffnw = AB_.mark()
    wsl = [AB_.alloc(f"wsl{i}", 2 * 16 * 128) for i in range(2)]
    hTf = AB_.alloc("hTf", 2 * 11 * 512)
    sil = AB_.alloc("sil", 512)
    for e_i in range(NE):
        for ft in range(11):
            sl = wsl[ft % 2]
            sn = f"wsl{ft % 2}"
            p.dma("pool", sl[:, 0:2048].rearrange("p (a f) -> p a f", a=16),
                  gateT[e_i].rearrange("(a p) f -> p a f", p=128)[:, :, ft * 128:(ft + 1) * 128],
                  writes=[sn], tag=f"fg{e_i}_{ft}")
            p.dma("pool", sl[:, 2048:4096].rearrange("p (a f) -> p a f", a=16),
                  upT[e_i].rearrange("(a p) f -> p a f", p=128)[:, :, ft * 128:(ft + 1) * 128],
                  writes=[sn], tag=f"fu{e_i}_{ft}")
            for ch in range(2):
                for dt_i in range(16):
                    p.mm(big6[:, ch * 1024: ch * 1024 + 512],
                         sl[:, dt_i * 128:(dt_i + 1) * 128],
                         hT2[:, dt_i * TL + ch * 512: dt_i * TL + (ch + 1) * 512],
                         start=(dt_i == 0), stop=(dt_i == 15),
                         reads=[sn, "hT2"], writes=["big6"], tag=f"gm{e_i}_{ch}_{ft}")
                for dt_i in range(16):
                    p.mm(big6[:, ch * 1024 + 512: ch * 1024 + 1024],
                         sl[:, 2048 + dt_i * 128: 2048 + (dt_i + 1) * 128],
                         hT2[:, dt_i * TL + ch * 512: dt_i * TL + (ch + 1) * 512],
                         start=(dt_i == 0), stop=(dt_i == 15),
                         reads=[sn, "hT2"], writes=["big6"], tag=f"um{e_i}_{ch}_{ft}")
                p.act(lambda e, ch=ch: e.activation(sil, big6[:, ch * 1024: ch * 1024 + 512],
                                                    AF.Silu, bias=cst[:, 0:1]),
                      reads=["big6", "cst"], writes=["sil"], tag=f"si{e_i}_{ch}_{ft}")
                p.dve(lambda e, o=hTf[:, ch * 5632 + ft * 512: ch * 5632 + (ft + 1) * 512],
                      i_=big6[:, ch * 1024 + 512: ch * 1024 + 1024]:
                      e.tensor_tensor(o, i_, sil, ALU.mult),
                      reads=["big6", "sil"], writes=["hTf"], tag=f"hm{e_i}_{ch}_{ft}")
        for dp in range(8):
            sl = wsl[dp % 2]
            sn = f"wsl{dp % 2}"
            p.dma("pool", sl[:, 0:11 * 256].rearrange("p (a f) -> p a f", a=11),
                  downT[e_i].rearrange("(a p) o -> p a o", p=128)[:, :, dp * 256:(dp + 1) * 256],
                  writes=[sn], tag=f"fd{e_i}_{dp}")
            for ch in range(2):
                for do in range(2):
                    for ft in range(11):
                        p.mm(big6[:, (ch * 2 + do) * 512:(ch * 2 + do + 1) * 512],
                             sl[:, ft * 256 + do * 128: ft * 256 + (do + 1) * 128],
                             hTf[:, ch * 5632 + ft * 512: ch * 5632 + (ft + 1) * 512],
                             start=(ft == 0), stop=(ft == 10),
                             reads=[sn, "hTf"], writes=["big6"], tag=f"dm{e_i}_{ch}_{dp}_{do}")
            for ch in range(2):
                for j in range(4):
                    tt = ch * 4 + j
                    for do in range(2):
                        p.dve(lambda e, o=sil[:, 0:128],
                              i_=big6[:, (ch * 2 + do) * 512 + j * 128:
                                      (ch * 2 + do) * 512 + (j + 1) * 128]:
                              e.tensor_copy(o, i_), reads=["big6"], writes=["sil"],
                              tag=f"de{e_i}_{ch}_{dp}_{do}_{j}")
                        p.tr(psT[:, do * 128:(do + 1) * 128], sil[:, 0:128], ident_bs,
                             reads=["sil", "identb"], writes=["psT"], tag=f"dt{e_i}_{ch}_{dp}_{do}_{j}")
                    p.dve(lambda e, o=x1[:, tt * DM + dp * 256: tt * DM + (dp + 1) * 256],
                          i_=psT[:, 0:256],
                          sci=tw_all[:, tt * NE + e_i: tt * NE + e_i + 1]:
                          e.scalar_tensor_tensor(o, i_, sci, o, ALU.mult, ALU.add),
                          reads=["psT", "tw", "x1"], writes=["x1"], tag=f"da{e_i}_{ch}_{dp}_{j}")
    for tt in range(NT):
        p.dma("sp", o_x2[tt * 128:(tt + 1) * 128, :], x1[:, tt * DM:(tt + 1) * DM],
              reads=["x1"], writes=["x2_dram"], tag=f"sx2{tt}")
    AB_.pop(m_ffnw)

    AB_.pop(m_ffn)
    m_h3 = AB_.mark()
    h3T = AB_.alloc("h3T", 16 * TL)
    for tt in range(NT):
        xap = x1[:, tt * DM:(tt + 1) * DM]
        ln_tile_ops(xap, "x1", f"ln3_{tt}")
        p.dve(lambda e, x=xap: e.tensor_scalar(h2t, x, m_t, r_t, ALU.subtract, ALU.mult),
              reads=["x1", "stat"], writes=["h2t"], tag=f"ln3n{tt}")
        for g in range(2):
            for d8 in range(8):
                dt_i = g * 8 + d8
                p.tr(psT[:, d8 * 128:(d8 + 1) * 128], h2t[:, dt_i * 128:(dt_i + 1) * 128],
                     ident_bs, reads=["h2t", "identb"], writes=["psT"], tag=f"h3tr{tt}")
            dst = h3T.rearrange("p (a t) -> p a t", a=16)[:, g * 8:(g + 1) * 8,
                                                         tt * 128:(tt + 1) * 128]
            p.dve(lambda e, o=dst, i_=psT[:, 0:1024].rearrange("p (a t) -> p a t", a=8):
                  e.tensor_copy(o, i_), reads=["psT"], writes=["h3T"], tag=f"h3ev{tt}")
    p.dma("sp", o_h3T.rearrange("(a p) t -> p a t", p=128),
          h3T.rearrange("p (a t) -> p a t", a=16), reads=["h3T"], writes=["h3T_dram"], tag="sh3")
    AB_.pop(m_h3)
    AF_.pop(m_x1)

    m_b8a = AB_.mark()
    h3_cur = AB_.alloc("h3c", 16 * 128)
    Wmb_h = AB_.alloc("Wmb", 16 * 1024)
    bias_r = AB_.alloc("biasr", 3072)
    p.dma("pool", bias_r[0:1, 0:512], bk_r, writes=["biasr"], tag="ldbk")
    p.dma("pool", bias_r[0:1, 512:1024], bv_r, writes=["biasr"], tag="ldbv")
    p.dma("pool", bias_r[0:1, 1024:1024 + DM], bmb_r, writes=["biasr"], tag="ldbmb")
    mb16 = AB_.alloc("mb16", 1024)
    m_dbg = AF_.mark()
    mbz_dbg = AF_.alloc("mbz", 1024) if debug else None
    for mh in range(2):
        p.dma("pool", Wmb_h.rearrange("p (a o) -> p a o", a=16),
              WmbT.rearrange("(a p) o -> p a o", p=128)[:, :, mh * 1024:(mh + 1) * 1024],
              writes=["Wmb"], tag=f"ldmb{mh}")
        if debug and mh == 0:
            p.dma("sp", d_wmbh, Wmb_h[:, 0:2048], reads=["Wmb"], tag="dwmbh_early")
        for tt in range(NT):
            p.dma("sp", h3_cur.rearrange("p (a t) -> p a t", a=16),
                  o_h3T.rearrange("(a p) t -> p a t", p=128)[:, :, tt * 128:(tt + 1) * 128],
                  reads=["h3T_dram"], writes=["h3c"], tag=f"h3l{mh}_{tt}")
            for chm in range(2):
                for dt_i in range(16):
                    p.mm(big6[:, chm * 512:(chm + 1) * 512],
                         h3_cur[:, dt_i * 128:(dt_i + 1) * 128],
                         Wmb_h[:, dt_i * 1024 + chm * 512: dt_i * 1024 + (chm + 1) * 512],
                         start=(dt_i == 0), stop=False,
                         reads=["h3c", "Wmb"], writes=["big6"], tag=f"mb{mh}_{tt}_{chm}")
                p.mm(big6[:, chm * 512:(chm + 1) * 512],
                     onesr_sb[0:1, :], bias_r[0:1, 1024 + mh * 1024 + chm * 512:
                                              1024 + mh * 1024 + (chm + 1) * 512],
                     start=False, stop=True, reads=["onesr", "biasr"], writes=["big6"],
                     tag=f"mbb{mh}_{tt}_{chm}")
            if debug and mh == 0 and tt == 0:
                p.dve(lambda e: e.tensor_copy(mbz_dbg, big6[:, 0:1024]),
                      reads=["big6"], writes=["mbz"], tag="mbzc")
                p.dma("sp", d_mbz, mbz_dbg, reads=["mbz"], tag="dmbz")
            p.act(lambda e, o=mb16: e.activation(o, big6[:, 0:1024], AF.Sigmoid, bias=cst[:, 0:1]),
                  reads=["big6", "cst"], writes=["mb16"], tag=f"mbs{mh}_{tt}")
            if debug and mh == 0 and tt == 0:
                p.dma("sp", d_h3b, h3_cur, reads=["h3c"], tag="dh3b")
                p.dma("sp", d_mb0, mb16, reads=["mb16"], tag="dmb0")
                p.dve(lambda e: e.tensor_copy(pen_f[0:1, 0:1], pen_f[0:1, 0:1]),
                      reads=["pen_f"], writes=["pen_f"], tag="nop_dbg")
                p.dve(lambda e: e.tensor_copy(dump2[:, 0:1] if False else mb16[:, 0:1], mb16[:, 0:1]),
                      reads=["mb16"], writes=["mb16"], tag="nop_dbg2")
            p.dma("sp", mb_dram[tt * 128:(tt + 1) * 128, mh * 1024:(mh + 1) * 1024],
                  mb16, reads=["mb16"], writes=["mb_dram"], tag=f"mbst{mh}_{tt}")
    AB_.pop(m_b8a)
    AF_.pop(m_dbg)

    m_b8f = AF_.mark()
    m_b8b = AB_.mark()
    h3_cur = AB_.alloc("h3c", 16 * 128)
    WkT_sb = AB_.alloc("WkT", 16 * 512)
    WvT_sb = AB_.alloc("WvT", 16 * 512)
    bias_r = AB_.alloc("biasr2", 1024)
    p.dma("pool", WkT_sb.rearrange("p (a o) -> p a o", a=16),
          WkT.rearrange("(a p) o -> p a o", p=128), writes=["WkT"], tag="ldWk")
    p.dma("pool", WvT_sb.rearrange("p (a o) -> p a o", a=16),
          WvT.rearrange("(a p) o -> p a o", p=128), writes=["WvT"], tag="ldWv")
    p.dma("pool", bias_r[0:1, 0:512], bk_r, writes=["biasr2"], tag="ldbk2")
    p.dma("pool", bias_r[0:1, 512:1024], bv_r, writes=["biasr2"], tag="ldbv2")
    mem_sb = AB_.alloc("mem", H * 6 * 128)
    p.dma("pool", mem_sb.rearrange("p (h f d) -> p h f d", h=H, f=6),
          memory.rearrange("h (f p) d -> p h f d", p=128), writes=["mem"], tag="ldmem")
    nT_sb = AB_.alloc("nT", 6 * H)
    p.dma("pool", nT_sb.rearrange("p (a h) -> p a h", a=6),
          normT.rearrange("(a p) h -> p a h", p=128), writes=["nT"], tag="ldnT")
    dmem_acc = AF_.alloc("dmem_acc", H * 6 * 128)
    dnrm_acc = AF_.alloc("dnrm_acc", 6 * H)
    p.dve(lambda e: e.memset(dmem_acc, 0.0), writes=["dmem_acc"], tag="z1")
    p.dve(lambda e: e.memset(dnrm_acc, 0.0), writes=["dnrm_acc"], tag="z2")
    dump2 = AF_.alloc("dump2", FD)
    xc_f = AF_.alloc("xc", KVH * 256)
    mk_f = AF_.alloc("mk", KVH * FD)
    mkst = AF_.alloc("mkst", 16)
    mk16 = AB_.alloc("mk16", KVH * FD)
    mkT16 = AB_.alloc("mkT", KVH * FD)
    vv_f = AF_.alloc("vv", KVH * 128)
    den_f = AF_.alloc("den", 2 * H)
    wmv16 = AB_.alloc("wmv", DM)
    mbt16 = AB_.alloc("mbt16", DM)
    num_f = AF_.alloc("num", DM)
    coef16 = AB_.alloc("coef", 16)

    for tt in range(NT):
        p.dma("sp", h3_cur.rearrange("p (a t) -> p a t", a=16),
              o_h3T.rearrange("(a p) t -> p a t", p=128)[:, :, tt * 128:(tt + 1) * 128],
              reads=["h3T_dram"], writes=["h3c"], tag=f"h3l2_{tt}")
        p.dma("sp", mbt16, mb_dram[tt * 128:(tt + 1) * 128, :],
              reads=["mb_dram"], writes=["mbt16"], tag=f"mbl{tt}")
        hsl = lambda dt_i: h3_cur[:, dt_i * 128:(dt_i + 1) * 128]
        for dt_i in range(16):
            p.mm(big6[:, 0:512], hsl(dt_i), WkT_sb[:, dt_i * 512:(dt_i + 1) * 512],
                 start=(dt_i == 0), stop=False,
                 reads=["h3c", "WkT"], writes=["big6"], tag=f"bk{tt}")
        p.mm(big6[:, 0:512], onesr_sb[0:1, :], bias_r[0:1, 0:512], start=False, stop=True,
             reads=["onesr", "biasr2"], writes=["big6"], tag=f"bkb{tt}")
        for dt_i in range(16):
            p.mm(big6[:, 512:1024], hsl(dt_i), WvT_sb[:, dt_i * 512:(dt_i + 1) * 512],
                 start=(dt_i == 0), stop=False,
                 reads=["h3c", "WvT"], writes=["big6"], tag=f"bv{tt}")
        p.mm(big6[:, 512:1024], onesr_sb[0:1, :], bias_r[0:1, 512:1024],
             start=False, stop=True, reads=["onesr", "biasr2"], writes=["big6"], tag=f"bvb{tt}")
        x3 = xc_f.rearrange("p (kv c) -> p kv c", kv=KVH)
        k3 = big6[:, 0:512].rearrange("p (kv d) -> p kv d", kv=KVH)
        p.act(lambda e: e.activation(x3[:, :, 0:128], k3, AF.Relu, bias=cst[:, 0:1]),
              reads=["big6", "cst"], writes=["xc"], tag=f"rp{tt}")
        p.act(lambda e: e.activation(x3[:, :, 128:256], k3, AF.Relu, bias=cst[:, 0:1], scale=-1.0),
              reads=["big6", "cst"], writes=["xc"], tag=f"rn{tt}")
        p.dve(lambda e: e.tensor_copy(vv_f, big6[:, 512:1024]),
              reads=["big6"], writes=["vv"], tag=f"ve{tt}")
        mm3 = mk_f.rearrange("p (kv f) -> p kv f", kv=KVH)
        for c_i in range(3):
            sh = c_i + 1
            p.dve(lambda e, o=mm3[:, :, c_i * 256 + sh: c_i * 256 + 256],
                  a=x3[:, :, sh:256], b=x3[:, :, 0:256 - sh]:
                  e.tensor_tensor(o, a, b, ALU.mult),
                  reads=["xc"], writes=["mk"], tag=f"mka{tt}_{c_i}")
            p.dve(lambda e, o=mm3[:, :, c_i * 256: c_i * 256 + sh],
                  a=x3[:, :, 0:sh], b=x3[:, :, 256 - sh:256]:
                  e.tensor_tensor(o, a, b, ALU.mult),
                  reads=["xc"], writes=["mk"], tag=f"mkb{tt}_{c_i}")
        for kv in range(KVH):
            p.act(lambda e, i_=mm3[:, kv, :], a=mkst[:, kv:kv + 1]:
                  e.activation(dump2[:, 0:FD], i_, AF.Square, accum_out=a, bias=cst[:, 0:1]),
                  reads=["mk", "cst"], writes=["dump2", "mkst"], tag=f"mkq{tt}_{kv}")
        p.act(lambda e: e.activation(mkst[:, 4:8], mkst[:, 0:4], AF.Sqrt, bias=cst[:, 0:1]),
              reads=["mkst", "cst"], writes=["mkst"], tag=f"mks{tt}")
        p.dve(lambda e: e.tensor_scalar_max(mkst[:, 4:8], mkst[:, 4:8], 1e-12),
              reads=["mkst"], writes=["mkst"], tag=f"mkm{tt}")
        p.dve(lambda e: e.reciprocal(mkst[:, 8:12], mkst[:, 4:8]),
              reads=["mkst"], writes=["mkst"], tag=f"mkr{tt}")
        for kv in range(KVH):
            p.dve(lambda e, o=mm3[:, kv, :], sci=mkst[:, 8 + kv:9 + kv]:
                  e.tensor_scalar_mul(o, o, sci),
                  reads=["mk", "mkst"], writes=["mk"], tag=f"mkn{tt}_{kv}")
        p.dve(lambda e: e.tensor_copy(mk16, mk_f), reads=["mk"], writes=["mk16"], tag=f"mkc{tt}")
        for kv in range(KVH):
            p.act(lambda e, i_=mm3[:, kv, :], a=mkst[:, 12 + kv:13 + kv]:
                  e.activation(dump2[:, 0:FD], i_, AF.Square, accum_out=a, bias=cst[:, 0:1]),
                  reads=["dump2", "mk", "cst"], writes=["dump2", "mkst"], tag=f"msq{tt}_{kv}")
        for grp in range(3):
            for j in range(8):
                idx = grp * 8 + j
                p.tr(psT[:, j * 128:(j + 1) * 128],
                     mk16[:, idx * 128:(idx + 1) * 128], ident_bs,
                     reads=["mk16", "identb"], writes=["psT"], tag=f"mkt{tt}_{grp}_{j}")
            p.dve(lambda e, o=mkT16[:, grp * 1024:(grp + 1) * 1024], i_=psT[:, 0:1024]:
                  e.tensor_copy(o, i_), reads=["psT"], writes=["mkT"], tag=f"mke{tt}_{grp}")
        for hh in range(H):
            kv = hh // 4
            for fc in range(6):
                p.mm(big6[:, hh * 128:(hh + 1) * 128],
                     mkT16[:, kv * FD + fc * 128: kv * FD + (fc + 1) * 128],
                     mem_sb[:, (hh * 6 + fc) * 128:(hh * 6 + fc + 1) * 128],
                     start=(fc == 0), stop=(fc == 5),
                     reads=["mkT", "mem"], writes=["big6"], tag=f"nm{tt}_{hh}")
        for kv in range(KVH):
            for fc in range(6):
                p.mm(ps1[:, kv * 4:(kv + 1) * 4],
                     mkT16[:, kv * FD + fc * 128: kv * FD + (fc + 1) * 128],
                     nT_sb[:, fc * H + kv * 4: fc * H + (kv + 1) * 4],
                     start=(fc == 0), stop=(fc == 5),
                     reads=["mkT", "nT"], writes=["big6"], tag=f"dnm{tt}_{kv}")
        p.dve(lambda e: e.tensor_scalar(den_f[:, 0:H], ps1[:, 0:H], 1e-8, None, ALU.add),
              reads=["big6"], writes=["den"], tag=f"dne{tt}")
        p.dve(lambda e: e.reciprocal(den_f[:, H:2 * H], den_f[:, 0:H]),
              reads=["den"], writes=["den"], tag=f"dnr{tt}")
        p.dve(lambda e: e.tensor_copy(num_f, big6[:, 0:2048]),
              reads=["big6"], writes=["num"], tag=f"nme{tt}")
        if debug and tt == 0:
            p.dma("sp", d_num, num_f, reads=["num"], tag="dnum")
            p.dma("sp", d_den, den_f, reads=["den"], tag="dden")
            p.dma("sp", d_mk16, mk16, reads=["mk16"], tag="dmk16")
        for hh in range(H):
            kv = hh // 4
            p.dve(lambda e, o=num_f[:, hh * 128:(hh + 1) * 128], sci=den_f[:, H + hh:H + hh + 1]:
                  e.tensor_scalar_mul(o, o, sci), reads=["num", "den"], writes=["num"],
                  tag=f"pvv{tt}_{hh}")
            p.dve(lambda e, o=num_f[:, hh * 128:(hh + 1) * 128],
                  a=vv_f[:, kv * 128:(kv + 1) * 128]:
                  e.tensor_sub(o, a, o), reads=["num", "vv"], writes=["num"], tag=f"mv{tt}_{hh}")
        p.dve(lambda e: e.tensor_tensor(wmv16, num_f, mbt16, ALU.mult),
              reads=["num", "mbt16"], writes=["wmv"], tag=f"wmv{tt}")
        if debug and tt == 0:
            p.dma("sp", d_wmv, wmv16, reads=["wmv"], tag="dwmv")
        p.dve(lambda e: e.reciprocal(mkst[:, 0:4], mkst[:, 12:16]),
              reads=["mkst"], writes=["mkst"], tag=f"crc{tt}")
        for kv in range(KVH):
            p.dve(lambda e, o=den_f[:, kv * 4:(kv + 1) * 4], sci=mkst[:, 0 + kv:1 + kv]:
                  e.tensor_scalar_mul(o, o, sci),
                  reads=["den", "mkst"], writes=["den"], tag=f"cdd{tt}_{kv}")
        p.dve(lambda e: e.tensor_scalar(den_f[:, 0:H], den_f[:, 0:H], -1.0, 1.0, ALU.mult, ALU.add),
              reads=["den"], writes=["den"], tag=f"cm{tt}")
        p.dve(lambda e: e.tensor_scalar_max(den_f[:, 0:H], den_f[:, 0:H], 0.0),
              reads=["den"], writes=["den"], tag=f"cx0{tt}")
        p.dve(lambda e: e.tensor_scalar_min(den_f[:, 0:H], den_f[:, 0:H], 1.0),
              reads=["den"], writes=["den"], tag=f"cx1{tt}")
        p.dve(lambda e: e.tensor_copy(coef16[:, 0:H], den_f[:, 0:H]),
              reads=["den"], writes=["coef"], tag=f"cc2{tt}")
        for hh in range(H):
            kv = hh // 4
            for fc in range(6):
                p.mm(ps1[:, 0:128],
                     mk16[:, kv * FD + fc * 128: kv * FD + (fc + 1) * 128],
                     wmv16[:, hh * 128:(hh + 1) * 128],
                     start=True, stop=True,
                     reads=["mk16", "wmv"], writes=["big6"], tag=f"dm{tt}_{hh}_{fc}")
                p.dve(lambda e, o=dmem_acc[:, (hh * 6 + fc) * 128:(hh * 6 + fc + 1) * 128]:
                      e.tensor_add(o, o, ps1[:, 0:128]),
                      reads=["big6", "dmem_acc"], writes=["dmem_acc"], tag=f"dma{tt}_{hh}_{fc}")
        for kv in range(KVH):
            for fc in range(6):
                p.mm(ps1[:, 128:132],
                     mk16[:, kv * FD + fc * 128: kv * FD + (fc + 1) * 128],
                     coef16[:, kv * 4:(kv + 1) * 4],
                     start=True, stop=True,
                     reads=["mk16", "coef"], writes=["big6"], tag=f"dn2{tt}_{kv}_{fc}")
                p.dve(lambda e, o=dnrm_acc[:, (kv * 6 + fc) * 4:(kv * 6 + fc + 1) * 4]:
                      e.tensor_add(o, o, ps1[:, 128:132]),
                      reads=["big6", "dnrm_acc"], writes=["dnrm_acc"], tag=f"dna{tt}_{kv}_{fc}")

    if debug:
        p.dma("sp", d_mb, mb_dram, reads=["mb_dram"], tag="dmb")
    p.dma("sp", o_dmem, dmem_acc, reads=["dmem_acc"], writes=["bdm"], tag="sdm")
    p.dma("sp", o_dnorm, dnrm_acc, reads=["dnrm_acc"], writes=["bdn"], tag="sdn")
    AF_.pop(m_b8f)
    AB_.pop(m_b8b)


def build_launch_B(debug=False):
    nc = bass.Bass("TRN2", target_bir_lowering=False, debug=False)
    I = lambda n, s, d=F32: nc.dram_tensor(n, s, d, kind="ExternalInput").ap()
    O = lambda n, s, d=F32: nc.dram_tensor(n, s, d, kind="ExternalOutput").ap()

    xin = I("xin", [TL, DM])
    lgT = I("lgT", [128, GT])
    lgT_loc = I("lgT_loc", [128, NT])
    kr_all = I("kr_all", [S, KVH * Dh], BF16)
    kh_all = I("kh_all", [S, KVH * Dh], BF16)
    v_all = I("v_all", [S, KVH * Dh], BF16)
    qrT = I("qrT", [DM, TL], BF16)
    qspT = I("qspT", [NCN, DM, SK])
    wo = I("wo", [DM, DM])
    gate_wT = I("gate_wT", [DM, NE])
    gateT = I("gateT", [NE, DM, FF])
    upT = I("upT", [NE, DM, FF])
    downT = I("downT", [NE, FF, DM])
    WkT = I("WkT", [DM, KVH * Dh])
    WvT = I("WvT", [DM, KVH * Dh])
    WmbT = I("WmbT", [DM, DM])
    bk_r = I("bk_r", [1, KVH * Dh])
    bv_r = I("bv_r", [1, KVH * Dh])
    bmb_r = I("bmb_r", [1, DM])
    memory = I("memory", [H, FD, Dh])
    normT = I("normT", [FD, H])
    kbeT = I("kbeT", [256, SK])
    kab_full = I("kab_full", [32, 128, SK])
    ka_eL = I("ka_eL", [SK, 4])
    kb_e = I("kb_e", [SK, 256])
    qnw_col = I("qnw_col", [Dh, 1])
    knw_t4 = I("knw_t4", [1, KVH * Dh])
    ss_c = I("ss_c", [1, 1])
    ident_b = I("ident_b", [128, 128], BF16)
    ones_row = I("ones_row", [1, 128], BF16)
    ones_col = I("ones_col", [128, 1], BF16)
    one16 = I("one16", [1, 1], BF16)
    iota_row = I("iota_row", [128, 128])
    iota_r1 = I("iota_r1", [1, 512])
    ltri = I("ltri", [128, 128], BF16)
    utri64 = I("utri64", [64, 64], BF16)
    sel8 = I("sel8", [64, NSB], BF16)
    rep8 = I("rep8", [64, GT], BF16)

    o_x2 = O("o_x2", [TL, DM])
    o_h3T = O("o_h3T", [DM, TL], BF16)
    o_dmem = O("o_dmem", [128, H * 6 * 128])
    o_dnorm = O("o_dnorm", [128, 6 * H])

    p = Prog(nc)
    with ExitStack() as st:
        big6 = st.enter_context(nc.psum_tensor("big6", [128, 3584], F32))
        psT = st.enter_context(nc.psum_tensor("psT", [128, 1024], BF16))
        af_t = st.enter_context(nc.sbuf_tensor("af", [128, 26000], F32))
        ab_t = st.enter_context(nc.sbuf_tensor("ab", [128, 49500], BF16))
        AF_ = Arena(p, af_t, 26000)
        AB_ = Arena(p, ab_t, 49500)
        C = make_shared_consts(p, AF_, AB_, ident_b=ident_b, ones_row=ones_row,
                               ones_col=ones_col, one16=one16, ltri=ltri,
                               utri64=utri64, sel8=sel8, rep8=rep8,
                               iota_row=iota_row, iota_r1=iota_r1)
        lgT_load = lambda dst: p.dma("sp", dst, lgT, writes=["lg"], tag="ldlg")
        lgl_load = lambda dst: p.dma("sp", dst, lgT_loc, writes=["lgl"], tag="ldlgl")
        emit_phase_B(p, AF_, AB_, big6, psT, C, debug,
                     xin, kr_all, kh_all, v_all, qrT,
                     wo, gate_wT, gateT, upT, downT, WkT, WvT, WmbT, bk_r, bv_r, bmb_r,
                     memory, normT, kbeT, kab_full, ka_eL, kb_e, qnw_col, knw_t4, ss_c,
                     o_x2, o_h3T, o_dmem, o_dnorm,
                     lgT_load, lgl_load, qspT=qspT, O=O)
        p.emit()
    return nc


def host_prep_B(inputs, resA):
    g = lambda k: np.asarray(inputs[k])
    x = g("L0_attn_a_hidden_states")[0]
    logits = np.concatenate([np.asarray(resA[c]["o_logits"][0]) for c in range(NCN)])
    lgT = _c(logits.reshape(GT, 128).T)
    kr_all = np.concatenate([np.asarray(resA[c]["o_kr"]) for c in range(NCN)], 0)
    kh_all = np.concatenate([np.asarray(resA[c]["o_kh"]) for c in range(NCN)], 0)
    v_all = np.concatenate([np.asarray(resA[c]["o_v"]) for c in range(NCN)], 0)
    qspT = np.stack([_c(np.asarray(resA[c]["o_qsp"]).T) for c in range(NCN)])
    ka, kb = g("L0_attn_a_kron_a"), g("L0_attn_a_kron_b")
    ika = np.arange(SK) % 20
    ikb = np.arange(SK) // 20
    ka_expand = _c(ka.T[:, ika])
    kab_full = _c(np.broadcast_to(ka_expand[:32, None, :], (32, 128, SK)).astype(np.float32))
    kb_expandT = _c(kb[ikb].T)
    kb_e = _c(kb[ikb])
    qnw, knw = g("L0_attn_a_q_norm_w"), g("L0_attn_a_k_norm_w")
    ident = np.eye(128, dtype=BF)
    ones_row = np.ones((1, 128), BF)
    ones_col = np.ones((128, 1), BF)
    one16 = np.ones((1, 1), BF)
    iota_row = _c(np.broadcast_to(np.arange(128, dtype=np.float32)[None, :], (128, 128)))
    iota_r1 = _c(np.arange(512, dtype=np.float32).reshape(1, 512))
    k_i = np.arange(128)
    ltri = _c((k_i[:, None] <= k_i[None, :]).astype(BF))
    k64 = np.arange(64)
    utri64 = _c((k64[:, None] < k64[None, :]).astype(BF))
    tile_super = np.zeros(GT, np.int64)
    for sb, (st0, ntl, cap) in enumerate(SUP):
        tile_super[st0:st0 + ntl] = sb
    sel8 = np.zeros((64, NSB), BF)
    sel8[np.arange(GT), tile_super] = 1
    rep8 = np.zeros((64, GT), BF)
    first_tile = np.array([SUP[s][0] for s in tile_super])
    rep8[first_tile, np.arange(GT)] = 1
    maps = []
    base = {
        "lgT": lgT, "kr_all": kr_all, "kh_all": kh_all, "v_all": v_all, "qspT": qspT,
        "wo": g("L0_attn_a_wo"),
        "gate_wT": _c(g("L0_ffn_gate_w").T),
        "gateT": _c(np.swapaxes(g("L0_ffn_gate_proj"), 1, 2)),
        "upT": _c(np.swapaxes(g("L0_ffn_up_proj"), 1, 2)),
        "downT": _c(np.swapaxes(g("L0_ffn_down_proj"), 1, 2)),
        "WkT": _c(g("L0_attn_b_Wk").T), "WvT": _c(g("L0_attn_b_Wv").T),
        "WmbT": _c(g("L0_attn_b_Wmb").T),
        "bk_r": _c(g("L0_attn_b_bk").reshape(1, -1)),
        "bv_r": _c(g("L0_attn_b_bv").reshape(1, -1)),
        "bmb_r": _c(g("L0_attn_b_bmb").reshape(1, -1)),
        "memory": g("L0_attn_b_memory"), "normT": _c(g("L0_attn_b_norm").T),
        "kbeT": kb_expandT, "kab_full": kab_full, "kb_e": kb_e,
        "qnw_col": _c(qnw.reshape(Dh, 1)),
        "knw_t4": _c(np.tile(knw, KVH).reshape(1, -1)),
        "ss_c": _c(np.asarray(inputs["L0_attn_a_sketch_scale"]).reshape(1, 1)),
        "ident_b": ident, "ones_row": ones_row, "ones_col": ones_col, "one16": one16,
        "iota_row": iota_row, "iota_r1": iota_r1, "ltri": ltri, "utri64": utri64,
        "sel8": sel8, "rep8": rep8,
    }
    for c in range(NCN):
        m = dict(base)
        m["xin"] = _c(x[c * TL:(c + 1) * TL])
        m["lgT_loc"] = _c(lgT[:, c * NT:(c + 1) * NT])
        m["qrT"] = np.asarray(resA[c]["o_qrT"])
        m["ka_eL"] = _c(ka_expand[:32].T[:, 4 * c:4 * c + 4])
        maps.append(m)
    return maps


def run_B(inputs, resA, debug=False):
    nc = build_launch_B(debug=debug)
    maps = host_prep_B(inputs, resA)
    res = run_bass_kernel_spmd(nc, maps, core_ids=list(range(NCN)))
    return res.results


def emit_phase_C(p, AF_, AB_, big6, psT, C, debug,
                 x2in, h3Tin, memL, nrmL, WqT, WgT, bq_r, bg_r,
                 gate_wT, gateT, upT, downT, o_out,
                 dmp=None, dnp=None, r_dm=None, r_dn=None, O=None):
    nc = p.nc
    ps1 = big6[:, 3072:3584]
    cst = C["cst"]; ident_bs = C["ident_bs"]; onesr_sb = C["onesr_sb"]
    stat = C["stat"]; dump = C["dump"]; tw_all = C["tw_all"]; rl_f = C["rl_f"]
    mh_dram = nc.dram_tensor("mh_dram", [TL, DM], F32).ap()
    if debug:
        d_nm = O("d_nm", [128, H * 6 * 128])
        d_nnT = O("d_nnT", [128, 6 * H])
        d_mh = O("d_mh", [TL, DM])
        d_x3 = O("d_x3", [TL, DM])
        d_tw2 = O("d_tw2", [TL, NE])

    s1 = stat[:, 0:1]; s2 = stat[:, 1:2]; m_t = stat[:, 2:3]; r_t = stat[:, 3:4]

    def ln_tile_ops(xap, bx, tag):
        p.act(lambda e: e.activation(dump, xap, AF.Square, accum_out=s2, bias=cst[:, 0:1]),
              reads=[bx, "cst"], writes=["dump", "stat"], tag=f"{tag}sq")
        p.dve(lambda e: e.reduce_sum(s1, xap, axis=AX), reads=[bx], writes=["stat"], tag=f"{tag}s1")
        p.dve(lambda e: e.tensor_scalar_mul(m_t, s1, 1.0 / DM), reads=["stat"], writes=["stat"], tag=f"{tag}m")
        p.dve(lambda e: e.tensor_scalar_mul(s2, s2, 1.0 / DM), reads=["stat"], writes=["stat"], tag=f"{tag}v1")
        p.dve(lambda e: e.tensor_tensor(s1, m_t, m_t, ALU.mult), reads=["stat"], writes=["stat"], tag=f"{tag}v2")
        p.dve(lambda e: e.tensor_sub(s2, s2, s1), reads=["stat"], writes=["stat"], tag=f"{tag}v3")
        p.act(lambda e: e.activation(s2, s2, AF.Sqrt, bias=cst[:, 1:2]),
              reads=["stat", "cst"], writes=["stat"], tag=f"{tag}sd")
        p.dve(lambda e: e.reciprocal(r_t, s2), reads=["stat"], writes=["stat"], tag=f"{tag}r")

    m1f = AF_.mark()
    nm_acc = AF_.alloc("nm_acc", H * 6 * 128)
    nn_acc = AF_.alloc("nn_acc", 6 * H)
    stg = AF_.alloc("stg", 6144)
    nnstg = AF_.alloc("nnstg", 6 * H)
    nnT_f = AF_.alloc("nnT_f", 6 * H)
    p.dma("sp", nm_acc, memL, writes=["nm_acc"], tag="ldmem")
    p.dma("sp", nn_acc, nrmL, writes=["nn_acc"], tag="ldnrm")
    if dmp is not None:
        srcs = [(dmp[c], dnp[c], f"{c}", [], []) for c in range(NCN)]
    else:
        srcs = [(r_dm, r_dn, "r", ["rdm"], ["rdn"])]
    for dmsrc, dnsrc, ctag, rdm_nm, rdn_nm in srcs:
        for hh in range(2):
            p.dma("sp", stg, dmsrc[:, hh * 6144:(hh + 1) * 6144],
                  reads=rdm_nm, writes=["stg"], tag=f"lddm{ctag}_{hh}")
            p.dve(lambda e, o=nm_acc[:, hh * 6144:(hh + 1) * 6144]:
                  e.tensor_add(o, o, stg), reads=["stg", "nm_acc"], writes=["nm_acc"],
                  tag=f"adm{ctag}_{hh}")
        p.dma("sp", nnstg, dnsrc, reads=rdn_nm, writes=["nnstg"], tag=f"lddn{ctag}")
        p.dve(lambda e: e.tensor_add(nn_acc, nn_acc, nnstg),
              reads=["nnstg", "nn_acc"], writes=["nn_acc"], tag=f"adn{ctag}")
    for kv in range(4):
        p.dve(lambda e, o=nnT_f.rearrange("p (a h) -> p a h", a=6)[:, :, kv * 4:(kv + 1) * 4],
              i_=nn_acc.rearrange("p (k f j) -> p k f j", k=4, f=6)[:, kv]:
              e.tensor_copy(o, i_), reads=["nn_acc"], writes=["nnT_f"], tag=f"nnp{kv}")
    m1b = AB_.mark()
    nm16 = AB_.alloc("nm16", H * 6 * 128)
    nnT16 = AB_.alloc("nnT16", 6 * H)
    p.dve(lambda e: e.tensor_copy(nnT16, nnT_f), reads=["nnT_f"], writes=["nnT16"], tag="nnc")
    p.dve(lambda e: e.tensor_copy(nm16, nm_acc), reads=["nm_acc"], writes=["nm16"], tag="nmc")
    if debug:
        p.dma("sp", d_nm, nm_acc, reads=["nm_acc"], tag="dnm")
        p.dma("sp", d_nnT, nnT_f, reads=["nnT_f"], tag="dnn")
    AF_.pop(m1f)

    x1 = AF_.alloc("x1", NT * DM)
    p.dma("sp", x1.rearrange("p (a d) -> p a d", a=NT),
          x2in.rearrange("(a p) d -> p a d", p=128),
          reads=["x2_dram"], writes=["x1"], tag="ldx2")

    m2b = AB_.mark()
    h3c = AB_.alloc("h3c", 16 * 128)
    wqh = AB_.alloc("wqh", 16 * 1024)
    bias_r = AB_.alloc("biasr", 1024)
    mk16c = AB_.alloc("mk16", KVH * FD)
    mqT16 = AB_.alloc("mqT", KVH * FD)
    m2f = AF_.mark()
    xc_f = AF_.alloc("xc", KVH * 256)
    mq_f = AF_.alloc("mq", KVH * FD)
    mkst = AF_.alloc("mkst", 16)
    den_f = AF_.alloc("den", 16)
    mh_half = AF_.alloc("mhh", 1024)
    for hf in range(2):
        p.dma("pool", wqh.rearrange("p (a o) -> p a o", a=16),
              WqT.rearrange("(a p) o -> p a o", p=128)[:, :, hf * 1024:(hf + 1) * 1024],
              writes=["wqh"], tag=f"ldwq{hf}")
        p.dma("pool", bias_r[0:1, :], bq_r[:, hf * 1024:(hf + 1) * 1024],
              writes=["biasr"], tag=f"ldbq{hf}")
        for tt in range(NT):
            p.dma("sp", h3c.rearrange("p (a t) -> p a t", a=16),
                  h3Tin.rearrange("(a p) t -> p a t", p=128)[:, :, tt * 128:(tt + 1) * 128],
                  reads=["h3T_dram"], writes=["h3c"], tag=f"h3l{hf}_{tt}")
            for chm in range(2):
                for dt_i in range(16):
                    p.mm(big6[:, chm * 512:(chm + 1) * 512],
                         h3c[:, dt_i * 128:(dt_i + 1) * 128],
                         wqh[:, dt_i * 1024 + chm * 512: dt_i * 1024 + (chm + 1) * 512],
                         start=(dt_i == 0), stop=False,
                         reads=["h3c", "wqh"], writes=["big6"], tag=f"qm{hf}_{tt}_{chm}")
                p.mm(big6[:, chm * 512:(chm + 1) * 512], onesr_sb[0:1, :],
                     bias_r[0:1, chm * 512:(chm + 1) * 512], start=False, stop=True,
                     reads=["onesr", "biasr"], writes=["big6"], tag=f"qb{hf}_{tt}_{chm}")
            for qq in range(2):
                xcv = xc_f.rearrange("p (kv c) -> p kv c", kv=KVH)
                k3 = big6[:, qq * 512:(qq + 1) * 512].rearrange("p (kv d) -> p kv d", kv=KVH)
                p.act(lambda e, k=k3: e.activation(xcv[:, :, 0:128], k, AF.Relu, bias=cst[:, 0:1]),
                      reads=["big6", "cst"], writes=["xc"], tag=f"rp{hf}_{tt}_{qq}")
                p.act(lambda e, k=k3: e.activation(xcv[:, :, 128:256], k, AF.Relu,
                                                   bias=cst[:, 0:1], scale=-1.0),
                      reads=["big6", "cst"], writes=["xc"], tag=f"rn{hf}_{tt}_{qq}")
                mm3 = mq_f.rearrange("p (kv f) -> p kv f", kv=KVH)
                for c_i in range(3):
                    sh = c_i + 1
                    p.dve(lambda e, o=mm3[:, :, c_i * 256 + sh: c_i * 256 + 256],
                          a=xcv[:, :, sh:256], b=xcv[:, :, 0:256 - sh]:
                          e.tensor_tensor(o, a, b, ALU.mult),
                          reads=["xc"], writes=["mq"], tag=f"mka{hf}{tt}{qq}_{c_i}")
                    p.dve(lambda e, o=mm3[:, :, c_i * 256: c_i * 256 + sh],
                          a=xcv[:, :, 0:sh], b=xcv[:, :, 256 - sh:256]:
                          e.tensor_tensor(o, a, b, ALU.mult),
                          reads=["xc"], writes=["mq"], tag=f"mkb{hf}{tt}{qq}_{c_i}")
                for kv in range(KVH):
                    p.act(lambda e, i_=mm3[:, kv, :], a=mkst[:, kv:kv + 1]:
                          e.activation(dump[:, 0:FD], i_, AF.Square, accum_out=a, bias=cst[:, 0:1]),
                          reads=["mq", "cst"], writes=["dump", "mkst"], tag=f"mkq{hf}{tt}{qq}_{kv}")
                p.act(lambda e: e.activation(mkst[:, 4:8], mkst[:, 0:4], AF.Sqrt, bias=cst[:, 0:1]),
                      reads=["mkst", "cst"], writes=["mkst"], tag=f"mks{hf}{tt}{qq}")
                p.dve(lambda e: e.tensor_scalar_max(mkst[:, 4:8], mkst[:, 4:8], 1e-12),
                      reads=["mkst"], writes=["mkst"], tag=f"mkm{hf}{tt}{qq}")
                p.dve(lambda e: e.reciprocal(mkst[:, 8:12], mkst[:, 4:8]),
                      reads=["mkst"], writes=["mkst"], tag=f"mkr{hf}{tt}{qq}")
                for kv in range(KVH):
                    p.dve(lambda e, o=mm3[:, kv, :], sci=mkst[:, 8 + kv:9 + kv]:
                          e.tensor_scalar_mul(o, o, sci),
                          reads=["mq", "mkst"], writes=["mq"], tag=f"mkn{hf}{tt}{qq}_{kv}")
                p.dve(lambda e: e.tensor_copy(mk16c, mq_f), reads=["mq"], writes=["mk16"],
                      tag=f"mkc{hf}{tt}{qq}")
                for grp in range(3):
                    for j in range(8):
                        idx = grp * 8 + j
                        p.tr(psT[:, j * 128:(j + 1) * 128],
                             mk16c[:, idx * 128:(idx + 1) * 128], ident_bs,
                             reads=["mk16", "identb"], writes=["psT"], tag=f"mkt{hf}{tt}{qq}_{grp}_{j}")
                    p.dve(lambda e, o=mqT16[:, grp * 1024:(grp + 1) * 1024], i_=psT[:, 0:1024]:
                          e.tensor_copy(o, i_), reads=["psT"], writes=["mqT"], tag=f"mke{hf}{tt}{qq}_{grp}")
                for j in range(4):
                    hglob = hf * 8 + qq * 4 + j
                    for fc in range(6):
                        p.mm(big6[:, 1024 + j * 128: 1024 + (j + 1) * 128],
                             mqT16[:, j * FD + fc * 128: j * FD + (fc + 1) * 128],
                             nm16[:, (hglob * 6 + fc) * 128:(hglob * 6 + fc + 1) * 128],
                             start=(fc == 0), stop=(fc == 5),
                             reads=["mqT", "nm16"], writes=["big6"], tag=f"num{hf}{tt}{qq}_{j}")
                    for fc in range(6):
                        p.mm(ps1[:, j:j + 1],
                             mqT16[:, j * FD + fc * 128: j * FD + (fc + 1) * 128],
                             nnT16[:, fc * H + hglob: fc * H + hglob + 1],
                             start=(fc == 0), stop=(fc == 5),
                             reads=["mqT", "nnT16"], writes=["big6"], tag=f"dnc{hf}{tt}{qq}_{j}")
                p.dve(lambda e: e.tensor_scalar(den_f[:, 0:4], ps1[:, 0:4], 1e-8, None, ALU.add),
                      reads=["big6"], writes=["den"], tag=f"dne{hf}{tt}{qq}")
                p.dve(lambda e: e.reciprocal(den_f[:, 4:8], den_f[:, 0:4]),
                      reads=["den"], writes=["den"], tag=f"dnr{hf}{tt}{qq}")
                for j in range(4):
                    hloc = qq * 4 + j
                    p.dve(lambda e, o=mh_half[:, hloc * 128:(hloc + 1) * 128],
                          i_=big6[:, 1024 + j * 128:1024 + (j + 1) * 128]:
                          e.tensor_copy(o, i_), reads=["big6"], writes=["mhh"],
                          tag=f"mhe{hf}{tt}{qq}_{j}")
                    p.dve(lambda e, o=mh_half[:, hloc * 128:(hloc + 1) * 128],
                          sci=den_f[:, 4 + j:5 + j]:
                          e.tensor_scalar_mul(o, o, sci), reads=["mhh", "den"], writes=["mhh"],
                          tag=f"mhd{hf}{tt}{qq}_{j}")
            p.dma("sp", mh_dram[tt * 128:(tt + 1) * 128, hf * 1024:(hf + 1) * 1024],
                  mh_half, reads=["mhh"], writes=["mh_dram"], tag=f"smh{hf}_{tt}")
    AB_.pop(m1b)
    AF_.pop(m2f)

    m3b = AB_.mark()
    h3c = AB_.alloc("h3c", 16 * 128)
    wgh = AB_.alloc("wgh", 16 * 1024)
    bias_r = AB_.alloc("biasr2", 1024)
    m3f = AF_.mark()
    gate_f = AF_.alloc("gatef", 1024)
    mh_st = AF_.alloc("mhst", 1024)
    tmp_h = AF_.alloc("tmph", DM)
    for hf in range(2):
        p.dma("pool", wgh.rearrange("p (a o) -> p a o", a=16),
              WgT.rearrange("(a p) o -> p a o", p=128)[:, :, hf * 1024:(hf + 1) * 1024],
              writes=["wgh"], tag=f"ldwg{hf}")
        p.dma("pool", bias_r[0:1, :], bg_r[:, hf * 1024:(hf + 1) * 1024],
              writes=["biasr2"], tag=f"ldbg{hf}")
        for tt in range(NT):
            xap = x1[:, tt * DM:(tt + 1) * DM]
            p.dma("sp", h3c.rearrange("p (a t) -> p a t", a=16),
                  h3Tin.rearrange("(a p) t -> p a t", p=128)[:, :, tt * 128:(tt + 1) * 128],
                  reads=["h3T_dram"], writes=["h3c"], tag=f"h3g{hf}_{tt}")
            if hf == 0:
                ln_tile_ops(xap, "x1", f"lnc{tt}")
                p.dve(lambda e, x=xap: e.tensor_scalar(tmp_h, x, m_t, r_t, ALU.subtract, ALU.mult),
                      reads=["x1", "stat"], writes=["tmph"], tag=f"lncn{tt}")
            for chm in range(2):
                for dt_i in range(16):
                    p.mm(big6[:, chm * 512:(chm + 1) * 512],
                         h3c[:, dt_i * 128:(dt_i + 1) * 128],
                         wgh[:, dt_i * 1024 + chm * 512: dt_i * 1024 + (chm + 1) * 512],
                         start=(dt_i == 0), stop=False,
                         reads=["h3c", "wgh"], writes=["big6"], tag=f"gm{hf}_{tt}_{chm}")
                p.mm(big6[:, chm * 512:(chm + 1) * 512], onesr_sb[0:1, :],
                     bias_r[0:1, chm * 512:(chm + 1) * 512], start=False, stop=True,
                     reads=["onesr", "biasr2"], writes=["big6"], tag=f"gb{hf}_{tt}_{chm}")
            p.act(lambda e: e.activation(gate_f, big6[:, 0:1024], AF.Sigmoid, bias=cst[:, 0:1]),
                  reads=["big6", "cst"], writes=["gatef"], tag=f"gs{hf}_{tt}")
            p.dma("sp", mh_st, mh_dram[tt * 128:(tt + 1) * 128, hf * 1024:(hf + 1) * 1024],
                  reads=["mh_dram"], writes=["mhst"], tag=f"lmh{hf}_{tt}")
            p.dve(lambda e: e.tensor_tensor(gate_f, gate_f, mh_st, ALU.mult),
                  reads=["gatef", "mhst"], writes=["gatef"], tag=f"gmm{hf}_{tt}")
            if hf == 0:
                p.dve(lambda e, x=xap: e.tensor_add(x, x, tmp_h),
                      reads=["x1", "tmph"], writes=["x1"], tag=f"xh{tt}")
            p.dve(lambda e, o=x1[:, tt * DM + hf * 1024: tt * DM + (hf + 1) * 1024]:
                  e.tensor_add(o, o, gate_f), reads=["x1", "gatef"], writes=["x1"],
                  tag=f"xg{hf}_{tt}")
    if debug:
        p.dma("sp", d_mh, mh_dram, reads=["mh_dram"], tag="dmh")
        for tt in range(NT):
            p.dma("sp", d_x3[tt * 128:(tt + 1) * 128, :], x1[:, tt * DM:(tt + 1) * DM],
                  reads=["x1"], tag=f"dx3{tt}")
    AB_.pop(m3b)
    AF_.pop(m3f)

    h2t = AB_.alloc("h2t", DM)
    gwT_sb = AB_.alloc("gwT", 16 * NE)
    hT2 = AB_.alloc("hT2", 16 * TL)
    p.dma("pool", gwT_sb.rearrange("p (a o) -> p a o", a=16),
          gate_wT.rearrange("(a p) o -> p a o", p=128), writes=["gwT"], tag="ldgw")
    for tt in range(NT):
        xap = x1[:, tt * DM:(tt + 1) * DM]
        ln_tile_ops(xap, "x1", f"ln4_{tt}")
        p.dve(lambda e, x=xap: e.tensor_scalar(h2t, x, m_t, r_t, ALU.subtract, ALU.mult),
              reads=["x1", "stat"], writes=["h2t"], tag=f"ln4n{tt}")
        for g in range(2):
            for d8 in range(8):
                dt_i = g * 8 + d8
                p.tr(psT[:, d8 * 128:(d8 + 1) * 128], h2t[:, dt_i * 128:(dt_i + 1) * 128],
                     ident_bs, reads=["h2t", "identb"], writes=["psT"], tag=f"h4tr{tt}")
            dst = hT2.rearrange("p (a t) -> p a t", a=16)[:, g * 8:(g + 1) * 8,
                                                         tt * 128:(tt + 1) * 128]
            p.dve(lambda e, o=dst, i_=psT[:, 0:1024].rearrange("p (a t) -> p a t", a=8):
                  e.tensor_copy(o, i_), reads=["psT"], writes=["hT2"], tag=f"h4ev{tt}")
        for dt_i in range(16):
            p.mm(ps1[:, 0:NE], hT2.rearrange("p (a t) -> p a t", a=16)[:, dt_i,
                                                                      tt * 128:(tt + 1) * 128],
                 gwT_sb[:, dt_i * NE:(dt_i + 1) * NE],
                 start=(dt_i == 0), stop=(dt_i == 15),
                 reads=["hT2", "gwT"], writes=["big6"], tag=f"rt{tt}")
        p.dve(lambda e: e.tensor_copy(rl_f[:, 0:NE], ps1[:, 0:NE]),
              reads=["big6"], writes=["rl"], tag=f"rte{tt}")
        p.dve(lambda e: e.reduce_max(rl_f[:, 8:9], rl_f[:, 0:NE], axis=AX),
              reads=["rl"], writes=["rl"], tag=f"rm1{tt}")
        p.dve(lambda e: e.tensor_scalar(dump[:, 0:NE], rl_f[:, 0:NE], rl_f[:, 8:9], None,
                                        ALU.is_equal),
              reads=["rl"], writes=["dump"], tag=f"re1{tt}")
        p.dve(lambda e: e.scalar_tensor_tensor(dump[:, 8:16], dump[:, 0:NE], -1e9,
                                               rl_f[:, 0:NE], ALU.mult, ALU.add),
              reads=["dump", "rl"], writes=["dump"], tag=f"rl2{tt}")
        p.dve(lambda e: e.reduce_max(rl_f[:, 9:10], dump[:, 8:16], axis=AX),
              reads=["dump"], writes=["rl"], tag=f"rm2{tt}")
        p.dve(lambda e: e.tensor_scalar(dump[:, 16:24], dump[:, 8:16], rl_f[:, 9:10], None,
                                        ALU.is_equal),
              reads=["dump", "rl"], writes=["dump"], tag=f"re2{tt}")
        p.dve(lambda e: e.tensor_tensor(rl_f[:, 10:11], rl_f[:, 8:9], rl_f[:, 9:10], ALU.subtract),
              reads=["rl"], writes=["rl"], tag=f"rd{tt}")
        p.act(lambda e: e.activation(rl_f[:, 11:12], rl_f[:, 10:11], AF.Sigmoid, bias=cst[:, 0:1]),
              reads=["rl", "cst"], writes=["rl"], tag=f"rp1{tt}")
        p.dve(lambda e: e.tensor_scalar(rl_f[:, 12:13], rl_f[:, 11:12], -1.0, 1.0,
                                        ALU.mult, ALU.add),
              reads=["rl"], writes=["rl"], tag=f"rp2{tt}")
        p.dve(lambda e, o=tw_all[:, tt * NE:(tt + 1) * NE]:
              e.tensor_scalar_mul(o, dump[:, 0:NE], rl_f[:, 11:12]),
              reads=["dump", "rl"], writes=["tw"], tag=f"tw1{tt}")
        p.dve(lambda e: e.tensor_scalar_mul(dump[:, 16:24], dump[:, 16:24], rl_f[:, 12:13]),
              reads=["dump", "rl"], writes=["dump"], tag=f"tw2{tt}")
        p.dve(lambda e, o=tw_all[:, tt * NE:(tt + 1) * NE]:
              e.tensor_add(o, o, dump[:, 16:24]),
              reads=["tw", "dump"], writes=["tw"], tag=f"tw3{tt}")
    if debug:
        p.dma("sp", d_tw2.rearrange("(t p) e -> p t e", p=128),
              tw_all.rearrange("p (t e) -> p t e", t=NT), reads=["tw"], tag="dtw")

    m4b = AB_.mark()
    wsl = [AB_.alloc(f"wsl{i}", 2 * 16 * 128) for i in range(2)]
    hTf = AB_.alloc("hTf", 2 * 11 * 512)
    sil = AB_.alloc("sil", 512)
    for e_i in range(NE):
        for ft in range(11):
            sl = wsl[ft % 2]
            sn = f"wsl{ft % 2}"
            p.dma("pool", sl[:, 0:2048].rearrange("p (a f) -> p a f", a=16),
                  gateT[e_i].rearrange("(a p) f -> p a f", p=128)[:, :, ft * 128:(ft + 1) * 128],
                  writes=[sn], tag=f"fg{e_i}_{ft}")
            p.dma("pool", sl[:, 2048:4096].rearrange("p (a f) -> p a f", a=16),
                  upT[e_i].rearrange("(a p) f -> p a f", p=128)[:, :, ft * 128:(ft + 1) * 128],
                  writes=[sn], tag=f"fu{e_i}_{ft}")
            for ch in range(2):
                for dt_i in range(16):
                    p.mm(big6[:, ch * 1024: ch * 1024 + 512],
                         sl[:, dt_i * 128:(dt_i + 1) * 128],
                         hT2[:, dt_i * TL + ch * 512: dt_i * TL + (ch + 1) * 512],
                         start=(dt_i == 0), stop=(dt_i == 15),
                         reads=[sn, "hT2"], writes=["big6"], tag=f"gm{e_i}_{ch}_{ft}")
                for dt_i in range(16):
                    p.mm(big6[:, ch * 1024 + 512: ch * 1024 + 1024],
                         sl[:, 2048 + dt_i * 128: 2048 + (dt_i + 1) * 128],
                         hT2[:, dt_i * TL + ch * 512: dt_i * TL + (ch + 1) * 512],
                         start=(dt_i == 0), stop=(dt_i == 15),
                         reads=[sn, "hT2"], writes=["big6"], tag=f"um{e_i}_{ch}_{ft}")
                p.act(lambda e, ch=ch: e.activation(sil, big6[:, ch * 1024: ch * 1024 + 512],
                                                    AF.Silu, bias=cst[:, 0:1]),
                      reads=["big6", "cst"], writes=["sil"], tag=f"si{e_i}_{ch}_{ft}")
                p.dve(lambda e, o=hTf[:, ch * 5632 + ft * 512: ch * 5632 + (ft + 1) * 512],
                      i_=big6[:, ch * 1024 + 512: ch * 1024 + 1024]:
                      e.tensor_tensor(o, i_, sil, ALU.mult),
                      reads=["big6", "sil"], writes=["hTf"], tag=f"hm{e_i}_{ch}_{ft}")
        for dp in range(8):
            sl = wsl[dp % 2]
            sn = f"wsl{dp % 2}"
            p.dma("pool", sl[:, 0:11 * 256].rearrange("p (a f) -> p a f", a=11),
                  downT[e_i].rearrange("(a p) o -> p a o", p=128)[:, :, dp * 256:(dp + 1) * 256],
                  writes=[sn], tag=f"fd{e_i}_{dp}")
            for ch in range(2):
                for do in range(2):
                    for ft in range(11):
                        p.mm(big6[:, (ch * 2 + do) * 512:(ch * 2 + do + 1) * 512],
                             sl[:, ft * 256 + do * 128: ft * 256 + (do + 1) * 128],
                             hTf[:, ch * 5632 + ft * 512: ch * 5632 + (ft + 1) * 512],
                             start=(ft == 0), stop=(ft == 10),
                             reads=[sn, "hTf"], writes=["big6"], tag=f"dm{e_i}_{ch}_{dp}_{do}")
            for ch in range(2):
                for j in range(4):
                    tt = ch * 4 + j
                    for do in range(2):
                        p.dve(lambda e, o=sil[:, 0:128],
                              i_=big6[:, (ch * 2 + do) * 512 + j * 128:
                                      (ch * 2 + do) * 512 + (j + 1) * 128]:
                              e.tensor_copy(o, i_), reads=["big6"], writes=["sil"],
                              tag=f"de{e_i}_{ch}_{dp}_{do}_{j}")
                        p.tr(psT[:, do * 128:(do + 1) * 128], sil[:, 0:128], ident_bs,
                             reads=["sil", "identb"], writes=["psT"], tag=f"dt{e_i}_{ch}_{dp}_{do}_{j}")
                    p.dve(lambda e, o=x1[:, tt * DM + dp * 256: tt * DM + (dp + 1) * 256],
                          i_=psT[:, 0:256],
                          sci=tw_all[:, tt * NE + e_i: tt * NE + e_i + 1]:
                          e.scalar_tensor_tensor(o, i_, sci, o, ALU.mult, ALU.add),
                          reads=["psT", "tw", "x1"], writes=["x1"], tag=f"da{e_i}_{ch}_{dp}_{j}")
    for tt in range(NT):
        p.dma("sp", o_out[tt * 128:(tt + 1) * 128, :], x1[:, tt * DM:(tt + 1) * DM],
              reads=["x1"], tag=f"sx4{tt}")
    AB_.pop(m4b)


def build_launch_C(debug=False):
    nc = bass.Bass("TRN2", target_bir_lowering=False, debug=False)
    I = lambda n, s, d=F32: nc.dram_tensor(n, s, d, kind="ExternalInput").ap()
    O = lambda n, s, d=F32: nc.dram_tensor(n, s, d, kind="ExternalOutput").ap()

    x2in = I("x2in", [TL, DM])
    h3Tin = I("h3Tin", [DM, TL], BF16)
    dmp = I("dmp", [NCN, 128, H * 6 * 128])
    dnp = I("dnp", [NCN, 128, 6 * H])
    memL = I("memL", [128, H * 6 * 128])
    nrmL = I("nrmL", [128, 6 * H])
    WqT = I("WqT", [DM, DM])
    WgT = I("WgT", [DM, DM])
    bq_r = I("bq_r", [1, DM])
    bg_r = I("bg_r", [1, DM])
    gate_wT = I("gate_wT", [DM, NE])
    gateT = I("gateT", [NE, DM, FF])
    upT = I("upT", [NE, DM, FF])
    downT = I("downT", [NE, FF, DM])
    ident_b = I("ident_b", [128, 128], BF16)
    ones_row = I("ones_row", [1, 128], BF16)

    o_out = O("o_out", [TL, DM])

    p = Prog(nc)
    with ExitStack() as st:
        big6 = st.enter_context(nc.psum_tensor("big6", [128, 3584], F32))
        psT = st.enter_context(nc.psum_tensor("psT", [128, 1024], BF16))
        af_t = st.enter_context(nc.sbuf_tensor("af", [128, 26000], F32))
        ab_t = st.enter_context(nc.sbuf_tensor("ab", [128, 49500], BF16))
        AF_ = Arena(p, af_t, 26000)
        AB_ = Arena(p, ab_t, 49500)
        C = make_shared_consts(p, AF_, AB_, ident_b=ident_b, ones_row=ones_row)
        emit_phase_C(p, AF_, AB_, big6, psT, C, debug,
                     x2in, h3Tin, memL, nrmL, WqT, WgT, bq_r, bg_r,
                     gate_wT, gateT, upT, downT, o_out,
                     dmp=dmp, dnp=dnp, O=O)
        p.emit()
    return nc


def host_prep_C(inputs, resB):
    g = lambda k: np.asarray(inputs[k])
    mem = g("L0_attn_b_memory")
    memL = _c(mem.reshape(H, 6, 128, Dh).transpose(2, 0, 1, 3).reshape(128, H * 6 * 128))
    nrm = g("L0_attn_b_norm")
    nrmL = _c(nrm.reshape(4, 4, 6, 128).transpose(3, 0, 2, 1).reshape(128, 6 * H))
    dmp = np.stack([np.asarray(resB[c]["o_dmem"]) for c in range(NCN)])
    dnp = np.stack([np.asarray(resB[c]["o_dnorm"]) for c in range(NCN)])
    base = {
        "dmp": dmp, "dnp": dnp, "memL": memL, "nrmL": nrmL,
        "WqT": _c(g("L0_attn_b_Wq").T), "bq_r": _c(g("L0_attn_b_bq").reshape(1, -1)),
        "WgT": _c(g("L0_attn_b_Wg").T), "bg_r": _c(g("L0_attn_b_bg").reshape(1, -1)),
        "gate_wT": _c(g("L0_ffn_gate_w").T),
        "gateT": _c(np.swapaxes(g("L0_ffn_gate_proj"), 1, 2)),
        "upT": _c(np.swapaxes(g("L0_ffn_up_proj"), 1, 2)),
        "downT": _c(np.swapaxes(g("L0_ffn_down_proj"), 1, 2)),
        "ident_b": np.eye(128, dtype=BF), "ones_row": np.ones((1, 128), BF),
    }
    maps = []
    for c in range(NCN):
        m = dict(base)
        m["x2in"] = np.asarray(resB[c]["o_x2"])
        m["h3Tin"] = np.asarray(resB[c]["o_h3T"])
        maps.append(m)
    return maps


def run_C(inputs, resB, debug=False):
    nc = build_launch_C(debug=debug)
    maps = host_prep_C(inputs, resB)
    res = run_bass_kernel_spmd(nc, maps, core_ids=list(range(NCN)))
    return res.results



def build_fused(debug=False):
    nc = bass.Bass("TRN2", target_bir_lowering=False, debug=False)
    I = lambda n, s, d=F32: nc.dram_tensor(n, s, d, kind="ExternalInput").ap()
    O = lambda n, s, d=F32: nc.dram_tensor(n, s, d, kind="ExternalOutput").ap()
    D = lambda n, s, d=F32: nc.dram_tensor(n, s, d).ap()

    xin = I("xin", [TL, DM])
    wq = I("wq", [DM, DM])
    wk = I("wk", [DM, KVH * Dh])
    wv = I("wv", [DM, KVH * Dh])
    impw1 = I("impw1", [DM, 64])
    impb1 = I("impb1", [64, 1])
    impw2 = I("impw2", [64, 1])
    impb2 = I("impb2", [1, 1])
    qnw = I("qnw", [1, Dh])
    knw = I("knw", [1, Dh])
    qnw_roll = I("qnw_roll", [1, Dh])
    knw_roll = I("knw_roll", [1, Dh])
    sgnv = I("sgnv", [1, Dh])
    cos_l = I("cos_l", [TL, Dh])
    sin_l = I("sin_l", [TL, Dh])
    kbeT = I("kbeT", [256, SK])
    kab = I("kab", [4, 128, SK])
    wo = I("wo", [DM, DM])
    gate_wT = I("gate_wT", [DM, NE])
    gateT = I("gateT", [NE, DM, FF])
    upT = I("upT", [NE, DM, FF])
    downT = I("downT", [NE, FF, DM])
    WkT = I("WkT", [DM, KVH * Dh])
    WvT = I("WvT", [DM, KVH * Dh])
    WmbT = I("WmbT", [DM, DM])
    bk_r = I("bk_r", [1, KVH * Dh])
    bv_r = I("bv_r", [1, KVH * Dh])
    bmb_r = I("bmb_r", [1, DM])
    memory = I("memory", [H, FD, Dh])
    normT = I("normT", [FD, H])
    kab_full = I("kab_full", [32, 128, SK])
    ka_eL = I("ka_eL", [SK, 4])
    kb_e = I("kb_e", [SK, 256])
    qnw_col = I("qnw_col", [Dh, 1])
    knw_t4 = I("knw_t4", [1, KVH * Dh])
    ss_c = I("ss_c", [1, 1])
    memL = I("memL", [128, H * 6 * 128])
    nrmL = I("nrmL", [128, 6 * H])
    WqT = I("WqT", [DM, DM])
    WgT = I("WgT", [DM, DM])
    bq_r = I("bq_r", [1, DM])
    bg_r = I("bg_r", [1, DM])
    ident_b = I("ident_b", [128, 128], BF16)
    ones_row = I("ones_row", [1, 128], BF16)
    ones_col = I("ones_col", [128, 1], BF16)
    one16 = I("one16", [1, 1], BF16)
    iota_row = I("iota_row", [128, 128])
    iota_r1 = I("iota_r1", [1, 512])
    ltri = I("ltri", [128, 128], BF16)
    utri64 = I("utri64", [64, 64], BF16)
    sel8 = I("sel8", [64, NSB], BF16)
    rep8 = I("rep8", [64, GT], BF16)

    o_out = O("o_out", [TL, DM])

    b_lg = D("b_lg", [1, TL])
    g_lg = D("g_lg", [NCN, TL])
    b_kh = D("b_kh", [TL, KVH * Dh], BF16)
    g_kh = D("g_kh", [S, KVH * Dh], BF16)
    b_kr = D("b_kr", [TL, KVH * Dh], BF16)
    g_kr = D("g_kr", [S, KVH * Dh], BF16)
    b_v = D("b_v", [TL, KVH * Dh], BF16)
    g_v = D("g_v", [S, KVH * Dh], BF16)
    b_qsp = D("b_qsp", [SK, DM])
    g_qsp = D("g_qsp", [SK, DM])
    qrT_dram = D("qrT_f", [DM, TL], BF16)
    x2_dram = D("x2_dram", [TL, DM])
    h3T_dram = D("h3T_dram_f", [DM, TL], BF16)
    b_dm = D("b_dm", [128, H * 6 * 128])
    r_dm = D("r_dm", [128, H * 6 * 128])
    b_dn = D("b_dn", [128, 6 * H])
    r_dn = D("r_dn", [128, 6 * H])

    p = Prog(nc)
    with ExitStack() as st:
        big6 = st.enter_context(nc.psum_tensor("big6", [128, 3584], F32))
        psT = st.enter_context(nc.psum_tensor("psT", [128, 1024], BF16))
        af_t = st.enter_context(nc.sbuf_tensor("af", [128, 26000], F32))
        ab_t = st.enter_context(nc.sbuf_tensor("ab", [128, 49500], BF16))
        AF_ = Arena(p, af_t, 26000)
        AB_ = Arena(p, ab_t, 49500)
        C_ = make_shared_consts(p, AF_, AB_, ident_b=ident_b, ones_row=ones_row,
                                ones_col=ones_col, one16=one16, ltri=ltri,
                                utri64=utri64, sel8=sel8, rep8=rep8,
                                iota_row=iota_row, iota_r1=iota_r1)
        emit_phase_A(p, AF_, AB_, big6, psT, C_["cst"], C_["ident_bs"],
                     C_["onesr_sb"], C_["one16_sb"], C_["stat"], C_["dump"],
                     C_["lg_col"],
                     xin, wq, wk, wv, impw1, impb1, impw2, impb2,
                     qnw, knw, qnw_roll, knw_roll, sgnv, cos_l, sin_l, kbeT, kab,
                     b_lg, b_kh, b_kr, b_v, b_qsp, qrT_dram)
        p.cc("AllGather", ALU.bypass, b_lg, g_lg, reads=["blg"], writes=["glg"], tag="cclg")
        p.cc("AllGather", ALU.bypass, b_kr, g_kr, reads=["bkr"], writes=["gkr"], tag="cckr")
        p.cc("AllGather", ALU.bypass, b_kh, g_kh, reads=["bkh"], writes=["gkh"], tag="cckh")
        p.cc("AllGather", ALU.bypass, b_v, g_v, reads=["bv"], writes=["gv"], tag="ccv")
        p.cc("AllReduce", ALU.add, b_qsp, g_qsp, reads=["bqsp"], writes=["gqsp"], tag="ccqsp")

        def lgT_load(dst):
            p.dma("sp", dst.rearrange("p (r a) -> p r a", r=NCN),
                  g_lg.rearrange("r (a p) -> p r a", p=128),
                  reads=["glg"], writes=["lg"], tag="ldlg",
                  allow_slow_non_contiguous=True)

        def lgl_load(dst):
            p.dve(lambda e: e.tensor_copy(dst, C_["lg_col"]),
                  reads=["lgcol"], writes=["lgl"], tag="lglc")

        emit_phase_B(p, AF_, AB_, big6, psT, C_, debug,
                     xin, g_kr, g_kh, g_v, qrT_dram,
                     wo, gate_wT, gateT, upT, downT, WkT, WvT, WmbT, bk_r, bv_r, bmb_r,
                     memory, normT, kbeT, kab_full, ka_eL, kb_e, qnw_col, knw_t4, ss_c,
                     x2_dram, h3T_dram, b_dm, b_dn,
                     lgT_load, lgl_load, qspT=None, g_qsp=g_qsp, O=O)
        p.cc("AllReduce", ALU.add, b_dm, r_dm, reads=["bdm"], writes=["rdm"], tag="ccdm")
        p.cc("AllReduce", ALU.add, b_dn, r_dn, reads=["bdn"], writes=["rdn"], tag="ccdn")
        emit_phase_C(p, AF_, AB_, big6, psT, C_, debug,
                     x2_dram, h3T_dram, memL, nrmL, WqT, WgT, bq_r, bg_r,
                     gate_wT, gateT, upT, downT, o_out,
                     dmp=None, dnp=None, r_dm=r_dm, r_dn=r_dn, O=O)
        p.emit()
    return nc


def host_prep_fused(inputs):
    mapsA = host_prep_A(inputs)
    g = lambda k: np.asarray(inputs[k])
    mem = g("L0_attn_b_memory")
    memL = _c(mem.reshape(H, 6, 128, Dh).transpose(2, 0, 1, 3).reshape(128, H * 6 * 128))
    nrm = g("L0_attn_b_norm")
    nrmL = _c(nrm.reshape(4, 4, 6, 128).transpose(3, 0, 2, 1).reshape(128, 6 * H))
    ka = g("L0_attn_a_kron_a")
    kb = g("L0_attn_a_kron_b")
    ika = np.arange(SK) % 20
    ikb = np.arange(SK) // 20
    ka_expand = _c(ka.T[:, ika])
    kab_full = _c(np.broadcast_to(ka_expand[:32, None, :], (32, 128, SK)).astype(np.float32))
    kb_e = _c(kb[ikb])
    qnw = g("L0_attn_a_q_norm_w")
    knw = g("L0_attn_a_k_norm_w")
    k_i = np.arange(128)
    ltri = _c((k_i[:, None] <= k_i[None, :]).astype(BF))
    k64 = np.arange(64)
    utri64 = _c((k64[:, None] < k64[None, :]).astype(BF))
    tile_super = np.zeros(GT, np.int64)
    for sb, (st0, ntl, cap) in enumerate(SUP):
        tile_super[st0:st0 + ntl] = sb
    sel8 = np.zeros((64, NSB), BF)
    sel8[np.arange(GT), tile_super] = 1
    rep8 = np.zeros((64, GT), BF)
    first_tile = np.array([SUP[s][0] for s in tile_super])
    rep8[first_tile, np.arange(GT)] = 1
    shared = {
        "wo": g("L0_attn_a_wo"),
        "gate_wT": _c(g("L0_ffn_gate_w").T),
        "gateT": _c(np.swapaxes(g("L0_ffn_gate_proj"), 1, 2)),
        "upT": _c(np.swapaxes(g("L0_ffn_up_proj"), 1, 2)),
        "downT": _c(np.swapaxes(g("L0_ffn_down_proj"), 1, 2)),
        "WkT": _c(g("L0_attn_b_Wk").T), "WvT": _c(g("L0_attn_b_Wv").T),
        "WmbT": _c(g("L0_attn_b_Wmb").T),
        "bk_r": _c(g("L0_attn_b_bk").reshape(1, -1)),
        "bv_r": _c(g("L0_attn_b_bv").reshape(1, -1)),
        "bmb_r": _c(g("L0_attn_b_bmb").reshape(1, -1)),
        "memory": g("L0_attn_b_memory"), "normT": _c(g("L0_attn_b_norm").T),
        "kab_full": kab_full, "kb_e": kb_e,
        "qnw_col": _c(qnw.reshape(Dh, 1)),
        "knw_t4": _c(np.tile(knw, KVH).reshape(1, -1)),
        "ss_c": _c(np.asarray(inputs["L0_attn_a_sketch_scale"]).reshape(1, 1)),
        "memL": memL, "nrmL": nrmL,
        "WqT": _c(g("L0_attn_b_Wq").T), "bq_r": _c(g("L0_attn_b_bq").reshape(1, -1)),
        "WgT": _c(g("L0_attn_b_Wg").T), "bg_r": _c(g("L0_attn_b_bg").reshape(1, -1)),
        "ones_col": np.ones((128, 1), BF), "one16": np.ones((1, 1), BF),
        "iota_row": _c(np.broadcast_to(np.arange(128, dtype=np.float32)[None, :],
                                       (128, 128))),
        "iota_r1": _c(np.arange(512, dtype=np.float32).reshape(1, 512)),
        "ltri": ltri, "utri64": utri64, "sel8": sel8, "rep8": rep8,
    }
    maps = []
    for c in range(NCN):
        m = dict(mapsA[c])
        m.update(shared)
        m["ka_eL"] = _c(ka_expand[:32].T[:, 4 * c:4 * c + 4])
        maps.append(m)
    return maps


def run_fused(inputs, debug=False):
    nc = build_fused(debug=debug)
    maps = host_prep_fused(inputs)
    res = run_bass_kernel_spmd(nc, maps, core_ids=list(range(NCN)))
    return res.results


def kernel(**inputs):
    res = run_fused(inputs, debug=False)
    out = np.concatenate([np.asarray(res[c]["o_out"], np.float32) for c in range(NCN)], 0)
    return out.reshape(1, S, DM)


def kernel_3launch(**inputs):
    resA = run_A(inputs)
    resB = run_B(inputs, resA, debug=False)
    resC = run_C(inputs, resB, debug=False)
    out = np.concatenate([np.asarray(resC[c]["o_out"], np.float32) for c in range(NCN)], 0)
    return out.reshape(1, S, DM)


# version 10
# speedup vs baseline: 1.1380x; 1.1380x over previous
import math
import numpy as np
import ml_dtypes
from contextlib import ExitStack

import concourse.bass as bass
import concourse.mybir as mybir
from concourse.bass_utils import run_bass_kernel_spmd

F32 = mybir.dt.float32
BF16 = mybir.dt.bfloat16
AF = mybir.ActivationFunctionType
ALU = mybir.AluOpType
AX = mybir.AxisListType.X

NCN = 8
S, DM = 8192, 2048
TL = S // NCN
NT = TL // 128
H, KVH, Dh = 16, 4, 128
SK, TK = 640, 2048
FF, NE = 1408, 8
FD = 768
LOG_SHIFT = float(np.log(np.float32(S) / np.float32(SK)))
BF = ml_dtypes.bfloat16


ENGS = ("pe", "act", "dve", "pool", "sp")
NDQ = 8


class _Op:
    __slots__ = ("eng", "fn", "dma", "cc", "need_inc", "cum", "waits", "tag", "semkey")

    def __init__(self, eng, fn, dma, tag):
        self.eng = eng
        self.fn = fn
        self.dma = dma
        self.cc = False
        self.need_inc = dma
        self.cum = None
        self.waits = []
        self.tag = tag
        self.semkey = None


class Prog:

    def __init__(self, nc):
        self.nc = nc
        self.ops = []
        self.wr = {}
        self.rd = {}
        self.dma_count = {}
        self.dma_prev = {}
        self.pending_alias = {}

    def alias(self, new_name, old_names):
        deps = []
        for b in old_names:
            w = self.wr.get(b)
            if w is not None:
                deps.append(w)
            deps.extend(self.rd.get(b, ()))
        self.pending_alias.setdefault(new_name, []).extend(deps)

    def op(self, eng, fn, reads=(), writes=(), dma=False, tag=""):
        i = len(self.ops)
        o = _Op(eng, fn, dma, tag)
        deps = set()
        for b in writes:
            for d in self.pending_alias.pop(b, ()):
                deps.add(d)
        if dma:
            j = self.dma_count.get(eng, 0)
            self.dma_count[eng] = j + 1
            o.semkey = (eng, j % NDQ)
            o.cum = 16 * (j // NDQ + 1)
            if j >= NDQ:
                deps.add(self.dma_prev[(eng, j % NDQ)])
            self.dma_prev[(eng, j % NDQ)] = i
        else:
            o.semkey = eng
        for b in reads:
            w = self.wr.get(b)
            if w is not None:
                deps.add(w)
        for b in writes:
            for r in self.rd.get(b, ()):
                deps.add(r)
            w = self.wr.get(b)
            if w is not None:
                deps.add(w)
        for d in deps:
            po = self.ops[d]
            if po.eng == eng == "pe" and not po.dma and not dma:
                continue
            po.need_inc = True
            o.waits.append(d)
        self.ops.append(o)
        for b in reads:
            self.rd.setdefault(b, []).append(i)
        for b in writes:
            self.wr[b] = i
            self.rd[b] = []
        return i

    def pe(self, fn, reads=(), writes=(), tag=""):
        return self.op("pe", fn, reads, writes, tag=tag)

    def act(self, fn, reads=(), writes=(), tag=""):
        return self.op("act", fn, reads, writes, tag=tag)

    def dve(self, fn, reads=(), writes=(), tag=""):
        return self.op("dve", fn, reads, writes, tag=tag)

    def pool(self, fn, reads=(), writes=(), tag=""):
        return self.op("pool", fn, reads, writes, tag=tag)

    def dma(self, eng, out, in_, reads=(), writes=(), tag="", **kw):
        def fn(e):
            return e.dma_start(out=out, in_=in_, **kw)
        return self.op(eng, fn, reads, writes, dma=True, tag=tag)

    def cc(self, kind, op, in_ap, out_ap, reads=(), writes=(), tag="cc"):
        def fn(e):
            return e.collective_compute(kind, op, replica_groups=[list(range(NCN))],
                                        ins=[in_ap.opt()], outs=[out_ap.opt()])
        i = self.op("pool", fn, reads, writes, dma=False, tag=tag)
        o = self.ops[i]
        o.cc = True
        o.semkey = ("cc", i)
        o.cum = 1
        o.need_inc = True
        return i

    def mm(self, out, lhsT, rhs, start, stop, reads, writes, tag="mm", pm=None):
        return self.pe(lambda e: e.matmul(out, lhsT, rhs, start=start, stop=stop,
                                          skip_group_check=True, perf_mode=pm),
                       reads=reads, writes=writes, tag=tag)

    def tr(self, out, in_, ident, reads, writes, tag="tr"):
        return self.pe(lambda e: e.transpose(out, in_, ident),
                       reads=reads, writes=writes, tag=tag)

    def emit(self):
        cums = {e: 0 for e in ENGS}
        semkeys = set()
        for o in self.ops:
            if o.dma or o.cc:
                semkeys.add(o.semkey)
            elif o.need_inc:
                cums[o.eng] += 1
                o.cum = cums[o.eng]
                semkeys.add(o.semkey)
        nc = self.nc
        with ExitStack() as st:
            sems = {}
            for k in sorted(semkeys, key=str):
                name = k if isinstance(k, str) else f"dq_{k[0]}_{k[1]}"
                sems[k] = st.enter_context(nc.semaphore(f"sem_{name}"))
            blk = st.enter_context(nc.Block())
            prog = self
            dma_tot = {}
            for o in prog.ops:
                if o.dma:
                    dma_tot[o.semkey] = max(dma_tot.get(o.semkey, 0), o.cum)

            def emit_engine(bass_eng, eng_name):
                watermark = {}
                for o in prog.ops:
                    if o.eng != eng_name:
                        continue
                    need = {}
                    for d in o.waits:
                        po = prog.ops[d]
                        assert po.cum is not None, (po.tag, po.eng)
                        need[po.semkey] = max(need.get(po.semkey, 0), po.cum)
                    for k in sorted(need, key=str):
                        v = need[k]
                        if v > watermark.get(k, 0):
                            bass_eng.wait_ge(sems[k], v)
                            watermark[k] = v
                    r = o.fn(bass_eng)
                    if o.dma:
                        r.then_inc(sems[o.semkey], 16)
                    elif o.cc:
                        r.then_inc(sems[o.semkey], 1)
                    elif o.need_inc:
                        assert r is not None, f"op {o.tag} returned None but needs inc"
                        r.then_inc(sems[o.semkey], 1)
                for k, v in sorted(dma_tot.items(), key=lambda kv: str(kv[0])):
                    if isinstance(k, tuple) and k[0] == eng_name and v > watermark.get(k, 0):
                        bass_eng.wait_ge(sems[k], v)
                if eng_name == "pool":
                    for o in prog.ops:
                        if o.cc and watermark.get(o.semkey, 0) < 1:
                            bass_eng.wait_ge(sems[o.semkey], 1)

            blk.tensor(lambda e: emit_engine(e, "pe"))
            blk.scalar(lambda e: emit_engine(e, "act"))
            blk.vector(lambda e: emit_engine(e, "dve"))
            blk.gpsimd(lambda e: emit_engine(e, "pool"))
            blk.sync(lambda e: emit_engine(e, "sp"))


PM2 = mybir.MatmulPerfMode.DoubleRow


def pair2(ap2d, off, stride, w):
    a = ap2d.ap
    assert len(a) == 2, a
    return bass.AP(ap2d.tensor, ap2d.offset + off, [list(a[0]), [stride, 2], [1, w]])


def rep_mid(ap2d, n):
    a = ap2d.ap
    assert len(a) == 2, a
    return bass.AP(ap2d.tensor, ap2d.offset, [list(a[0]), [0, n], list(a[1])])



class Arena:

    def __init__(self, prog, tensor, total_cols):
        self.p, self.t, self.total = prog, tensor, total_cols
        self.off = 0
        self.events = []

    def mark(self):
        return self.off

    def pop(self, mark):
        self.off = mark

    def alloc(self, name, cols, aka=()):
        names = (name,) + tuple(aka)
        start, end = self.off, self.off + cols
        assert end <= self.total, (name, end, self.total)
        olds = sorted({n for (s, e, ns) in self.events for n in ns
                       if s < end and e > start and n not in names})
        if olds:
            for nm in names:
                self.p.alias(nm, olds)
        self.events.append((start, end, names))
        self.off = end
        return self.t[:, start:end]



def emit_phase_A(p, AF_, AB_, big6, psT, cst, ident_bs, ones_sb, one16_sb,
                 stat, dump, lg_col,
                 xin, wq, wk, wv, impw1, impb1, impw2, impb2,
                 qnw, knw, qnw_roll, knw_roll, sgnv, cos_l, sin_l, kbeT, kab,
                 o_logits, o_kh, o_kr, o_v, o_qsp, o_qrT):
    pb = [big6[:, i * 512:(i + 1) * 512] for i in range(6)]
    mA_f = AF_.mark()
    mA_b = AB_.mark()

    x_t = [AF_.alloc(f"xa{i}", DM) for i in range(2)]
    coss = AF_.alloc("cs", NT * Dh)
    sins = AF_.alloc("sn", NT * Dh)
    c2q = AF_.alloc("c2q", NT * Dh)
    s2q = AF_.alloc("s2q", NT * Dh)
    c2k = AF_.alloc("c2k", NT * Dh)
    s2k = AF_.alloc("s2k", NT * Dh)
    nrm_rows = AF_.alloc("nrm", 5 * Dh)[0:1, :]
    nb = AF_.alloc("nb", 2 * Dh)
    nbr = AF_.alloc("nbr", 2 * Dh)
    kbeT_sb = AF_.alloc("kbeT", 2 * SK)
    kab_sb = AF_.alloc("kab", 4 * SK)
    lg_row = AF_.alloc("lgrow", TL)
    w_col = AF_.alloc("wcola", NT)
    qh = AF_.alloc("qh", 512)
    rr = AF_.alloc("rr", 16)
    tmp_a = AF_.alloc("tmpa", 512)
    tmp_b = AF_.alloc("tmpb", 512)
    kh_f = AF_.alloc("khf", KVH * Dh)
    qsp_sb = AF_.alloc("qsp", 512)
    b2s = AF_.alloc("b2", 1)[0:1, :]
    impb1_sb = AF_.alloc("impb1", 1)[0:64, :]

    hT = AB_.alloc("hT", 16 * TL)
    nrm16 = AB_.alloc("nrm16", 4 * Dh)[0:1, :]
    h16 = AB_.alloc("h16", DM)
    impw1_sb = AB_.alloc("impw1", 16 * 64)
    impw2_sb = AB_.alloc("impw2", 1)[0:64, :]
    t1T = AB_.alloc("t1T", TL)[0:64, :]
    lg16 = AB_.alloc("lg16", TL)[0:1, :]
    PT_t = AB_.alloc("PTa", SK)
    ysk = AB_.alloc("ysk", 512)
    qr16 = AB_.alloc("qr16", 512)
    qrT16 = AB_.alloc("qrT16", 512)
    kv16 = AB_.alloc("kv16", 2 * KVH * Dh)
    v16 = AB_.alloc("v16", KVH * Dh)

    p.dma("sp", kbeT_sb.rearrange("p (a s) -> p a s", a=2),
          kbeT.rearrange("(a p) s -> p a s", p=128), writes=["kbeT"], tag="ldkbe")
    p.dma("sp", kab_sb.rearrange("p (a s) -> p a s", a=4),
          kab.rearrange("a p s -> p a s"), writes=["kab"], tag="ldkab")
    for i, srcv in enumerate([qnw, knw, qnw_roll, knw_roll, sgnv]):
        p.dma("sp", nrm_rows[:, i * Dh:(i + 1) * Dh], srcv, writes=["nrm"], tag=f"ldnr{i}")
    p.dma("sp", coss.rearrange("p (t d) -> p t d", t=NT),
          cos_l.rearrange("(t p) d -> p t d", p=128), writes=["cs"], tag="ldcos")
    p.dma("sp", sins.rearrange("p (t d) -> p t d", t=NT),
          sin_l.rearrange("(t p) d -> p t d", p=128), writes=["sn"], tag="ldsin")
    p.dma("sp", impb1_sb, impb1, writes=["impb1"], tag="ldib1")
    p.dma("sp", b2s, impb2, writes=["b2"], tag="ldib2")
    p.dma("pool", impw2_sb, impw2, writes=["impw2"], tag="ldiw2")
    p.dma("pool", impw1_sb.rearrange("p (a o) -> p a o", a=16),
          impw1.rearrange("(a p) o -> p a o", p=128), writes=["impw1"], tag="ldiw1")

    p.dve(lambda e: e.tensor_mul(nrm_rows[:, 2 * Dh:3 * Dh], nrm_rows[:, 2 * Dh:3 * Dh],
                                 nrm_rows[:, 4 * Dh:5 * Dh]), reads=["nrm"], writes=["nrm"], tag="sgnq")
    p.dve(lambda e: e.tensor_mul(nrm_rows[:, 3 * Dh:4 * Dh], nrm_rows[:, 3 * Dh:4 * Dh],
                                 nrm_rows[:, 4 * Dh:5 * Dh]), reads=["nrm"], writes=["nrm"], tag="sgnk")
    p.dve(lambda e: e.tensor_copy(nrm16, nrm_rows[0:1, 0:4 * Dh]),
          reads=["nrm"], writes=["nrm16"], tag="nrmcast")
    p.mm(pb[0][:, 0:2 * Dh], ones_sb, nrm16[:, 0:2 * Dh], True, True,
         reads=["onesr", "nrm16"], writes=["pb0"], tag="bcast0")
    p.dve(lambda e: e.tensor_copy(nb, pb[0][:, 0:2 * Dh]),
          reads=["pb0"], writes=["nb"], tag="nbcopy")
    p.mm(pb[0][:, 0:2 * Dh], ones_sb, nrm16[:, 2 * Dh:4 * Dh], True, True,
         reads=["onesr", "nrm16"], writes=["pb0"], tag="bcast1")
    p.dve(lambda e: e.tensor_copy(nbr, pb[0][:, 0:2 * Dh]),
          reads=["pb0"], writes=["nbr"], tag="nbrcopy")

    for tt in range(NT):
        cs_t = coss[:, tt * Dh:(tt + 1) * Dh]
        sn_t = sins[:, tt * Dh:(tt + 1) * Dh]
        p.dve(lambda e, o=c2q[:, tt * Dh:(tt + 1) * Dh], a=cs_t:
              e.tensor_mul(o, a, nb[:, 0:Dh]), reads=["cs", "nb"], writes=["c2q"], tag="c2q")
        p.dve(lambda e, o=c2k[:, tt * Dh:(tt + 1) * Dh], a=cs_t:
              e.tensor_mul(o, a, nb[:, Dh:2 * Dh]), reads=["cs", "nb"], writes=["c2k"], tag="c2k")
        p.dve(lambda e, o=s2q[:, tt * Dh:(tt + 1) * Dh], a=sn_t:
              e.tensor_mul(o, a, nbr[:, 0:Dh]), reads=["sn", "nbr"], writes=["s2q"], tag="s2q")
        p.dve(lambda e, o=s2k[:, tt * Dh:(tt + 1) * Dh], a=sn_t:
              e.tensor_mul(o, a, nbr[:, Dh:2 * Dh]), reads=["sn", "nbr"], writes=["s2k"], tag="s2k")

    s1 = stat[:, 0:1]; s2 = stat[:, 1:2]; m_t = stat[:, 2:3]; r_t = stat[:, 3:4]
    for tt in range(NT):
        xb = x_t[tt % 2]
        bx = f"xa{tt % 2}"
        p.dma("sp", xb, xin[tt * 128:(tt + 1) * 128, :], writes=[bx], tag=f"ldx{tt}")
        p.act(lambda e, x=xb: e.activation(dump, x, AF.Square, accum_out=s2),
              reads=[bx], writes=["dump", "stat"], tag=f"lnsq{tt}")
        p.dve(lambda e, x=xb: e.reduce_sum(s1, x, axis=AX),
              reads=[bx], writes=["stat"], tag=f"lns1{tt}")
        p.dve(lambda e: e.tensor_scalar_mul(m_t, s1, 1.0 / DM), reads=["stat"], writes=["stat"], tag="lnm")
        p.dve(lambda e: e.tensor_scalar_mul(s2, s2, 1.0 / DM), reads=["stat"], writes=["stat"], tag="lnv1")
        p.dve(lambda e: e.tensor_tensor(s1, m_t, m_t, ALU.mult), reads=["stat"], writes=["stat"], tag="lnv2")
        p.dve(lambda e: e.tensor_sub(s2, s2, s1), reads=["stat"], writes=["stat"], tag="lnv3")
        p.act(lambda e: e.activation(s2, s2, AF.Sqrt, bias=cst[:, 1:2]), reads=["stat", "cst"], writes=["stat"], tag="lnsd")
        p.dve(lambda e: e.reciprocal(r_t, s2), reads=["stat"], writes=["stat"], tag="lnr")
        p.dve(lambda e, x=xb: e.tensor_scalar(x, x, m_t, r_t, ALU.subtract, ALU.mult),
              reads=[bx, "stat"], writes=[bx], tag=f"lnn{tt}")
        p.dve(lambda e, x=xb: e.tensor_copy(h16, x),
              reads=[bx], writes=["h16"], tag=f"hcast{tt}")
        for g in range(2):
            for d8 in range(8):
                dt_i = g * 8 + d8
                p.tr(psT[:, d8 * 128:(d8 + 1) * 128],
                     h16[:, dt_i * 128:(dt_i + 1) * 128], ident_bs,
                     reads=["h16", "identb"], writes=["psT"], tag=f"trh{tt}")
            dst = hT.rearrange("p (a t) -> p a t", a=16)[:, g * 8:(g + 1) * 8,
                                                         tt * 128:(tt + 1) * 128]
            srcv = psT[:, 0:1024].rearrange("p (a t) -> p a t", a=8)
            p.dve(lambda e, o=dst, i_=srcv: e.tensor_copy(o, i_),
                  reads=["psT"], writes=["hT"], tag=f"evh{tt}")

    for half in range(2):
        hb = pb[half]
        for dt_i in range(16):
            p.mm(hb[0:64, :], impw1_sb[:, dt_i * 64:(dt_i + 1) * 64],
                 hT[:, dt_i * TL + half * 512: dt_i * TL + (half + 1) * 512],
                 start=(dt_i == 0), stop=(dt_i == 15),
                 reads=["impw1", "hT"], writes=[f"pb{half}"], tag="immm")
        p.act(lambda e, o=t1T[:, half * 512:(half + 1) * 512], i_=hb[0:64, :]:
              e.activation(o, i_, AF.Tanh, bias=impb1_sb),
              reads=[f"pb{half}", "impb1"], writes=["t1T"], tag="tanh")
    p.dve(lambda e: e.tensor_scalar_add(b2s, b2s, -LOG_SHIFT),
          reads=["b2"], writes=["b2"], tag="b2p")
    for half in range(2):
        p.mm(pb[2][0:1, :], impw2_sb, t1T[:, half * 512:(half + 1) * 512],
             True, True, reads=["impw2", "t1T"], writes=["pb2"], tag="lgmm")
        p.act(lambda e, o=lg_row[0:1, half * 512:(half + 1) * 512], i_=pb[2][0:1, :]:
              e.activation(o, i_, AF.Identity, bias=b2s),
              reads=["pb2", "b2"], writes=["lgrow"], tag="lgev")
    p.dma("sp", o_logits, lg_row[0:1, :], reads=["lgrow"], writes=["blg"], tag="stlg")
    p.dve(lambda e: e.tensor_copy(lg16, lg_row[0:1, :]), reads=["lgrow"], writes=["lg16"], tag="lgc")
    for tt in range(NT):
        p.mm(pb[3][0:128, 0:1], lg16[:, tt * 128:(tt + 1) * 128], one16_sb,
             True, True, reads=["lg16", "one16"], writes=["pb3"], tag="lgtr")
        p.dve(lambda e, o=lg_col[:, tt:tt + 1]: e.tensor_copy(o, pb[3][0:128, 0:1]),
              reads=["pb3"], writes=["lgcol"], tag="lgcv")
    p.act(lambda e: e.activation(w_col, lg_col, AF.Sigmoid),
          reads=["lgcol"], writes=["wcola"], tag="sigw")

    m_wq = AB_.mark()
    wq_ch = [AB_.alloc(f"wqa{i}", 16 * 512) for i in range(2)]
    for ch in range(4):
        wbuf = wq_ch[ch % 2]
        bw = f"wqa{ch % 2}"
        p.dma("pool", wbuf.rearrange("p (a o) -> p a o", a=16),
              wq.rearrange("(a p) o -> p a o", p=128)[:, :, ch * 512:(ch + 1) * 512],
              writes=[bw], tag=f"ldwq{ch}")
        for tt in range(NT):
            i_loc, jh = tt // 2, tt % 2
            p.dve(lambda e, a=kbeT_sb[:, jh * SK:(jh + 1) * SK],
                  b=kab_sb[:, i_loc * SK:(i_loc + 1) * SK]:
                  e.tensor_mul(PT_t, a, b), reads=["kbeT", "kab"], writes=["PTa"], tag=f"PTb{ch}_{tt}")
            for dt_i in range(16):
                p.mm(pb[5][:, :],
                     hT[:, dt_i * TL + tt * 128: dt_i * TL + (tt + 1) * 128],
                     wbuf[:, dt_i * 512:(dt_i + 1) * 512],
                     start=(dt_i == 0), stop=(dt_i == 15),
                     reads=["hT", bw], writes=["pb5"], tag=f"qmm{ch}_{tt}")
            for hh in range(4):
                p.act(lambda e, i_=pb[5][:, hh * 128:(hh + 1) * 128],
                      o=dump[:, hh * 128:(hh + 1) * 128], a=rr[:, hh:hh + 1]:
                      e.activation(o, i_, AF.Square, accum_out=a),
                      reads=["pb5"], writes=["dump", "rr"], tag=f"qsq{ch}_{tt}")
            p.dve(lambda e: e.tensor_scalar_mul(rr[:, 0:4], rr[:, 0:4], 1.0 / Dh),
                  reads=["rr"], writes=["rr"], tag="rrm")
            p.act(lambda e: e.activation(rr[:, 0:4], rr[:, 0:4], AF.Sqrt, bias=cst[:, 2:3]),
                  reads=["rr", "cst"], writes=["rr"], tag="rrs")
            p.dve(lambda e: e.reciprocal(rr[:, 0:4], rr[:, 0:4]),
                  reads=["rr"], writes=["rr"], tag="rrr")
            for hh in range(4):
                p.dve(lambda e, o=qh[:, hh * 128:(hh + 1) * 128],
                      i_=pb[5][:, hh * 128:(hh + 1) * 128], sc=rr[:, hh:hh + 1]:
                      e.tensor_scalar_mul(o, i_, sc),
                      reads=["pb5", "rr"], writes=["qh"], tag=f"qsc{ch}_{tt}")
            p.dve(lambda e, sc=w_col[:, tt:tt + 1]: e.tensor_scalar_mul(ysk, qh, sc),
                  reads=["qh", "wcola"], writes=["ysk"], tag=f"ysk{ch}_{tt}")
            for sp in range(5):
                p.mm(pb[sp][:, :], PT_t[:, sp * 128:(sp + 1) * 128],
                     ysk, start=(tt == 0), stop=(tt == NT - 1),
                     reads=["PTa", "ysk"], writes=[f"pb{sp}"], tag=f"skmm{ch}_{tt}")
            c2t = c2q[:, tt * Dh:(tt + 1) * Dh]
            s2t0 = s2q[:, tt * Dh: tt * Dh + 64]
            s2t1 = s2q[:, tt * Dh + 64: (tt + 1) * Dh]
            p.dve(lambda e, o=tmp_a.rearrange("p (h d) -> p h d", h=4),
                  a=qh.rearrange("p (h d) -> p h d", h=4), b=c2t:
                  e.tensor_tensor(o, a, rep_mid(b, 4), ALU.mult),
                  reads=["qh", "c2q"], writes=["tmpa"], tag=f"rpc{ch}_{tt}")
            p.dve(lambda e, o=tmp_b.rearrange("p (h d) -> p h d", h=4)[:, :, 0:64],
                  a=qh.rearrange("p (h d) -> p h d", h=4)[:, :, 64:128], b=s2t0:
                  e.tensor_tensor(o, a, rep_mid(b, 4), ALU.mult),
                  reads=["qh", "s2q"], writes=["tmpb"], tag=f"rps0{ch}_{tt}")
            p.dve(lambda e, o=tmp_b.rearrange("p (h d) -> p h d", h=4)[:, :, 64:128],
                  a=qh.rearrange("p (h d) -> p h d", h=4)[:, :, 0:64], b=s2t1:
                  e.tensor_tensor(o, a, rep_mid(b, 4), ALU.mult),
                  reads=["qh", "s2q"], writes=["tmpb"], tag=f"rps1{ch}_{tt}")
            p.dve(lambda e: e.tensor_add(qr16, tmp_a, tmp_b),
                  reads=["tmpa", "tmpb"], writes=["qr16"], tag=f"rpa{ch}_{tt}")
            for j in range(4):
                p.tr(psT[:, j * 128:(j + 1) * 128], qr16[:, j * 128:(j + 1) * 128],
                     ident_bs, reads=["qr16", "identb"], writes=["psT"], tag=f"qtr{ch}_{tt}_{j}")
            p.dve(lambda e: e.tensor_copy(qrT16, psT[:, 0:512]),
                  reads=["psT"], writes=["qrT16"], tag=f"qte{ch}_{tt}")
            p.dma("sp", o_qrT.rearrange("(a p) t -> p a t", p=128)[:, ch * 4:(ch + 1) * 4,
                                                                   tt * 128:(tt + 1) * 128],
                  qrT16.rearrange("p (a t) -> p a t", a=4),
                  reads=["qrT16"], writes=["qrT_dram"], tag=f"stqrT{ch}_{tt}")
        for sp in range(5):
            p.dve(lambda e, i_=pb[sp][:, :]: e.tensor_copy(qsp_sb, i_),
                  reads=[f"pb{sp}"], writes=["qsp"], tag=f"skev{ch}_{sp}")
            p.dma("sp", o_qsp[sp * 128:(sp + 1) * 128, ch * 512:(ch + 1) * 512], qsp_sb,
                  reads=["qsp"], writes=["bqsp"], tag=f"stqsp{ch}_{sp}")

    AB_.pop(m_wq)
    wkv_sb = AB_.alloc("wkv", 16 * 1024)
    p.dma("pool", wkv_sb.rearrange("p (a o) -> p a o", a=16)[:, :, 0:512],
          wk.rearrange("(a p) o -> p a o", p=128), writes=["wkv"], tag="ldwk")
    p.dma("pool", wkv_sb.rearrange("p (a o) -> p a o", a=16)[:, :, 512:1024],
          wv.rearrange("(a p) o -> p a o", p=128), writes=["wkv"], tag="ldwv")
    for tt in range(NT):
        for dt_i in range(16):
            p.mm(pb[0][:, :], hT[:, dt_i * TL + tt * 128: dt_i * TL + (tt + 1) * 128],
                 wkv_sb[:, dt_i * 1024: dt_i * 1024 + 512],
                 start=(dt_i == 0), stop=(dt_i == 15),
                 reads=["hT", "wkv"], writes=["pb0"], tag=f"kmm{tt}")
        for dt_i in range(16):
            p.mm(pb[1][:, :], hT[:, dt_i * TL + tt * 128: dt_i * TL + (tt + 1) * 128],
                 wkv_sb[:, dt_i * 1024 + 512: (dt_i + 1) * 1024],
                 start=(dt_i == 0), stop=(dt_i == 15),
                 reads=["hT", "wkv"], writes=["pb1"], tag=f"vmm{tt}")
        p.dve(lambda e: e.tensor_copy(v16, pb[1][:, :]),
              reads=["pb1"], writes=["v16"], tag=f"vc{tt}")
        p.dma("sp", o_v[tt * 128:(tt + 1) * 128, :], v16, reads=["v16"], writes=["bv"], tag=f"stv{tt}")
        for hh in range(KVH):
            p.act(lambda e, i_=pb[0][:, hh * 128:(hh + 1) * 128],
                  o=dump[:, hh * 128:(hh + 1) * 128], a=rr[:, hh:hh + 1]:
                  e.activation(o, i_, AF.Square, accum_out=a),
                  reads=["pb0"], writes=["dump", "rr"], tag=f"ksq{tt}")
        p.dve(lambda e: e.tensor_scalar_mul(rr[:, 0:KVH], rr[:, 0:KVH], 1.0 / Dh),
              reads=["rr"], writes=["rr"], tag="krm")
        p.act(lambda e: e.activation(rr[:, 0:KVH], rr[:, 0:KVH], AF.Sqrt, bias=cst[:, 2:3]),
              reads=["rr", "cst"], writes=["rr"], tag="krs")
        p.dve(lambda e: e.reciprocal(rr[:, 0:KVH], rr[:, 0:KVH]),
              reads=["rr"], writes=["rr"], tag="krr")
        for hh in range(KVH):
            p.dve(lambda e, o=kh_f[:, hh * 128:(hh + 1) * 128],
                  i_=pb[0][:, hh * 128:(hh + 1) * 128], sc=rr[:, hh:hh + 1]:
                  e.tensor_scalar_mul(o, i_, sc),
                  reads=["pb0", "rr"], writes=["khf"], tag=f"ksc{tt}")
        p.dve(lambda e: e.tensor_copy(kv16[:, 0:512], kh_f),
              reads=["khf"], writes=["kv16"], tag=f"khc{tt}")
        p.dma("sp", o_kh[tt * 128:(tt + 1) * 128, :], kv16[:, 0:512],
              reads=["kv16"], writes=["bkh"], tag=f"stkh{tt}")
        c2t = c2k[:, tt * Dh:(tt + 1) * Dh]
        s2t0 = s2k[:, tt * Dh: tt * Dh + 64]
        s2t1 = s2k[:, tt * Dh + 64: (tt + 1) * Dh]
        p.dve(lambda e, o=tmp_a.rearrange("p (h d) -> p h d", h=KVH),
              a=kh_f.rearrange("p (h d) -> p h d", h=KVH), b=c2t:
              e.tensor_tensor(o, a, rep_mid(b, KVH), ALU.mult),
              reads=["khf", "c2k"], writes=["tmpa"], tag=f"krc{tt}")
        p.dve(lambda e, o=tmp_b.rearrange("p (h d) -> p h d", h=KVH)[:, :, 0:64],
              a=kh_f.rearrange("p (h d) -> p h d", h=KVH)[:, :, 64:128], b=s2t0:
              e.tensor_tensor(o, a, rep_mid(b, KVH), ALU.mult),
              reads=["khf", "s2k"], writes=["tmpb"], tag=f"krs0{tt}")
        p.dve(lambda e, o=tmp_b.rearrange("p (h d) -> p h d", h=KVH)[:, :, 64:128],
              a=kh_f.rearrange("p (h d) -> p h d", h=KVH)[:, :, 0:64], b=s2t1:
              e.tensor_tensor(o, a, rep_mid(b, KVH), ALU.mult),
              reads=["khf", "s2k"], writes=["tmpb"], tag=f"krs1{tt}")
        p.dve(lambda e: e.tensor_add(kv16[:, 512:1024], tmp_a, tmp_b),
              reads=["tmpa", "tmpb"], writes=["kv16"], tag=f"kra{tt}")
        p.dma("sp", o_kr[tt * 128:(tt + 1) * 128, :], kv16[:, 512:1024],
              reads=["kv16"], writes=["bkr"], tag=f"stkr{tt}")

    AF_.pop(mA_f)
    AB_.pop(mA_b)
    p.alias("big6", [f"pb{i}" for i in range(6)])


def make_shared_consts(p, AF_, AB_, ident_b=None, ones_row=None, ones_col=None,
                       one16=None, ltri=None, utri64=None, sel8=None, rep8=None,
                       iota_row=None, iota_r1=None):
    C = {}
    cst = AF_.alloc("cst", 8)
    p.dve(lambda e: e.memset(cst[:, 0:1], 0.0), writes=["cst"], tag="c0")
    p.dve(lambda e: e.memset(cst[:, 1:2], 1e-5), writes=["cst"], tag="c1")
    p.dve(lambda e: e.memset(cst[:, 2:3], 1e-6), writes=["cst"], tag="c2")
    C["cst"] = cst
    C["stat"] = AF_.alloc("stat", 8)
    C["dump"] = AF_.alloc("dump", DM)
    C["tw_all"] = AF_.alloc("tw", NT * NE)
    C["rl_f"] = AF_.alloc("rl", 16)
    C["lg_col"] = AF_.alloc("lgcol", NT)
    C["ident_bs"] = AB_.alloc("identb", 128)
    p.dma("sp", C["ident_bs"], ident_b, writes=["identb"], tag="ldid")
    onesr = AB_.alloc("onesr", 128)
    p.dma("sp", onesr[0:1, :], ones_row, writes=["onesr"], tag="ldor")
    C["onesr_sb"] = onesr[0:1, :]
    if ones_col is not None:
        oc = AB_.alloc("ones", 1)
        p.dma("sp", oc, ones_col, writes=["ones"], tag="ldoc")
        C["ones_sb"] = oc
    o16 = AB_.alloc("one16", 1)
    if one16 is not None:
        p.dma("sp", o16[0:1, :], one16, writes=["one16"], tag="ld116")
    else:
        p.dve(lambda e: e.memset(o16[0:1, :], 1.0), writes=["one16"], tag="one16m")
    C["one16_sb"] = o16[0:1, :]
    if ltri is not None:
        lt = AB_.alloc("ltri", 128)
        p.dma("sp", lt, ltri, writes=["ltri"], tag="ldlt")
        C["ltri_sb"] = lt
    if utri64 is not None:
        ut = AB_.alloc("utri", 64)
        p.dma("sp", ut[0:64, :], utri64, writes=["utri"], tag="ldut")
        C["utri_sb"] = ut[0:64, :]
    if sel8 is not None:
        s8 = AB_.alloc("sel8", NSB)
        p.dma("sp", s8[0:64, :], sel8, writes=["sel8"], tag="ldsel")
        C["sel8_sb"] = s8[0:64, :]
    if rep8 is not None:
        r8 = AB_.alloc("rep8", GT)
        p.dma("sp", r8[0:64, :], rep8, writes=["rep8"], tag="ldrep")
        C["rep8_sb"] = r8[0:64, :]
    if iota_row is not None:
        io = AF_.alloc("iota", 128)
        p.dma("sp", io, iota_row, writes=["iota"], tag="ldio")
        C["iota_sb"] = io
    if iota_r1 is not None:
        io1 = AF_.alloc("iota1", 512)
        p.dma("sp", io1[0:1, :], iota_r1, writes=["iota1"], tag="ldio1")
        C["iota1_sb"] = io1[0:1, :]
    return C


def build_launch_A():
    nc = bass.Bass("TRN2", target_bir_lowering=False, debug=False)
    I = lambda n, s, d=F32: nc.dram_tensor(n, s, d, kind="ExternalInput").ap()
    O = lambda n, s, d=F32: nc.dram_tensor(n, s, d, kind="ExternalOutput").ap()

    xin = I("xin", [TL, DM])
    wq = I("wq", [DM, DM])
    wk = I("wk", [DM, KVH * Dh])
    wv = I("wv", [DM, KVH * Dh])
    impw1 = I("impw1", [DM, 64])
    impb1 = I("impb1", [64, 1])
    impw2 = I("impw2", [64, 1])
    impb2 = I("impb2", [1, 1])
    qnw = I("qnw", [1, Dh])
    knw = I("knw", [1, Dh])
    qnw_roll = I("qnw_roll", [1, Dh])
    knw_roll = I("knw_roll", [1, Dh])
    sgnv = I("sgnv", [1, Dh])
    cos_l = I("cos_l", [TL, Dh])
    sin_l = I("sin_l", [TL, Dh])
    kbeT = I("kbeT", [256, SK])
    kab = I("kab", [4, 128, SK])
    ident_b = I("ident_b", [128, 128], BF16)
    ones_row = I("ones_row", [1, 128], BF16)

    o_logits = O("o_logits", [1, TL])
    o_kh = O("o_kh", [TL, KVH * Dh], BF16)
    o_kr = O("o_kr", [TL, KVH * Dh], BF16)
    o_v = O("o_v", [TL, KVH * Dh], BF16)
    o_qrT = O("o_qrT", [DM, TL], BF16)
    o_qsp = O("o_qsp", [SK, DM])

    p = Prog(nc)
    with ExitStack() as st:
        big6 = st.enter_context(nc.psum_tensor("big6", [128, 3584], F32))
        psT = st.enter_context(nc.psum_tensor("psT", [128, 1024], BF16))
        af_t = st.enter_context(nc.sbuf_tensor("af", [128, 26000], F32))
        ab_t = st.enter_context(nc.sbuf_tensor("ab", [128, 49500], BF16))
        AF_ = Arena(p, af_t, 26000)
        AB_ = Arena(p, ab_t, 49500)
        C = make_shared_consts(p, AF_, AB_, ident_b=ident_b, ones_row=ones_row)
        emit_phase_A(p, AF_, AB_, big6, psT, C["cst"], C["ident_bs"], C["onesr_sb"],
                     C["one16_sb"], C["stat"], C["dump"], C["lg_col"],
                     xin, wq, wk, wv, impw1, impb1, impw2, impb2,
                     qnw, knw, qnw_roll, knw_roll, sgnv, cos_l, sin_l, kbeT, kab,
                     o_logits, o_kh, o_kr, o_v, o_qsp, o_qrT)
        p.emit()
    return nc



def _c(a):
    return np.ascontiguousarray(a)


def host_prep_A(inputs):
    x = np.asarray(inputs["L0_attn_a_hidden_states"])[0]
    cos = np.asarray(inputs["L0_attn_a_cos"])[0, :, 0, :]
    sin = np.asarray(inputs["L0_attn_a_sin"])[0, :, 0, :]
    ka = np.asarray(inputs["L0_attn_a_kron_a"])
    kb = np.asarray(inputs["L0_attn_a_kron_b"])
    ika = np.arange(SK) % 20
    ikb = np.arange(SK) // 20
    ka_expand = _c(ka.T[:, ika])
    kb_expandT = _c(kb[ikb].T)
    qnw = np.asarray(inputs["L0_attn_a_q_norm_w"])
    knw = np.asarray(inputs["L0_attn_a_k_norm_w"])
    sgn = np.concatenate([-np.ones(64, np.float32), np.ones(64, np.float32)])
    ident = np.eye(128, dtype=BF)
    ones_row = np.ones((1, 128), BF)

    maps = []
    for c in range(NCN):
        t0 = c * TL
        kab = np.broadcast_to(ka_expand.T[None, :, :], (1, SK, 128))
        kab = np.broadcast_to(ka_expand[4 * c:4 * c + 4, None, :], (4, 128, SK))
        maps.append({
            "xin": _c(x[t0:t0 + TL]),
            "wq": np.asarray(inputs["L0_attn_a_wq"]),
            "wk": np.asarray(inputs["L0_attn_a_wk"]),
            "wv": np.asarray(inputs["L0_attn_a_wv"]),
            "impw1": np.asarray(inputs["L0_attn_a_imp_w1"]),
            "impb1": _c(np.asarray(inputs["L0_attn_a_imp_b1"]).reshape(64, 1)),
            "impw2": np.asarray(inputs["L0_attn_a_imp_w2"]),
            "impb2": _c(np.asarray(inputs["L0_attn_a_imp_b2"]).reshape(1, 1)),
            "qnw": _c(qnw.reshape(1, Dh)),
            "knw": _c(knw.reshape(1, Dh)),
            "qnw_roll": _c(np.roll(qnw, 64).reshape(1, Dh)),
            "knw_roll": _c(np.roll(knw, 64).reshape(1, Dh)),
            "sgnv": _c(sgn.reshape(1, Dh)),
            "cos_l": _c(cos[t0:t0 + TL]),
            "sin_l": _c(sin[t0:t0 + TL]),
            "kbeT": kb_expandT,
            "kab": _c(kab),
            "ident_b": ident,
            "ones_row": ones_row,
        })
    return maps


def run_A(inputs):
    nc = build_launch_A()
    maps = host_prep_A(inputs)
    res = run_bass_kernel_spmd(nc, maps, core_ids=list(range(NCN)))
    return res.results



INV_SQRT_D = float(1.0 / np.sqrt(np.float32(Dh)))
GT = S // 128
SUP = [(0, 9, 384), (9, 9, 384), (18, 9, 384), (27, 9, 384),
       (36, 9, 384), (45, 9, 384), (54, 10, 512)]
NSB = len(SUP)
SUPOFF = [0]
for _s in SUP:
    SUPOFF.append(SUPOFF[-1] + _s[2])
GK = SUPOFF[-1]
KEYS = GK + SK
NKT = KEYS // 128
NKC = [512] * 6 + [384]
NQ = TL + SK



def emit_phase_B(p, AF_, AB_, big6, psT, C, debug,
                 xin, kr_all, kh_all, v_all, qrT,
                 wo, gate_wT, gateT, upT, downT, WkT, WvT, WmbT, bk_r, bv_r, bmb_r,
                 memory, normT, kbeT, kab_full, ka_eL, kb_e, qnw_col, knw_t4, ss_c,
                 o_x2, o_h3T, o_dmem, o_dnorm,
                 lgT_load, lgl_load, qspT=None, g_qsp=None, O=None):
    nc = p.nc
    ps1 = big6[:, 3072:3584]
    cst = C["cst"]; ident_bs = C["ident_bs"]; onesr_sb = C["onesr_sb"]
    ones_sb = C["ones_sb"]; one16_sb = C["one16_sb"]; ltri_sb = C["ltri_sb"]
    utri_sb = C["utri_sb"]; sel8_sb = C["sel8_sb"]; rep8_sb = C["rep8_sb"]
    iota_sb = C["iota_sb"]; iota1_sb = C["iota1_sb"]
    stat = C["stat"]; dump = C["dump"]; tw_all = C["tw_all"]; rl_f = C["rl_f"]
    qsT_dram = nc.dram_tensor("qsT_dram", [DM, SK], BF16).ap()
    osk_dram = nc.dram_tensor("osk_dram", [5 * 128, DM], BF16).ap()
    fullT_dram = nc.dram_tensor("fullT_dram", [DM, TL], BF16).ap()
    mb_dram = nc.dram_tensor("mb_dram", [TL, DM], BF16).ap()
    if debug:
        d_stats = O("d_stats", [1, 8])
        d_ks = O("d_ks", [SK, KVH * Dh])
        d_vs = O("d_vs", [SK, KVH * Dh])
        d_kdT = O("d_kdT", [128, KVH * KEYS], BF16)
        d_vall = O("d_vall", [128, NKT * 512], BF16)
        d_pen = O("d_pen", [1, KEYS])
        d_x1 = O("d_x1", [TL, DM])
        d_tw = O("d_tw", [TL, NE])
        d_osk = O("d_osk", [5 * 128, DM], BF16)
        d_mb = O("d_mb", [TL, DM], BF16)
        d_num = O("d_num", [128, DM])
        d_wmv = O("d_wmv", [128, DM], BF16)
        d_den = O("d_den", [128, 2 * H])
        d_mk16 = O("d_mk16", [128, KVH * FD], BF16)
        d_h3b = O("d_h3b", [128, 16 * 128], BF16)
        d_mb0 = O("d_mb0", [128, 1024], BF16)
        d_wmbh = O("d_wmbh", [128, 2048], BF16)
        d_mbz = O("d_mbz", [128, 1024])
    lg_sb = AF_.alloc("lg", GT)
    lgT_load(lg_sb)
    wcol = AF_.alloc("wcol", GT)
    p.act(lambda e: e.activation(wcol, lg_sb, AF.Sigmoid, bias=cst[:, 0:1]),
          reads=["lg", "cst"], writes=["wcol"], tag="sigw")
    red = AF_.alloc("red", 4)
    red16 = AB_.alloc("red16", 4)
    p.dve(lambda e: e.reduce_max(red[:, 0:1], lg_sb, axis=AX), reads=["lg"], writes=["red"], tag="gmx")
    p.dve(lambda e: e.reduce_max(red[:, 1:2], lg_sb, axis=AX, negate=True),
          reads=["lg"], writes=["red"], tag="gmn")
    p.dve(lambda e: e.tensor_copy(red16[:, 0:2], red[:, 0:2]), reads=["red"], writes=["red16"], tag="redc")
    p.tr(psT[0:2, 0:128], red16[:, 0:2], ident_bs, reads=["red16", "identb"], writes=["psT"], tag="redtr")
    rr_row = AF_.alloc("rrrow", 256)
    p.dve(lambda e: e.tensor_copy(rr_row[0:2, 0:128], psT[0:2, 0:128]),
          reads=["psT"], writes=["rrrow"], tag="redcp")
    th = AF_.alloc("th", 4)
    p.dve(lambda e: e.reduce_max(th[0:2, 0:1], rr_row[0:2, 0:128], axis=AX),
          reads=["rrrow"], writes=["th"], tag="redmx")
    th16 = AB_.alloc("th16", 2)
    p.dve(lambda e: e.tensor_copy(th16[0:2, 0:1], th[0:2, 0:1]), reads=["th"], writes=["th16"], tag="thc")
    p.tr(psT[0:1, 128:130], th16[0:2, 0:1], ident_bs[0:2, 0:2],
         reads=["th16", "identb"], writes=["psT"], tag="thtr")
    sc_ = AF_.alloc("sc", 16)
    mm16 = AB_.alloc("mm16", 2)
    p.dve(lambda e: e.tensor_copy(mm16[0:1, 0:2], psT[0:1, 128:130]),
          reads=["psT"], writes=["mm16"], tag="scini0")
    p.mm(ps1[:, 0:2], onesr_sb[0:1, :], mm16[0:1, 0:2], True, True,
         reads=["onesr", "mm16"], writes=["big6"], tag="scbc")
    p.dve(lambda e: e.tensor_copy(sc_[:, 0:2], ps1[:, 0:2]), reads=["big6"], writes=["sc"], tag="scini")
    p.dve(lambda e: e.tensor_scalar(sc_[:, 1:2], sc_[:, 0:1], 1.0, None, ALU.add),
          reads=["sc"], writes=["sc"], tag="hii")
    p.dve(lambda e: e.tensor_scalar(sc_[:, 0:1], ps1[:, 1:2], -1.0, -1.0, ALU.mult, ALU.add),
          reads=["big6"], writes=["sc"], tag="loi")
    cmp_sb = AF_.alloc("cmp", GT)
    cnt16 = AB_.alloc("cnt16", 2)
    NBIS = 44
    for it in range(NBIS):
        p.dve(lambda e: e.tensor_scalar(sc_[:, 2:3], sc_[:, 0:1], sc_[:, 1:2], 0.5,
                                        ALU.add, ALU.mult), reads=["sc"], writes=["sc"], tag=f"mid{it}")
        p.dve(lambda e: e.tensor_scalar(cmp_sb, lg_sb, sc_[:, 2:3], None, ALU.is_ge),
              reads=["lg", "sc"], writes=["cmp"], tag=f"cmp{it}")
        p.dve(lambda e: e.reduce_sum(red[:, 2:3], cmp_sb, axis=AX),
              reads=["cmp"], writes=["red"], tag=f"cr{it}")
        p.dve(lambda e: e.tensor_copy(cnt16[:, 0:1], red[:, 2:3]),
              reads=["red"], writes=["cnt16"], tag=f"cc{it}")
        p.mm(ps1[0:1, 0:1], cnt16[:, 0:1], ones_sb, True, True,
             reads=["cnt16", "ones"], writes=["big6"], tag=f"cm{it}")
        p.dve(lambda e: e.tensor_copy(cnt16[0:1, 1:2], ps1[0:1, 0:1]),
              reads=["big6"], writes=["cnt16"], tag=f"cc2{it}")
        p.mm(ps1[:, 1:2], onesr_sb[0:1, :], cnt16[0:1, 1:2], True, True,
             reads=["onesr", "cnt16"], writes=["big6"], tag=f"cb{it}")
        p.dve(lambda e: e.tensor_single_scalar(sc_[:, 4:5], ps1[:, 1:2], float(TK), ALU.is_ge),
              reads=["big6"], writes=["sc"], tag=f"pred{it}")
        p.dve(lambda e: e.tensor_tensor(sc_[:, 5:6], sc_[:, 2:3], sc_[:, 0:1], ALU.subtract),
              reads=["sc"], writes=["sc"], tag=f"d1{it}")
        p.dve(lambda e: e.tensor_tensor(sc_[:, 5:6], sc_[:, 5:6], sc_[:, 4:5], ALU.mult),
              reads=["sc"], writes=["sc"], tag=f"d2{it}")
        p.dve(lambda e: e.tensor_tensor(sc_[:, 0:1], sc_[:, 0:1], sc_[:, 5:6], ALU.add),
              reads=["sc"], writes=["sc"], tag=f"d3{it}")
        p.dve(lambda e: e.tensor_tensor(sc_[:, 6:7], sc_[:, 2:3], sc_[:, 1:2], ALU.subtract),
              reads=["sc"], writes=["sc"], tag=f"d4{it}")
        p.dve(lambda e: e.tensor_scalar(sc_[:, 7:8], sc_[:, 4:5], -1.0, 1.0, ALU.mult, ALU.add),
              reads=["sc"], writes=["sc"], tag=f"d5{it}")
        p.dve(lambda e: e.tensor_tensor(sc_[:, 6:7], sc_[:, 6:7], sc_[:, 7:8], ALU.mult),
              reads=["sc"], writes=["sc"], tag=f"d7{it}")
        p.dve(lambda e: e.tensor_tensor(sc_[:, 1:2], sc_[:, 1:2], sc_[:, 6:7], ALU.add),
              reads=["sc"], writes=["sc"], tag=f"d8{it}")
    mask_sb = AF_.alloc("mask", GT)
    p.dve(lambda e: e.tensor_scalar(mask_sb, lg_sb, sc_[:, 0:1], None, ALU.is_ge),
          reads=["lg", "sc"], writes=["mask"], tag="mask")
    restw = AF_.alloc("restw", GT)
    p.dve(lambda e: e.tensor_scalar(restw, mask_sb, -1.0, 1.0, ALU.mult, ALU.add),
          reads=["mask"], writes=["restw"], tag="rw1")
    p.dve(lambda e: e.tensor_tensor(restw, restw, wcol, ALU.mult),
          reads=["restw", "wcol"], writes=["restw"], tag="rw3")
    if debug:
        stat_o = AF_.alloc("stat_o", 8)
        p.dve(lambda e: e.tensor_copy(stat_o[0:1, 0:8], sc_[0:1, 0:8]),
              reads=["sc"], writes=["stat_o"], tag="sto")
        p.dma("sp", d_stats, stat_o[0:1, 0:8], reads=["stat_o"], tag="std")

    mask16 = AB_.alloc("mask16", GT)
    p.dve(lambda e: e.tensor_copy(mask16, mask_sb), reads=["mask"], writes=["mask16"], tag="m16")
    prefx = AF_.alloc("prefx", GT)
    for gt in range(GT):
        p.mm(ps1[:, 0:1], ltri_sb, mask16[:, gt:gt + 1], True, True,
             reads=["ltri", "mask16"], writes=["big6"], tag=f"pf{gt}")
        p.dve(lambda e, o=prefx[:, gt:gt + 1]: e.tensor_copy(o, ps1[:, 0:1]),
              reads=["big6"], writes=["prefx"], tag=f"pfc{gt}")
    p.mm(ps1[0:1, 0:GT], ones_sb, mask16, True, True,
         reads=["ones", "mask16"], writes=["big6"], tag="tot")
    tot16 = AB_.alloc("tot16", GT)
    p.dve(lambda e: e.tensor_copy(tot16[0:1, :], ps1[0:1, 0:GT]), reads=["big6"], writes=["tot16"], tag="totc")
    p.mm(ps1[0:64, 1:2], tot16[0:1, :], one16_sb[0:1, :], True, True,
         reads=["tot16", "one16"], writes=["big6"], tag="totT")
    totc16 = AB_.alloc("totc16", 1)
    p.dve(lambda e: e.tensor_copy(totc16[0:64, :], ps1[0:64, 1:2]), reads=["big6"], writes=["totc16"], tag="totcc")
    csup = AF_.alloc("csup", NSB)
    p.mm(ps1[0:1, 8:8 + NSB], totc16[0:64, :], sel8_sb[0:64, :], True, True,
         reads=["totc16", "sel8"], writes=["big6"], tag="csup")
    p.dve(lambda e: e.tensor_copy(csup[0:1, :], ps1[0:1, 8:8 + NSB]),
          reads=["big6"], writes=["csup"], tag="csupc")
    p.mm(ps1[0:1, 32:32 + GT], totc16[0:64, :], utri_sb[0:64, :], True, True,
         reads=["totc16", "utri"], writes=["big6"], tag="offs")
    offs = AF_.alloc("offs", GT)
    p.dve(lambda e: e.tensor_copy(offs[0:1, :], ps1[0:1, 32:32 + GT]), reads=["big6"], writes=["offs"], tag="offc")
    offs16r = AB_.alloc("offs16r", GT)
    p.dve(lambda e: e.tensor_copy(offs16r[0:1, :], offs[0:1, :]), reads=["offs"], writes=["offs16r"], tag="of16")
    p.mm(ps1[0:64, 2:3], offs16r[0:1, :], one16_sb[0:1, :], True, True,
         reads=["offs16r", "one16"], writes=["big6"], tag="offT")
    offc16 = AB_.alloc("offc16", 1)
    p.dve(lambda e: e.tensor_copy(offc16[0:64, :], ps1[0:64, 2:3]), reads=["big6"], writes=["offc16"], tag="offcc")
    p.mm(ps1[0:1, 96:96 + GT], offc16[0:64, :], rep8_sb[0:64, :], True, True,
         reads=["offc16", "rep8"], writes=["big6"], tag="strt")
    p.dve(lambda e: e.tensor_tensor(offs[0:1, :], offs[0:1, :], ps1[0:1, 96:96 + GT], ALU.subtract),
          reads=["offs", "big6"], writes=["offs"], tag="orel")
    offs16 = AB_.alloc("offs16", GT)
    p.dve(lambda e: e.tensor_copy(offs16[0:1, :], offs[0:1, :]), reads=["offs"], writes=["offs16"], tag="orc")
    p.mm(ps1[:, 0:GT], onesr_sb[0:1, :], offs16[0:1, :], True, True,
         reads=["onesr", "offs16"], writes=["big6"], tag="orb")
    rkb = AF_.alloc("rkb", GT)
    p.dve(lambda e: e.tensor_scalar(rkb, mask_sb, -1000.0, 999.0, ALU.mult, ALU.add),
          reads=["mask"], writes=["rkb"], tag="rk1")
    p.dve(lambda e: e.tensor_tensor(rkb, rkb, prefx, ALU.add),
          reads=["rkb", "prefx"], writes=["rkb"], tag="rk2")
    p.dve(lambda e: e.tensor_tensor(rkb, rkb, ps1[:, 0:GT], ALU.add),
          reads=["rkb", "big6"], writes=["rkb"], tag="rk3")

    pen16 = AB_.alloc("pen16", KEYS)
    pen_f = AF_.alloc("pen_f", 512)
    p.dve(lambda e: e.memset(pen16[0:1, :], 0.0), writes=["pen16"], tag="penz")
    for sb, (st0, ntl, cap) in enumerate(SUP):
        p.dve(lambda e, sci=csup[0:1, sb:sb + 1], o=pen_f[0:1, 0:cap], i_=iota1_sb[0:1, 0:cap]:
              e.tensor_single_scalar(o, i_, sci, ALU.is_ge),
              reads=["iota1", "csup"], writes=["pen_f"], tag=f"pen{sb}")
        p.dve(lambda e, o=pen16[0:1, SUPOFF[sb]:SUPOFF[sb + 1]], i_=pen_f[0:1, 0:cap]:
              e.tensor_scalar_mul(o, i_, -30000.0),
              reads=["pen_f"], writes=["pen16"], tag=f"pen2{sb}")
    if debug:
        pen_o = AF_.alloc("pen_o", KEYS)
        p.dve(lambda e: e.tensor_copy(pen_o[0:1, :], pen16[0:1, :]), reads=["pen16"], writes=["pen_o"], tag="peno")
        p.dma("sp", d_pen, pen_o[0:1, :], reads=["pen_o"], tag="dpen")

    m_attn = AB_.mark()
    m_attnf = AF_.mark()
    kdT = AB_.alloc("kdT", KVH * KEYS)
    vall = AB_.alloc("vall", NKT * 512)
    m_gat = AB_.mark()
    m_gaf = AF_.mark()
    G16 = AB_.alloc("G16", 128)
    kv_in = AB_.alloc("kvin", 1024)
    gat_o = AB_.alloc("gato", 1024)
    shc = AF_.alloc("shc", 1)
    for sb, (st0, ntl, cap) in enumerate(SUP):
        nblk = cap // 128
        blk_groups = [list(range(g, min(g + 3, nblk))) for g in range(0, nblk, 3)]
        for bg in blk_groups:
            for tt8 in range(ntl):
                gt = st0 + tt8
                p.dma("sp", kv_in[:, 0:512], kr_all[gt * 128:(gt + 1) * 128, :],
                      reads=["gkr"], writes=["kvin"], tag=f"gld{gt}")
                p.dma("sp", kv_in[:, 512:1024], v_all[gt * 128:(gt + 1) * 128, :],
                      reads=["gv"], writes=["kvin"], tag=f"gldv{gt}")
                for bi, blk in enumerate(bg):
                    p.dve(lambda e, sci=rkb[:, gt:gt + 1], b=-(blk * 128.0):
                          e.tensor_scalar(shc, sci, b, None, ALU.add),
                          reads=["rkb"], writes=["shc"], tag=f"sh{gt}_{blk}")
                    p.dve(lambda e: e.tensor_scalar(G16, iota_sb, shc[:, 0:1], None, ALU.is_equal),
                          reads=["iota", "shc"], writes=["G16"], tag=f"G{gt}_{blk}")
                    for hf in range(2):
                        p.mm(big6[:, bi * 1024 + hf * 512: bi * 1024 + (hf + 1) * 512],
                             G16, kv_in[:, hf * 512:(hf + 1) * 512],
                             start=(tt8 == 0), stop=(tt8 == ntl - 1),
                             reads=["G16", "kvin"], writes=["big6"], tag=f"gm{gt}_{blk}_{hf}")
            for bi, blk in enumerate(bg):
                ktile = SUPOFF[sb] // 128 + blk
                p.dve(lambda e, o=gat_o[:, 0:1024], i_=big6[:, bi * 1024:(bi + 1) * 1024]:
                      e.tensor_copy(o, i_), reads=["big6"], writes=["gato"], tag=f"ge{sb}_{blk}")
                for kv in range(KVH):
                    p.tr(psT[:, kv * 128:(kv + 1) * 128], gat_o[:, kv * 128:(kv + 1) * 128],
                         ident_bs, reads=["gato", "identb"], writes=["psT"], tag=f"gt{sb}_{blk}_{kv}")
                for kv in range(KVH):
                    p.dve(lambda e, o=kdT[:, kv * KEYS + ktile * 128: kv * KEYS + (ktile + 1) * 128],
                          i_=psT[:, kv * 128:(kv + 1) * 128]:
                          e.tensor_copy(o, i_), reads=["psT"], writes=["kdT"], tag=f"ge2{sb}_{blk}_{kv}")
                p.dve(lambda e, o=vall[:, ktile * 512:(ktile + 1) * 512], i_=gat_o[:, 512:1024]:
                      e.tensor_copy(o, i_), reads=["gato"], writes=["vall"], tag=f"gv{sb}_{blk}")
    AB_.pop(m_gat)
    AF_.pop(m_gaf)

    m_afsk = AF_.mark()
    m_absk = AB_.mark()
    kbe_sb = AF_.alloc("kbe", 2 * SK)
    p.dma("sp", kbe_sb.rearrange("p (a s) -> p a s", a=2),
          kbeT.rearrange("(a p) s -> p a s", p=128), writes=["kbe"], tag="ldkbeB")
    ss_sb = AF_.alloc("ss", 2)
    ss16t = AB_.alloc("ss16t", 1)
    p.dma("sp", ss_sb[0:1, 0:1], ss_c, writes=["ss"], tag="ldss")
    p.dve(lambda e: e.tensor_copy(ss16t[0:1, :], ss_sb[0:1, 0:1]),
          reads=["ss"], writes=["ss16t"], tag="ssc")
    p.mm(ps1[:, 2:3], onesr_sb[0:1, :], ss16t[0:1, :], True, True,
         reads=["onesr", "ss16t"], writes=["big6"], tag="ssbc")
    p.dve(lambda e: e.tensor_copy(ss_sb[:, 1:2], ps1[:, 2:3]), reads=["big6"], writes=["ss"], tag="ssbe")
    knwss = AF_.alloc("knwss", 512)
    knw4 = AF_.alloc("knw4", 512)
    p.dma("sp", knw4[0:1, :], knw_t4, writes=["knw4"], tag="ldknw4")
    p.dve(lambda e: e.tensor_single_scalar(knw4[0:1, :], knw4[0:1, :], ss_sb[0:1, 0:1], ALU.mult),
          reads=["knw4", "ss"], writes=["knw4"], tag="knwss1")
    knw416 = AB_.alloc("knw416", 512)
    p.dve(lambda e: e.tensor_copy(knw416[0:1, :], knw4[0:1, :]), reads=["knw4"], writes=["knw416"], tag="knwssc")
    p.mm(ps1[:, 0:512], onesr_sb[0:1, :], knw416[0:1, :], True, True,
         reads=["onesr", "knw416"], writes=["big6"], tag="knwbc")
    p.dve(lambda e: e.tensor_copy(knwss, ps1[:, 0:512]), reads=["big6"], writes=["knwss"], tag="knwbe")

    kabf = AF_.alloc("kabf0", 2 * SK, aka=("kabf1",))
    PTt = AB_.alloc("PTt", SK)
    ysk_b = AB_.alloc("yskb", 512)
    kin_b = AB_.alloc("kinb", 512)
    ksf = AF_.alloc("ksf", 5 * 512)
    ks16 = AB_.alloc("ks16", 5 * 512)
    for pas in range(2):
        src_all = kh_all if pas == 0 else v_all
        src_nm = "gkh" if pas == 0 else "gv"
        for gt in range(GT):
            i_glob, jh = gt // 2, gt % 2
            if jh == 0:
                p.dma("sp", kabf[:, (i_glob % 2) * SK:(i_glob % 2 + 1) * SK],
                      kab_full[i_glob], writes=[f"kabf{i_glob % 2}"], tag=f"kab{pas}_{i_glob}")
            p.dma("sp", kin_b, src_all[gt * 128:(gt + 1) * 128, :],
                  reads=[src_nm], writes=["kinb"], tag=f"skld{pas}_{gt}")
            p.dve(lambda e, sci=restw[:, gt:gt + 1]: e.tensor_scalar_mul(ysk_b, kin_b, sci),
                  reads=["kinb", "restw"], writes=["yskb"], tag=f"sky{pas}_{gt}")
            p.dve(lambda e, a=kbe_sb[:, jh * SK:(jh + 1) * SK],
                  b=kabf[:, (i_glob % 2) * SK:(i_glob % 2 + 1) * SK]:
                  e.tensor_mul(PTt, a, b),
                  reads=["kbe", f"kabf{i_glob % 2}"], writes=["PTt"], tag=f"skp{pas}_{gt}")
            for sp in range(5):
                p.mm(big6[:, sp * 512:(sp + 1) * 512], PTt[:, sp * 128:(sp + 1) * 128],
                     ysk_b, start=(gt == 0), stop=(gt == GT - 1),
                     reads=["PTt", "yskb"], writes=["big6"], tag=f"skm{pas}_{gt}_{sp}")
        for sp in range(5):
            p.dve(lambda e, o=ksf[:, sp * 512:(sp + 1) * 512], i_=big6[:, sp * 512:(sp + 1) * 512]:
                  e.tensor_copy(o, i_), reads=["big6"], writes=["ksf"], tag=f"ske{pas}_{sp}")
        if pas == 0:
            for sp in range(5):
                p.dve(lambda e, o=ksf[:, sp * 512:(sp + 1) * 512]:
                      e.tensor_tensor(o, o, knwss, ALU.mult),
                      reads=["ksf", "knwss"], writes=["ksf"], tag=f"skw{sp}")
        else:
            for sp in range(5):
                p.dve(lambda e, o=ksf[:, sp * 512:(sp + 1) * 512]:
                      e.tensor_scalar_mul(o, o, ss_sb[:, 1:2]),
                      reads=["ksf", "ss"], writes=["ksf"], tag=f"svw{sp}")
        p.dve(lambda e: e.tensor_copy(ks16, ksf), reads=["ksf"], writes=["ks16"], tag=f"skc{pas}")
        if debug:
            dd = d_ks if pas == 0 else d_vs
            p.dma("pool", dd.rearrange("(a p) c -> p a c", p=128),
                  ks16.rearrange("p (a c) -> p a c", a=5), reads=["ks16"], tag=f"dks{pas}")
        if pas == 0:
            for sp in range(5):
                for kv in range(KVH):
                    p.tr(psT[:, (sp % 2) * 128:(sp % 2) * 128 + 128],
                         ks16[:, sp * 512 + kv * 128: sp * 512 + (kv + 1) * 128],
                         ident_bs, reads=["ks16", "identb"], writes=["psT"], tag=f"ktr{sp}_{kv}")
                    p.dve(lambda e, o=kdT[:, kv * KEYS + GK + sp * 128:
                                          kv * KEYS + GK + (sp + 1) * 128],
                          i_=psT[:, (sp % 2) * 128:(sp % 2) * 128 + 128]:
                          e.tensor_copy(o, i_),
                          reads=["psT"], writes=["kdT"], tag=f"kte{sp}_{kv}")
        else:
            for sp in range(5):
                p.dve(lambda e, o=vall[:, (22 + sp) * 512:(23 + sp) * 512],
                      i_=ks16[:, sp * 512:(sp + 1) * 512]:
                      e.tensor_copy(o, i_), reads=["ks16"], writes=["vall"], tag=f"vse{sp}")
    AB_.pop(m_absk)
    AF_.pop(m_afsk)
    if debug:
        p.dma("sp", d_kdT, kdT, reads=["kdT"], tag="dkdT")
        p.dma("sp", d_vall, vall, reads=["vall"], tag="dvall")

    m_q3f = AF_.mark()
    m_q3b = AB_.mark()
    ss_sb2 = AF_.alloc("ss2", 2)
    ss16b = AB_.alloc("ss16b", 1)
    p.dma("sp", ss_sb2[0:1, 0:1], ss_c, writes=["ss2"], tag="ldss2")
    p.dve(lambda e: e.tensor_copy(ss16b[0:1, :], ss_sb2[0:1, 0:1]), reads=["ss2"], writes=["ss16b"], tag="ss2c")
    p.mm(ps1[:, 2:3], onesr_sb[0:1, :], ss16b[0:1, :], True, True,
         reads=["onesr", "ss16b"], writes=["big6"], tag="ss2bc")
    p.dve(lambda e: e.tensor_copy(ss_sb2[:, 1:2], ps1[:, 2:3]), reads=["big6"], writes=["ss2"], tag="ss2be")
    qnwc = AF_.alloc("qnwc", 1)
    p.dma("sp", qnwc, qnw_col, writes=["qnwc"], tag="ldqnwc")
    p.dve(lambda e: e.tensor_scalar_mul(qnwc, qnwc, ss_sb2[:, 1:2]),
          reads=["qnwc", "ss2"], writes=["qnwc"], tag="qnwss")
    qsT16 = AB_.alloc("qsT16", 2 * SK)
    HSK = SK // 2
    qs_acc = AF_.alloc("qs_acc", 16 * HSK)
    qs_in = AF_.alloc("qs_in", 16 * HSK)
    if qspT is not None:
        for hs in range(2):
            p.dma("sp", qs_acc.rearrange("p (a s) -> p a s", a=16),
                  qspT[0].rearrange("(a p) s -> p a s", p=128)[:, :, hs * HSK:(hs + 1) * HSK],
                  writes=["qs_acc"], tag=f"qsp0_{hs}")
            for c in range(1, NCN):
                p.dma("sp", qs_in.rearrange("p (a s) -> p a s", a=16),
                      qspT[c].rearrange("(a p) s -> p a s", p=128)[:, :, hs * HSK:(hs + 1) * HSK],
                      writes=["qs_in"], tag=f"qsp{c}_{hs}")
                p.dve(lambda e: e.tensor_add(qs_acc, qs_acc, qs_in),
                      reads=["qs_acc", "qs_in"], writes=["qs_acc"], tag=f"qsa{c}_{hs}")
            for a in range(16):
                p.dve(lambda e, o=qsT16[:, (a % 2) * SK + hs * HSK:(a % 2) * SK + (hs + 1) * HSK],
                      i_=qs_acc[:, a * HSK:(a + 1) * HSK]:
                      e.tensor_scalar_mul(o, i_, qnwc[:, 0:1]),
                      reads=["qs_acc", "qnwc"], writes=["qsT16"], tag=f"qsc{a}_{hs}")
                p.dma("sp", qsT_dram[a * 128:(a + 1) * 128, hs * HSK:(hs + 1) * HSK],
                      qsT16[:, (a % 2) * SK + hs * HSK:(a % 2) * SK + (hs + 1) * HSK],
                      reads=["qsT16"], writes=["qsT_dram"], tag=f"qst{a}_{hs}")

    else:
        qs16i = AB_.alloc("qs16i", 128)
        qsTf = AF_.alloc("qsTf", SK)
        qs_inb = AF_.alloc("qs_inb", 128)
        for a in range(16):
            for sp in range(5):
                p.dma("sp", qs_inb, g_qsp[sp * 128:(sp + 1) * 128, a * 128:(a + 1) * 128],
                      reads=["gqsp"], writes=["qs_inb"], tag=f"qgl{a}_{sp}")
                p.dve(lambda e: e.tensor_copy(qs16i, qs_inb),
                      reads=["qs_inb"], writes=["qs16i"], tag=f"qgc{a}_{sp}")
                p.tr(psT[:, sp * 128:(sp + 1) * 128], qs16i, ident_bs,
                     reads=["qs16i", "identb"], writes=["psT"], tag=f"qgt{a}_{sp}")
            p.dve(lambda e: e.tensor_copy(qsTf, psT[:, 0:SK]),
                  reads=["psT"], writes=["qsTf"], tag=f"qge{a}")
            p.dve(lambda e: e.tensor_scalar_mul(qsTf, qsTf, qnwc[:, 0:1]),
                  reads=["qsTf", "qnwc"], writes=["qsTf"], tag=f"qgs{a}")
            p.dve(lambda e: e.tensor_copy(qsT16[:, 0:SK], qsTf),
                  reads=["qsTf"], writes=["qsT16"], tag=f"qgb{a}")
            p.dma("sp", qsT_dram[a * 128:(a + 1) * 128, :], qsT16[:, 0:SK],
                  reads=["qsT16"], writes=["qsT_dram"], tag=f"qgw{a}")
    AF_.pop(m_q3f)
    AB_.pop(m_q3b)

    p_sb = AB_.alloc("p_sb", 2 * KEYS)
    pstg = AB_.alloc("pstg", 1024)
    att_st = AF_.alloc("attst", 8)
    qcur = AB_.alloc("qcur", 2 * TL)

    kaeL_sb = AF_.alloc("kaeL", 5 * 4)
    kbe2_sb = AF_.alloc("kbe2", 5 * 256)
    p.dma("sp", kaeL_sb.rearrange("p (a i) -> p a i", a=5),
          ka_eL.rearrange("(a p) i -> p a i", p=128), writes=["kaeL"], tag="ldkaeL")
    p.dma("sp", kbe2_sb.rearrange("p (a j) -> p a j", a=5),
          kb_e.rearrange("(a p) j -> p a j", p=128), writes=["kbe2"], tag="ldkbe2")
    ploc = AB_.alloc("ploc", 5 * 512)

    def build_ploc(half, tagx):
        for sp in range(5):
            for i2 in range(2):
                il = half * 2 + i2
                p.dve(lambda e, o=ploc[:, sp * 512 + i2 * 256: sp * 512 + (i2 + 1) * 256],
                      a=kbe2_sb[:, sp * 256:(sp + 1) * 256],
                      sci=kaeL_sb[:, sp * 4 + il: sp * 4 + il + 1]:
                      e.tensor_scalar_mul(o, a, sci),
                      reads=["kbe2", "kaeL"], writes=["ploc"], tag=f"plc{tagx}_{sp}_{il}")
    lgl_sb = AF_.alloc("lgl", NT)
    lgl_load(lgl_sb)
    maskb = AB_.alloc("maskb", TL)
    m_mrow = AB_.mark()
    mloc16 = AB_.alloc("mloc16", NT)
    p.dve(lambda e: e.tensor_scalar(mloc16, lgl_sb, sc_[:, 0:1], None, ALU.is_ge),
          reads=["lgl", "sc"], writes=["mloc16"], tag="mloc")
    mrow16 = AB_.alloc("mrow16", TL)
    for tt in range(NT):
        p.tr(psT[0:1, 256:384], mloc16[:, tt:tt + 1], ident_bs,
             reads=["mloc16", "identb"], writes=["psT"], tag=f"mtr{tt}")
        p.dve(lambda e, o=mrow16[0:1, tt * 128:(tt + 1) * 128]:
              e.tensor_copy(o, psT[0:1, 256:384]), reads=["psT"], writes=["mrow16"], tag=f"mte{tt}")
    for hf in range(2):
        p.mm(ps1[:, 0:512], onesr_sb[0:1, :], mrow16[0:1, hf * 512:(hf + 1) * 512],
             True, True, reads=["onesr", "mrow16"], writes=["big6"], tag=f"mbc{hf}")
        p.dve(lambda e, o=maskb[:, hf * 512:(hf + 1) * 512]: e.tensor_copy(o, ps1[:, 0:512]),
              reads=["big6"], writes=["maskb"], tag=f"mbe{hf}")
    AB_.pop(m_mrow)

    p.alias("b6s", ["big6"])
    p.alias("b6pv", ["big6"])
    pend = [None]
    nstep = [0]

    def attn_scores(hh, qsrc, qname, tagp, buf):
        kbase = (hh // 4) * KEYS
        off = 0
        for ci, w in enumerate(NKC):
            p.mm(big6[:, off:off + w], qsrc, kdT[:, kbase + off: kbase + off + w],
                 start=True, stop=True, reads=[qname, "kdT"], writes=["b6s"], tag=f"sc{tagp}_{ci}")
            off += w
        st_ = att_st[:, buf * 4:(buf + 1) * 4]
        p.dve(lambda e: e.reduce_max(st_[:, 0:1], big6[:, 0:KEYS], axis=AX),
              reads=["b6s"], writes=["attst"], tag=f"mx{tagp}")
        p.dve(lambda e: e.tensor_scalar_mul(st_[:, 1:2], st_[:, 0:1], -INV_SQRT_D),
              reads=["attst"], writes=["attst"], tag=f"mb{tagp}")
        p.act(lambda e, o=p_sb[:, buf * KEYS:(buf + 1) * KEYS]:
              e.activation(o, big6[:, 0:KEYS], AF.Exp,
                           bias=st_[:, 1:2], scale=INV_SQRT_D, accum_out=st_[:, 2:3]),
              reads=["b6s", "attst"], writes=["p_sb", "attst"], tag=f"exp{tagp}")
        p.act(lambda e: e.activation(st_[:, 0:1], st_[:, 1:2], AF.Exp, bias=cst[:, 0:1]),
              reads=["attst", "cst"], writes=["attst"], tag=f"pz{tagp}")
        p.dve(lambda e: e.tensor_scalar_mul(st_[:, 0:1], st_[:, 0:1], -float(GK - TK)),
              reads=["attst"], writes=["attst"], tag=f"pz2{tagp}")
        p.dve(lambda e: e.tensor_add(st_[:, 2:3], st_[:, 2:3], st_[:, 0:1]),
              reads=["attst"], writes=["attst"], tag=f"pz3{tagp}")
        p.dve(lambda e: e.reciprocal(st_[:, 3:4], st_[:, 2:3]),
              reads=["attst"], writes=["attst"], tag=f"li{tagp}")

    def attn_pv(hh, tagp, buf, evac_fn):
        kv = hh // 4
        pbase = buf * KEYS
        groups = [list(range(g, min(g + 4, NKT))) for g in range(0, NKT, 4)]
        for gi in range(len(groups) + 1):
            if gi < len(groups):
                grp = groups[gi]
                half = gi % 2
                for j, kt in enumerate(grp):
                    p.tr(psT[:, half * 512 + j * 128: half * 512 + (j + 1) * 128],
                         p_sb[:, pbase + kt * 128: pbase + (kt + 1) * 128],
                         ident_bs, reads=["p_sb", "identb"], writes=["psT"], tag=f"pt{tagp}_{kt}")
            if gi >= 1:
                pg = groups[gi - 1]
                ph = (gi - 1) % 2
                p.dve(lambda e, o=pstg[:, ph * 512: ph * 512 + len(pg) * 128],
                      i_=psT[:, ph * 512: ph * 512 + len(pg) * 128]:
                      e.tensor_copy(o, i_), reads=["psT"], writes=["pstg"], tag=f"pe{tagp}_{gi}")
                for j, kt in enumerate(pg):
                    p.mm(big6[:, 3456:3584],
                         vall[:, kt * 512 + kv * 128: kt * 512 + (kv + 1) * 128],
                         pstg[:, ph * 512 + j * 128: ph * 512 + (j + 1) * 128],
                         start=(kt == 0), stop=(kt == NKT - 1),
                         reads=["vall", "pstg"], writes=["b6pv"], tag=f"pv{tagp}_{kt}")
        evac_fn(att_st[:, buf * 4 + 3: buf * 4 + 4])

    def flush_pv():
        if pend[0] is not None:
            hh0, tag0, buf0, ev0 = pend[0]
            pend[0] = None
            attn_pv(hh0, tag0, buf0, ev0)

    def attn_step(hh, qsrc, qname, tagp, evac_fn):
        buf = nstep[0] % 2
        nstep[0] += 1
        attn_scores(hh, qsrc, qname, tagp, buf)
        flush_pv()
        pend[0] = (hh, tagp, buf, evac_fn)

    for hh in range(H):
        qcb = (hh % 2) * TL
        p.dma("sp", qcur[:, qcb:qcb + SK], qsT_dram[hh * 128:(hh + 1) * 128, :],
              reads=["qsT_dram"], writes=["qcur"], tag=f"qsl{hh}")
        for sq in range(5):
            def evac(rc, hh=hh, sq=sq):
                p.dve(lambda e, o=pstg[:, 0:128]:
                      e.tensor_scalar_mul(o, big6[:, 3456:3584], rc),
                      reads=["b6pv", "attst"], writes=["pstg"], tag=f"ske{hh}_{sq}")
                p.tr(psT[:, 896:1024], pstg[:, 0:128], ident_bs,
                     reads=["pstg", "identb"], writes=["psT"], tag=f"str{hh}_{sq}")
                p.dve(lambda e, o=pstg[:, 128:256], i_=psT[:, 896:1024]:
                      e.tensor_copy(o, i_), reads=["psT"], writes=["pstg"], tag=f"ske2{hh}_{sq}")
                p.dma("sp", osk_dram[sq * 128:(sq + 1) * 128, hh * 128:(hh + 1) * 128],
                      pstg[:, 128:256], reads=["pstg"], writes=["osk_dram"], tag=f"osks{hh}_{sq}")
            attn_step(hh, qcur[:, qcb + sq * 128: qcb + (sq + 1) * 128], "qcur",
                      f"s{hh}_{sq}", evac)
    if debug:
        p.dma("sp", d_osk, osk_dram, reads=["osk_dram"], tag="dosk")

    exp16 = AB_.alloc("exp16", TL)
    det16 = AB_.alloc("det16", TL)
    osk_h = AB_.alloc("osk_h", 2 * 5 * 128)
    for hh in range(H):
        qcb = (hh % 2) * TL
        okb = (hh % 2) * 640
        p.dma("sp", qcur[:, qcb:qcb + TL], qrT[hh * 128:(hh + 1) * 128, :],
              reads=["qrT_dram"], writes=["qcur"], tag=f"qld{hh}")
        p.dma("sp", osk_h[:, okb:okb + 640].rearrange("p (a c) -> p a c", a=5),
              osk_dram.rearrange("(a p) c -> p a c", p=128)[:, :, hh * 128:(hh + 1) * 128],
              reads=["osk_dram"], writes=["osk_h"], tag=f"oskl{hh}")
        for half in range(2):
            build_ploc(half, f"{hh}_{half}")
            for sp in range(5):
                p.mm(big6[:, 2048 + half * 512: 2048 + (half + 1) * 512],
                     osk_h[:, okb + sp * 128: okb + (sp + 1) * 128],
                     ploc[:, sp * 512:(sp + 1) * 512],
                     start=(sp == 0), stop=(sp == 4),
                     reads=["osk_h", "ploc"], writes=["b6s"], tag=f"ex{hh}_{half}_{sp}")
            p.dve(lambda e, o=exp16[:, half * 512:(half + 1) * 512],
                  i_=big6[:, 2048 + half * 512: 2048 + (half + 1) * 512]:
                  e.tensor_copy(o, i_), reads=["b6s"], writes=["exp16"], tag=f"exe{hh}_{half}")
        for qt in range(NT):
            def evac(rc, hh=hh, qt=qt):
                p.dve(lambda e, o=det16[:, qt * 128:(qt + 1) * 128]:
                      e.tensor_scalar_mul(o, big6[:, 3456:3584], rc),
                      reads=["b6pv", "attst"], writes=["det16"],
                      tag=f"dte{hh}_{qt}")
            attn_step(hh, qcur[:, qcb + qt * 128: qcb + (qt + 1) * 128], "qcur",
                      f"d{hh}_{qt}", evac)
        flush_pv()
        p.dve(lambda e: e.tensor_sub(det16, det16, exp16),
              reads=["det16", "exp16"], writes=["det16"], tag=f"bl1_{hh}")
        p.dve(lambda e: e.tensor_mul(det16, det16, maskb),
              reads=["det16", "maskb"], writes=["det16"], tag=f"bl2_{hh}")
        p.dve(lambda e: e.tensor_add(det16, det16, exp16),
              reads=["det16", "exp16"], writes=["det16"], tag=f"bl3_{hh}")
        p.dma("sp", fullT_dram[hh * 128:(hh + 1) * 128, :], det16,
              reads=["det16"], writes=["fullT_dram"], tag=f"fst{hh}")
    p.alias("big6", ["b6s", "b6pv"])

    AB_.pop(m_attn)
    AF_.pop(m_attnf)
    m_wo = AB_.mark()
    fullT_sb = AB_.alloc("fullT_sb", 16 * TL)
    p.dma("sp", fullT_sb.rearrange("p (a t) -> p a t", a=16),
          fullT_dram.rearrange("(a p) t -> p a t", p=128),
          reads=["fullT_dram"], writes=["fullT_sb"], tag="flld")
    m_x1 = AF_.mark()
    x1 = AF_.alloc("x1", NT * DM)
    for tt in range(NT):
        p.dma("sp", x1[:, tt * DM:(tt + 1) * DM], xin[tt * 128:(tt + 1) * 128, :],
              writes=["x1"], tag=f"xld{tt}")
    wo_q = AB_.alloc("wo_q", 16 * 512)
    for chq in range(4):
        p.dma("pool", wo_q.rearrange("p (a o) -> p a o", a=16),
              wo.rearrange("(a p) o -> p a o", p=128)[:, :, chq * 512:(chq + 1) * 512],
              writes=["wo_q"], tag=f"wold{chq}")
        for tt in range(NT):
            for dt_i in range(16):
                p.mm(ps1[:, 0:512],
                     fullT_sb[:, dt_i * TL + tt * 128: dt_i * TL + (tt + 1) * 128],
                     wo_q[:, dt_i * 512:(dt_i + 1) * 512],
                     start=(dt_i == 0), stop=(dt_i == 15),
                     reads=["fullT_sb", "wo_q"], writes=["big6"], tag=f"wom{chq}_{tt}")
            p.dve(lambda e, o=x1[:, tt * DM + chq * 512: tt * DM + (chq + 1) * 512]:
                  e.tensor_add(o, o, ps1[:, 0:512]),
                  reads=["big6", "x1"], writes=["x1"], tag=f"woe{chq}_{tt}")
    if debug:
        for tt in range(NT):
            p.dma("sp", d_x1[tt * 128:(tt + 1) * 128, :], x1[:, tt * DM:(tt + 1) * DM],
                  reads=["x1"], tag=f"dx1{tt}")
    AB_.pop(m_wo)

    h2t = AB_.alloc("h2t", DM)
    gwT_sb = AB_.alloc("gwT", 16 * NE)
    s1 = stat[:, 0:1]; s2 = stat[:, 1:2]; m_t = stat[:, 2:3]; r_t = stat[:, 3:4]
    m_ffn = AB_.mark()
    hT2 = AB_.alloc("hT2", 16 * TL)
    p.dma("pool", gwT_sb.rearrange("p (a o) -> p a o", a=16),
          gate_wT.rearrange("(a p) o -> p a o", p=128), writes=["gwT"], tag="ldgw")

    def ln_tile_ops(xap, bx, tag):
        p.act(lambda e: e.activation(dump, xap, AF.Square, accum_out=s2, bias=cst[:, 0:1]),
              reads=[bx, "cst"], writes=["dump", "stat"], tag=f"{tag}sq")
        p.dve(lambda e: e.reduce_sum(s1, xap, axis=AX), reads=[bx], writes=["stat"], tag=f"{tag}s1")
        p.dve(lambda e: e.tensor_scalar_mul(m_t, s1, 1.0 / DM), reads=["stat"], writes=["stat"], tag=f"{tag}m")
        p.dve(lambda e: e.tensor_scalar_mul(s2, s2, 1.0 / DM), reads=["stat"], writes=["stat"], tag=f"{tag}v1")
        p.dve(lambda e: e.tensor_tensor(s1, m_t, m_t, ALU.mult), reads=["stat"], writes=["stat"], tag=f"{tag}v2")
        p.dve(lambda e: e.tensor_sub(s2, s2, s1), reads=["stat"], writes=["stat"], tag=f"{tag}v3")
        p.act(lambda e: e.activation(s2, s2, AF.Sqrt, bias=cst[:, 1:2]),
              reads=["stat", "cst"], writes=["stat"], tag=f"{tag}sd")
        p.dve(lambda e: e.reciprocal(r_t, s2), reads=["stat"], writes=["stat"], tag=f"{tag}r")

    for tt in range(NT):
        xap = x1[:, tt * DM:(tt + 1) * DM]
        ln_tile_ops(xap, "x1", f"ln2_{tt}")
        p.dve(lambda e, x=xap: e.tensor_scalar(h2t, x, m_t, r_t, ALU.subtract, ALU.mult),
              reads=["x1", "stat"], writes=["h2t"], tag=f"ln2n{tt}")
        for g in range(2):
            for d8 in range(8):
                dt_i = g * 8 + d8
                p.tr(psT[:, d8 * 128:(d8 + 1) * 128], h2t[:, dt_i * 128:(dt_i + 1) * 128],
                     ident_bs, reads=["h2t", "identb"], writes=["psT"], tag=f"h2tr{tt}")
            dst = hT2.rearrange("p (a t) -> p a t", a=16)[:, g * 8:(g + 1) * 8,
                                                         tt * 128:(tt + 1) * 128]
            p.dve(lambda e, o=dst, i_=psT[:, 0:1024].rearrange("p (a t) -> p a t", a=8):
                  e.tensor_copy(o, i_), reads=["psT"], writes=["hT2"], tag=f"h2ev{tt}")
        for dt_i in range(16):
            p.mm(ps1[:, 0:NE], hT2.rearrange("p (a t) -> p a t", a=16)[:, dt_i,
                                                                      tt * 128:(tt + 1) * 128],
                 gwT_sb[:, dt_i * NE:(dt_i + 1) * NE],
                 start=(dt_i == 0), stop=(dt_i == 15),
                 reads=["hT2", "gwT"], writes=["big6"], tag=f"rt{tt}")
        p.dve(lambda e: e.tensor_copy(rl_f[:, 0:NE], ps1[:, 0:NE]),
              reads=["big6"], writes=["rl"], tag=f"rte{tt}")
        p.dve(lambda e: e.reduce_max(rl_f[:, 8:9], rl_f[:, 0:NE], axis=AX),
              reads=["rl"], writes=["rl"], tag=f"rm1{tt}")
        p.dve(lambda e: e.tensor_scalar(dump[:, 0:NE], rl_f[:, 0:NE], rl_f[:, 8:9], None,
                                        ALU.is_equal),
              reads=["rl"], writes=["dump"], tag=f"re1{tt}")
        p.dve(lambda e: e.scalar_tensor_tensor(dump[:, 8:16], dump[:, 0:NE], -1e9,
                                               rl_f[:, 0:NE], ALU.mult, ALU.add),
              reads=["dump", "rl"], writes=["dump"], tag=f"rl2{tt}")
        p.dve(lambda e: e.reduce_max(rl_f[:, 9:10], dump[:, 8:16], axis=AX),
              reads=["dump"], writes=["rl"], tag=f"rm2{tt}")
        p.dve(lambda e: e.tensor_scalar(dump[:, 16:24], dump[:, 8:16], rl_f[:, 9:10], None,
                                        ALU.is_equal),
              reads=["dump", "rl"], writes=["dump"], tag=f"re2{tt}")
        p.dve(lambda e: e.tensor_tensor(rl_f[:, 10:11], rl_f[:, 8:9], rl_f[:, 9:10], ALU.subtract),
              reads=["rl"], writes=["rl"], tag=f"rd{tt}")
        p.act(lambda e: e.activation(rl_f[:, 11:12], rl_f[:, 10:11], AF.Sigmoid, bias=cst[:, 0:1]),
              reads=["rl", "cst"], writes=["rl"], tag=f"rp1{tt}")
        p.dve(lambda e: e.tensor_scalar(rl_f[:, 12:13], rl_f[:, 11:12], -1.0, 1.0,
                                        ALU.mult, ALU.add),
              reads=["rl"], writes=["rl"], tag=f"rp2{tt}")
        p.dve(lambda e, o=tw_all[:, tt * NE:(tt + 1) * NE]:
              e.tensor_scalar_mul(o, dump[:, 0:NE], rl_f[:, 11:12]),
              reads=["dump", "rl"], writes=["tw"], tag=f"tw1{tt}")
        p.dve(lambda e: e.tensor_scalar_mul(dump[:, 16:24], dump[:, 16:24], rl_f[:, 12:13]),
              reads=["dump", "rl"], writes=["dump"], tag=f"tw2{tt}")
        p.dve(lambda e, o=tw_all[:, tt * NE:(tt + 1) * NE]:
              e.tensor_add(o, o, dump[:, 16:24]),
              reads=["tw", "dump"], writes=["tw"], tag=f"tw3{tt}")
    if debug:
        p.dma("sp", d_tw.rearrange("(t p) e -> p t e", p=128),
              tw_all.rearrange("p (t e) -> p t e", t=NT), reads=["tw"], tag="dtw")

    m_ffnw = AB_.mark()
    wsl = [AB_.alloc(f"wsl{i}", 2 * 16 * 128) for i in range(2)]
    hTf = AB_.alloc("hTf", 2 * 11 * 512)
    sil = AB_.alloc("sil", 512)
    for e_i in range(NE):
        for ft in range(11):
            sl = wsl[ft % 2]
            sn = f"wsl{ft % 2}"
            p.dma("pool", sl[:, 0:2048].rearrange("p (a f) -> p a f", a=16),
                  gateT[e_i].rearrange("(a p) f -> p a f", p=128)[:, :, ft * 128:(ft + 1) * 128],
                  writes=[sn], tag=f"fg{e_i}_{ft}")
            p.dma("pool", sl[:, 2048:4096].rearrange("p (a f) -> p a f", a=16),
                  upT[e_i].rearrange("(a p) f -> p a f", p=128)[:, :, ft * 128:(ft + 1) * 128],
                  writes=[sn], tag=f"fu{e_i}_{ft}")
            for ch in range(2):
                for dt_i in range(16):
                    p.mm(big6[:, ch * 1024: ch * 1024 + 512],
                         sl[:, dt_i * 128:(dt_i + 1) * 128],
                         hT2[:, dt_i * TL + ch * 512: dt_i * TL + (ch + 1) * 512],
                         start=(dt_i == 0), stop=(dt_i == 15),
                         reads=[sn, "hT2"], writes=["big6"], tag=f"gm{e_i}_{ch}_{ft}")
                for dt_i in range(16):
                    p.mm(big6[:, ch * 1024 + 512: ch * 1024 + 1024],
                         sl[:, 2048 + dt_i * 128: 2048 + (dt_i + 1) * 128],
                         hT2[:, dt_i * TL + ch * 512: dt_i * TL + (ch + 1) * 512],
                         start=(dt_i == 0), stop=(dt_i == 15),
                         reads=[sn, "hT2"], writes=["big6"], tag=f"um{e_i}_{ch}_{ft}")
                p.act(lambda e, ch=ch: e.activation(sil, big6[:, ch * 1024: ch * 1024 + 512],
                                                    AF.Silu, bias=cst[:, 0:1]),
                      reads=["big6", "cst"], writes=["sil"], tag=f"si{e_i}_{ch}_{ft}")
                p.dve(lambda e, o=hTf[:, ch * 5632 + ft * 512: ch * 5632 + (ft + 1) * 512],
                      i_=big6[:, ch * 1024 + 512: ch * 1024 + 1024]:
                      e.tensor_tensor(o, i_, sil, ALU.mult),
                      reads=["big6", "sil"], writes=["hTf"], tag=f"hm{e_i}_{ch}_{ft}")
        for dp in range(8):
            sl = wsl[dp % 2]
            sn = f"wsl{dp % 2}"
            p.dma("pool", sl[:, 0:11 * 256].rearrange("p (a f) -> p a f", a=11),
                  downT[e_i].rearrange("(a p) o -> p a o", p=128)[:, :, dp * 256:(dp + 1) * 256],
                  writes=[sn], tag=f"fd{e_i}_{dp}")
            for ch in range(2):
                for do in range(2):
                    for ft in range(11):
                        p.mm(big6[:, (ch * 2 + do) * 512:(ch * 2 + do + 1) * 512],
                             sl[:, ft * 256 + do * 128: ft * 256 + (do + 1) * 128],
                             hTf[:, ch * 5632 + ft * 512: ch * 5632 + (ft + 1) * 512],
                             start=(ft == 0), stop=(ft == 10),
                             reads=[sn, "hTf"], writes=["big6"], tag=f"dm{e_i}_{ch}_{dp}_{do}")
            for ch in range(2):
                for j in range(4):
                    tt = ch * 4 + j
                    for do in range(2):
                        p.dve(lambda e, o=sil[:, 0:128],
                              i_=big6[:, (ch * 2 + do) * 512 + j * 128:
                                      (ch * 2 + do) * 512 + (j + 1) * 128]:
                              e.tensor_copy(o, i_), reads=["big6"], writes=["sil"],
                              tag=f"de{e_i}_{ch}_{dp}_{do}_{j}")
                        p.tr(psT[:, do * 128:(do + 1) * 128], sil[:, 0:128], ident_bs,
                             reads=["sil", "identb"], writes=["psT"], tag=f"dt{e_i}_{ch}_{dp}_{do}_{j}")
                    p.dve(lambda e, o=x1[:, tt * DM + dp * 256: tt * DM + (dp + 1) * 256],
                          i_=psT[:, 0:256],
                          sci=tw_all[:, tt * NE + e_i: tt * NE + e_i + 1]:
                          e.scalar_tensor_tensor(o, i_, sci, o, ALU.mult, ALU.add),
                          reads=["psT", "tw", "x1"], writes=["x1"], tag=f"da{e_i}_{ch}_{dp}_{j}")
    for tt in range(NT):
        p.dma("sp", o_x2[tt * 128:(tt + 1) * 128, :], x1[:, tt * DM:(tt + 1) * DM],
              reads=["x1"], writes=["x2_dram"], tag=f"sx2{tt}")
    AB_.pop(m_ffnw)

    AB_.pop(m_ffn)
    m_h3 = AB_.mark()
    h3T = AB_.alloc("h3T", 16 * TL)
    for tt in range(NT):
        xap = x1[:, tt * DM:(tt + 1) * DM]
        ln_tile_ops(xap, "x1", f"ln3_{tt}")
        p.dve(lambda e, x=xap: e.tensor_scalar(h2t, x, m_t, r_t, ALU.subtract, ALU.mult),
              reads=["x1", "stat"], writes=["h2t"], tag=f"ln3n{tt}")
        for g in range(2):
            for d8 in range(8):
                dt_i = g * 8 + d8
                p.tr(psT[:, d8 * 128:(d8 + 1) * 128], h2t[:, dt_i * 128:(dt_i + 1) * 128],
                     ident_bs, reads=["h2t", "identb"], writes=["psT"], tag=f"h3tr{tt}")
            dst = h3T.rearrange("p (a t) -> p a t", a=16)[:, g * 8:(g + 1) * 8,
                                                         tt * 128:(tt + 1) * 128]
            p.dve(lambda e, o=dst, i_=psT[:, 0:1024].rearrange("p (a t) -> p a t", a=8):
                  e.tensor_copy(o, i_), reads=["psT"], writes=["h3T"], tag=f"h3ev{tt}")
    p.dma("sp", o_h3T.rearrange("(a p) t -> p a t", p=128),
          h3T.rearrange("p (a t) -> p a t", a=16), reads=["h3T"], writes=["h3T_dram"], tag="sh3")
    AB_.pop(m_h3)
    AF_.pop(m_x1)

    m_b8a = AB_.mark()
    h3_cur = AB_.alloc("h3c", 16 * 128)
    Wmb_h = AB_.alloc("Wmb", 16 * 1024)
    bias_r = AB_.alloc("biasr", 3072)
    p.dma("pool", bias_r[0:1, 0:512], bk_r, writes=["biasr"], tag="ldbk")
    p.dma("pool", bias_r[0:1, 512:1024], bv_r, writes=["biasr"], tag="ldbv")
    p.dma("pool", bias_r[0:1, 1024:1024 + DM], bmb_r, writes=["biasr"], tag="ldbmb")
    mb16 = AB_.alloc("mb16", 1024)
    m_dbg = AF_.mark()
    mbz_dbg = AF_.alloc("mbz", 1024) if debug else None
    for mh in range(2):
        p.dma("pool", Wmb_h.rearrange("p (a o) -> p a o", a=16),
              WmbT.rearrange("(a p) o -> p a o", p=128)[:, :, mh * 1024:(mh + 1) * 1024],
              writes=["Wmb"], tag=f"ldmb{mh}")
        if debug and mh == 0:
            p.dma("sp", d_wmbh, Wmb_h[:, 0:2048], reads=["Wmb"], tag="dwmbh_early")
        for tt in range(NT):
            p.dma("sp", h3_cur.rearrange("p (a t) -> p a t", a=16),
                  o_h3T.rearrange("(a p) t -> p a t", p=128)[:, :, tt * 128:(tt + 1) * 128],
                  reads=["h3T_dram"], writes=["h3c"], tag=f"h3l{mh}_{tt}")
            for chm in range(2):
                for dt_i in range(16):
                    p.mm(big6[:, chm * 512:(chm + 1) * 512],
                         h3_cur[:, dt_i * 128:(dt_i + 1) * 128],
                         Wmb_h[:, dt_i * 1024 + chm * 512: dt_i * 1024 + (chm + 1) * 512],
                         start=(dt_i == 0), stop=False,
                         reads=["h3c", "Wmb"], writes=["big6"], tag=f"mb{mh}_{tt}_{chm}")
                p.mm(big6[:, chm * 512:(chm + 1) * 512],
                     onesr_sb[0:1, :], bias_r[0:1, 1024 + mh * 1024 + chm * 512:
                                              1024 + mh * 1024 + (chm + 1) * 512],
                     start=False, stop=True, reads=["onesr", "biasr"], writes=["big6"],
                     tag=f"mbb{mh}_{tt}_{chm}")
            if debug and mh == 0 and tt == 0:
                p.dve(lambda e: e.tensor_copy(mbz_dbg, big6[:, 0:1024]),
                      reads=["big6"], writes=["mbz"], tag="mbzc")
                p.dma("sp", d_mbz, mbz_dbg, reads=["mbz"], tag="dmbz")
            p.act(lambda e, o=mb16: e.activation(o, big6[:, 0:1024], AF.Sigmoid, bias=cst[:, 0:1]),
                  reads=["big6", "cst"], writes=["mb16"], tag=f"mbs{mh}_{tt}")
            if debug and mh == 0 and tt == 0:
                p.dma("sp", d_h3b, h3_cur, reads=["h3c"], tag="dh3b")
                p.dma("sp", d_mb0, mb16, reads=["mb16"], tag="dmb0")
                p.dve(lambda e: e.tensor_copy(pen_f[0:1, 0:1], pen_f[0:1, 0:1]),
                      reads=["pen_f"], writes=["pen_f"], tag="nop_dbg")
                p.dve(lambda e: e.tensor_copy(dump2[:, 0:1] if False else mb16[:, 0:1], mb16[:, 0:1]),
                      reads=["mb16"], writes=["mb16"], tag="nop_dbg2")
            p.dma("sp", mb_dram[tt * 128:(tt + 1) * 128, mh * 1024:(mh + 1) * 1024],
                  mb16, reads=["mb16"], writes=["mb_dram"], tag=f"mbst{mh}_{tt}")
    AB_.pop(m_b8a)
    AF_.pop(m_dbg)

    m_b8f = AF_.mark()
    m_b8b = AB_.mark()
    h3_cur = AB_.alloc("h3c", 16 * 128)
    WkT_sb = AB_.alloc("WkT", 16 * 512)
    WvT_sb = AB_.alloc("WvT", 16 * 512)
    bias_r = AB_.alloc("biasr2", 1024)
    p.dma("pool", WkT_sb.rearrange("p (a o) -> p a o", a=16),
          WkT.rearrange("(a p) o -> p a o", p=128), writes=["WkT"], tag="ldWk")
    p.dma("pool", WvT_sb.rearrange("p (a o) -> p a o", a=16),
          WvT.rearrange("(a p) o -> p a o", p=128), writes=["WvT"], tag="ldWv")
    p.dma("pool", bias_r[0:1, 0:512], bk_r, writes=["biasr2"], tag="ldbk2")
    p.dma("pool", bias_r[0:1, 512:1024], bv_r, writes=["biasr2"], tag="ldbv2")
    mem_sb = AB_.alloc("mem", H * 6 * 128)
    p.dma("pool", mem_sb.rearrange("p (h f d) -> p h f d", h=H, f=6),
          memory.rearrange("h (f p) d -> p h f d", p=128), writes=["mem"], tag="ldmem")
    nT_sb = AB_.alloc("nT", 6 * H)
    p.dma("pool", nT_sb.rearrange("p (a h) -> p a h", a=6),
          normT.rearrange("(a p) h -> p a h", p=128), writes=["nT"], tag="ldnT")
    dmem_acc = AF_.alloc("dmem_acc", H * 6 * 128)
    dnrm_acc = AF_.alloc("dnrm_acc", 6 * H)
    p.dve(lambda e: e.memset(dmem_acc, 0.0), writes=["dmem_acc"], tag="z1")
    p.dve(lambda e: e.memset(dnrm_acc, 0.0), writes=["dnrm_acc"], tag="z2")
    dump2 = AF_.alloc("dump2", FD)
    xc_f = AF_.alloc("xc", KVH * 256)
    mk_f = AF_.alloc("mk", KVH * FD)
    mkst = AF_.alloc("mkst", 16)
    mk16 = AB_.alloc("mk16", KVH * FD)
    mkT16 = AB_.alloc("mkT", KVH * FD)
    vv_f = AF_.alloc("vv", KVH * 128)
    den_f = AF_.alloc("den", 2 * H)
    wmv16 = AB_.alloc("wmv", DM)
    mbt16 = AB_.alloc("mbt16", DM)
    num_f = AF_.alloc("num", DM)
    coef16 = AB_.alloc("coef", 16)

    for tt in range(NT):
        p.dma("sp", h3_cur.rearrange("p (a t) -> p a t", a=16),
              o_h3T.rearrange("(a p) t -> p a t", p=128)[:, :, tt * 128:(tt + 1) * 128],
              reads=["h3T_dram"], writes=["h3c"], tag=f"h3l2_{tt}")
        p.dma("sp", mbt16, mb_dram[tt * 128:(tt + 1) * 128, :],
              reads=["mb_dram"], writes=["mbt16"], tag=f"mbl{tt}")
        hsl = lambda dt_i: h3_cur[:, dt_i * 128:(dt_i + 1) * 128]
        for dt_i in range(16):
            p.mm(big6[:, 0:512], hsl(dt_i), WkT_sb[:, dt_i * 512:(dt_i + 1) * 512],
                 start=(dt_i == 0), stop=False,
                 reads=["h3c", "WkT"], writes=["big6"], tag=f"bk{tt}")
        p.mm(big6[:, 0:512], onesr_sb[0:1, :], bias_r[0:1, 0:512], start=False, stop=True,
             reads=["onesr", "biasr2"], writes=["big6"], tag=f"bkb{tt}")
        for dt_i in range(16):
            p.mm(big6[:, 512:1024], hsl(dt_i), WvT_sb[:, dt_i * 512:(dt_i + 1) * 512],
                 start=(dt_i == 0), stop=False,
                 reads=["h3c", "WvT"], writes=["big6"], tag=f"bv{tt}")
        p.mm(big6[:, 512:1024], onesr_sb[0:1, :], bias_r[0:1, 512:1024],
             start=False, stop=True, reads=["onesr", "biasr2"], writes=["big6"], tag=f"bvb{tt}")
        x3 = xc_f.rearrange("p (kv c) -> p kv c", kv=KVH)
        k3 = big6[:, 0:512].rearrange("p (kv d) -> p kv d", kv=KVH)
        p.act(lambda e: e.activation(x3[:, :, 0:128], k3, AF.Relu, bias=cst[:, 0:1]),
              reads=["big6", "cst"], writes=["xc"], tag=f"rp{tt}")
        p.act(lambda e: e.activation(x3[:, :, 128:256], k3, AF.Relu, bias=cst[:, 0:1], scale=-1.0),
              reads=["big6", "cst"], writes=["xc"], tag=f"rn{tt}")
        p.dve(lambda e: e.tensor_copy(vv_f, big6[:, 512:1024]),
              reads=["big6"], writes=["vv"], tag=f"ve{tt}")
        mm3 = mk_f.rearrange("p (kv f) -> p kv f", kv=KVH)
        for c_i in range(3):
            sh = c_i + 1
            p.dve(lambda e, o=mm3[:, :, c_i * 256 + sh: c_i * 256 + 256],
                  a=x3[:, :, sh:256], b=x3[:, :, 0:256 - sh]:
                  e.tensor_tensor(o, a, b, ALU.mult),
                  reads=["xc"], writes=["mk"], tag=f"mka{tt}_{c_i}")
            p.dve(lambda e, o=mm3[:, :, c_i * 256: c_i * 256 + sh],
                  a=x3[:, :, 0:sh], b=x3[:, :, 256 - sh:256]:
                  e.tensor_tensor(o, a, b, ALU.mult),
                  reads=["xc"], writes=["mk"], tag=f"mkb{tt}_{c_i}")
        for kv in range(KVH):
            p.act(lambda e, i_=mm3[:, kv, :], a=mkst[:, kv:kv + 1]:
                  e.activation(dump2[:, 0:FD], i_, AF.Square, accum_out=a, bias=cst[:, 0:1]),
                  reads=["mk", "cst"], writes=["dump2", "mkst"], tag=f"mkq{tt}_{kv}")
        p.act(lambda e: e.activation(mkst[:, 4:8], mkst[:, 0:4], AF.Sqrt, bias=cst[:, 0:1]),
              reads=["mkst", "cst"], writes=["mkst"], tag=f"mks{tt}")
        p.dve(lambda e: e.tensor_scalar_max(mkst[:, 4:8], mkst[:, 4:8], 1e-12),
              reads=["mkst"], writes=["mkst"], tag=f"mkm{tt}")
        p.dve(lambda e: e.reciprocal(mkst[:, 8:12], mkst[:, 4:8]),
              reads=["mkst"], writes=["mkst"], tag=f"mkr{tt}")
        for kv in range(KVH):
            p.dve(lambda e, o=mm3[:, kv, :], sci=mkst[:, 8 + kv:9 + kv]:
                  e.tensor_scalar_mul(o, o, sci),
                  reads=["mk", "mkst"], writes=["mk"], tag=f"mkn{tt}_{kv}")
        p.dve(lambda e: e.tensor_copy(mk16, mk_f), reads=["mk"], writes=["mk16"], tag=f"mkc{tt}")
        for kv in range(KVH):
            p.act(lambda e, i_=mm3[:, kv, :], a=mkst[:, 12 + kv:13 + kv]:
                  e.activation(dump2[:, 0:FD], i_, AF.Square, accum_out=a, bias=cst[:, 0:1]),
                  reads=["dump2", "mk", "cst"], writes=["dump2", "mkst"], tag=f"msq{tt}_{kv}")
        for grp in range(3):
            for j in range(8):
                idx = grp * 8 + j
                p.tr(psT[:, j * 128:(j + 1) * 128],
                     mk16[:, idx * 128:(idx + 1) * 128], ident_bs,
                     reads=["mk16", "identb"], writes=["psT"], tag=f"mkt{tt}_{grp}_{j}")
            p.dve(lambda e, o=mkT16[:, grp * 1024:(grp + 1) * 1024], i_=psT[:, 0:1024]:
                  e.tensor_copy(o, i_), reads=["psT"], writes=["mkT"], tag=f"mke{tt}_{grp}")
        for hh in range(H):
            kv = hh // 4
            for fc in range(6):
                p.mm(big6[:, hh * 128:(hh + 1) * 128],
                     mkT16[:, kv * FD + fc * 128: kv * FD + (fc + 1) * 128],
                     mem_sb[:, (hh * 6 + fc) * 128:(hh * 6 + fc + 1) * 128],
                     start=(fc == 0), stop=(fc == 5),
                     reads=["mkT", "mem"], writes=["big6"], tag=f"nm{tt}_{hh}")
        for kv in range(KVH):
            for fc in range(6):
                p.mm(ps1[:, kv * 4:(kv + 1) * 4],
                     mkT16[:, kv * FD + fc * 128: kv * FD + (fc + 1) * 128],
                     nT_sb[:, fc * H + kv * 4: fc * H + (kv + 1) * 4],
                     start=(fc == 0), stop=(fc == 5),
                     reads=["mkT", "nT"], writes=["big6"], tag=f"dnm{tt}_{kv}")
        p.dve(lambda e: e.tensor_scalar(den_f[:, 0:H], ps1[:, 0:H], 1e-8, None, ALU.add),
              reads=["big6"], writes=["den"], tag=f"dne{tt}")
        p.dve(lambda e: e.reciprocal(den_f[:, H:2 * H], den_f[:, 0:H]),
              reads=["den"], writes=["den"], tag=f"dnr{tt}")
        p.dve(lambda e: e.tensor_copy(num_f, big6[:, 0:2048]),
              reads=["big6"], writes=["num"], tag=f"nme{tt}")
        if debug and tt == 0:
            p.dma("sp", d_num, num_f, reads=["num"], tag="dnum")
            p.dma("sp", d_den, den_f, reads=["den"], tag="dden")
            p.dma("sp", d_mk16, mk16, reads=["mk16"], tag="dmk16")
        for hh in range(H):
            kv = hh // 4
            p.dve(lambda e, o=num_f[:, hh * 128:(hh + 1) * 128], sci=den_f[:, H + hh:H + hh + 1]:
                  e.tensor_scalar_mul(o, o, sci), reads=["num", "den"], writes=["num"],
                  tag=f"pvv{tt}_{hh}")
            p.dve(lambda e, o=num_f[:, hh * 128:(hh + 1) * 128],
                  a=vv_f[:, kv * 128:(kv + 1) * 128]:
                  e.tensor_sub(o, a, o), reads=["num", "vv"], writes=["num"], tag=f"mv{tt}_{hh}")
        p.dve(lambda e: e.tensor_tensor(wmv16, num_f, mbt16, ALU.mult),
              reads=["num", "mbt16"], writes=["wmv"], tag=f"wmv{tt}")
        if debug and tt == 0:
            p.dma("sp", d_wmv, wmv16, reads=["wmv"], tag="dwmv")
        p.dve(lambda e: e.reciprocal(mkst[:, 0:4], mkst[:, 12:16]),
              reads=["mkst"], writes=["mkst"], tag=f"crc{tt}")
        for kv in range(KVH):
            p.dve(lambda e, o=den_f[:, kv * 4:(kv + 1) * 4], sci=mkst[:, 0 + kv:1 + kv]:
                  e.tensor_scalar_mul(o, o, sci),
                  reads=["den", "mkst"], writes=["den"], tag=f"cdd{tt}_{kv}")
        p.dve(lambda e: e.tensor_scalar(den_f[:, 0:H], den_f[:, 0:H], -1.0, 1.0, ALU.mult, ALU.add),
              reads=["den"], writes=["den"], tag=f"cm{tt}")
        p.dve(lambda e: e.tensor_scalar_max(den_f[:, 0:H], den_f[:, 0:H], 0.0),
              reads=["den"], writes=["den"], tag=f"cx0{tt}")
        p.dve(lambda e: e.tensor_scalar_min(den_f[:, 0:H], den_f[:, 0:H], 1.0),
              reads=["den"], writes=["den"], tag=f"cx1{tt}")
        p.dve(lambda e: e.tensor_copy(coef16[:, 0:H], den_f[:, 0:H]),
              reads=["den"], writes=["coef"], tag=f"cc2{tt}")
        for hh in range(H):
            kv = hh // 4
            for fc in range(6):
                p.mm(ps1[:, 0:128],
                     mk16[:, kv * FD + fc * 128: kv * FD + (fc + 1) * 128],
                     wmv16[:, hh * 128:(hh + 1) * 128],
                     start=True, stop=True,
                     reads=["mk16", "wmv"], writes=["big6"], tag=f"dm{tt}_{hh}_{fc}")
                p.dve(lambda e, o=dmem_acc[:, (hh * 6 + fc) * 128:(hh * 6 + fc + 1) * 128]:
                      e.tensor_add(o, o, ps1[:, 0:128]),
                      reads=["big6", "dmem_acc"], writes=["dmem_acc"], tag=f"dma{tt}_{hh}_{fc}")
        for kv in range(KVH):
            for fc in range(6):
                p.mm(ps1[:, 128:132],
                     mk16[:, kv * FD + fc * 128: kv * FD + (fc + 1) * 128],
                     coef16[:, kv * 4:(kv + 1) * 4],
                     start=True, stop=True,
                     reads=["mk16", "coef"], writes=["big6"], tag=f"dn2{tt}_{kv}_{fc}")
                p.dve(lambda e, o=dnrm_acc[:, (kv * 6 + fc) * 4:(kv * 6 + fc + 1) * 4]:
                      e.tensor_add(o, o, ps1[:, 128:132]),
                      reads=["big6", "dnrm_acc"], writes=["dnrm_acc"], tag=f"dna{tt}_{kv}_{fc}")

    if debug:
        p.dma("sp", d_mb, mb_dram, reads=["mb_dram"], tag="dmb")
    p.dma("sp", o_dmem, dmem_acc, reads=["dmem_acc"], writes=["bdm"], tag="sdm")
    p.dma("sp", o_dnorm, dnrm_acc, reads=["dnrm_acc"], writes=["bdn"], tag="sdn")
    AF_.pop(m_b8f)
    AB_.pop(m_b8b)


def build_launch_B(debug=False):
    nc = bass.Bass("TRN2", target_bir_lowering=False, debug=False)
    I = lambda n, s, d=F32: nc.dram_tensor(n, s, d, kind="ExternalInput").ap()
    O = lambda n, s, d=F32: nc.dram_tensor(n, s, d, kind="ExternalOutput").ap()

    xin = I("xin", [TL, DM])
    lgT = I("lgT", [128, GT])
    lgT_loc = I("lgT_loc", [128, NT])
    kr_all = I("kr_all", [S, KVH * Dh], BF16)
    kh_all = I("kh_all", [S, KVH * Dh], BF16)
    v_all = I("v_all", [S, KVH * Dh], BF16)
    qrT = I("qrT", [DM, TL], BF16)
    qspT = I("qspT", [NCN, DM, SK])
    wo = I("wo", [DM, DM])
    gate_wT = I("gate_wT", [DM, NE])
    gateT = I("gateT", [NE, DM, FF])
    upT = I("upT", [NE, DM, FF])
    downT = I("downT", [NE, FF, DM])
    WkT = I("WkT", [DM, KVH * Dh])
    WvT = I("WvT", [DM, KVH * Dh])
    WmbT = I("WmbT", [DM, DM])
    bk_r = I("bk_r", [1, KVH * Dh])
    bv_r = I("bv_r", [1, KVH * Dh])
    bmb_r = I("bmb_r", [1, DM])
    memory = I("memory", [H, FD, Dh])
    normT = I("normT", [FD, H])
    kbeT = I("kbeT", [256, SK])
    kab_full = I("kab_full", [32, 128, SK])
    ka_eL = I("ka_eL", [SK, 4])
    kb_e = I("kb_e", [SK, 256])
    qnw_col = I("qnw_col", [Dh, 1])
    knw_t4 = I("knw_t4", [1, KVH * Dh])
    ss_c = I("ss_c", [1, 1])
    ident_b = I("ident_b", [128, 128], BF16)
    ones_row = I("ones_row", [1, 128], BF16)
    ones_col = I("ones_col", [128, 1], BF16)
    one16 = I("one16", [1, 1], BF16)
    iota_row = I("iota_row", [128, 128])
    iota_r1 = I("iota_r1", [1, 512])
    ltri = I("ltri", [128, 128], BF16)
    utri64 = I("utri64", [64, 64], BF16)
    sel8 = I("sel8", [64, NSB], BF16)
    rep8 = I("rep8", [64, GT], BF16)

    o_x2 = O("o_x2", [TL, DM])
    o_h3T = O("o_h3T", [DM, TL], BF16)
    o_dmem = O("o_dmem", [128, H * 6 * 128])
    o_dnorm = O("o_dnorm", [128, 6 * H])

    p = Prog(nc)
    with ExitStack() as st:
        big6 = st.enter_context(nc.psum_tensor("big6", [128, 3584], F32))
        psT = st.enter_context(nc.psum_tensor("psT", [128, 1024], BF16))
        af_t = st.enter_context(nc.sbuf_tensor("af", [128, 26000], F32))
        ab_t = st.enter_context(nc.sbuf_tensor("ab", [128, 49500], BF16))
        AF_ = Arena(p, af_t, 26000)
        AB_ = Arena(p, ab_t, 49500)
        C = make_shared_consts(p, AF_, AB_, ident_b=ident_b, ones_row=ones_row,
                               ones_col=ones_col, one16=one16, ltri=ltri,
                               utri64=utri64, sel8=sel8, rep8=rep8,
                               iota_row=iota_row, iota_r1=iota_r1)
        lgT_load = lambda dst: p.dma("sp", dst, lgT, writes=["lg"], tag="ldlg")
        lgl_load = lambda dst: p.dma("sp", dst, lgT_loc, writes=["lgl"], tag="ldlgl")
        emit_phase_B(p, AF_, AB_, big6, psT, C, debug,
                     xin, kr_all, kh_all, v_all, qrT,
                     wo, gate_wT, gateT, upT, downT, WkT, WvT, WmbT, bk_r, bv_r, bmb_r,
                     memory, normT, kbeT, kab_full, ka_eL, kb_e, qnw_col, knw_t4, ss_c,
                     o_x2, o_h3T, o_dmem, o_dnorm,
                     lgT_load, lgl_load, qspT=qspT, O=O)
        p.emit()
    return nc


def host_prep_B(inputs, resA):
    g = lambda k: np.asarray(inputs[k])
    x = g("L0_attn_a_hidden_states")[0]
    logits = np.concatenate([np.asarray(resA[c]["o_logits"][0]) for c in range(NCN)])
    lgT = _c(logits.reshape(GT, 128).T)
    kr_all = np.concatenate([np.asarray(resA[c]["o_kr"]) for c in range(NCN)], 0)
    kh_all = np.concatenate([np.asarray(resA[c]["o_kh"]) for c in range(NCN)], 0)
    v_all = np.concatenate([np.asarray(resA[c]["o_v"]) for c in range(NCN)], 0)
    qspT = np.stack([_c(np.asarray(resA[c]["o_qsp"]).T) for c in range(NCN)])
    ka, kb = g("L0_attn_a_kron_a"), g("L0_attn_a_kron_b")
    ika = np.arange(SK) % 20
    ikb = np.arange(SK) // 20
    ka_expand = _c(ka.T[:, ika])
    kab_full = _c(np.broadcast_to(ka_expand[:32, None, :], (32, 128, SK)).astype(np.float32))
    kb_expandT = _c(kb[ikb].T)
    kb_e = _c(kb[ikb])
    qnw, knw = g("L0_attn_a_q_norm_w"), g("L0_attn_a_k_norm_w")
    ident = np.eye(128, dtype=BF)
    ones_row = np.ones((1, 128), BF)
    ones_col = np.ones((128, 1), BF)
    one16 = np.ones((1, 1), BF)
    iota_row = _c(np.broadcast_to(np.arange(128, dtype=np.float32)[None, :], (128, 128)))
    iota_r1 = _c(np.arange(512, dtype=np.float32).reshape(1, 512))
    k_i = np.arange(128)
    ltri = _c((k_i[:, None] <= k_i[None, :]).astype(BF))
    k64 = np.arange(64)
    utri64 = _c((k64[:, None] < k64[None, :]).astype(BF))
    tile_super = np.zeros(GT, np.int64)
    for sb, (st0, ntl, cap) in enumerate(SUP):
        tile_super[st0:st0 + ntl] = sb
    sel8 = np.zeros((64, NSB), BF)
    sel8[np.arange(GT), tile_super] = 1
    rep8 = np.zeros((64, GT), BF)
    first_tile = np.array([SUP[s][0] for s in tile_super])
    rep8[first_tile, np.arange(GT)] = 1
    maps = []
    base = {
        "lgT": lgT, "kr_all": kr_all, "kh_all": kh_all, "v_all": v_all, "qspT": qspT,
        "wo": g("L0_attn_a_wo"),
        "gate_wT": _c(g("L0_ffn_gate_w").T),
        "gateT": _c(np.swapaxes(g("L0_ffn_gate_proj"), 1, 2)),
        "upT": _c(np.swapaxes(g("L0_ffn_up_proj"), 1, 2)),
        "downT": _c(np.swapaxes(g("L0_ffn_down_proj"), 1, 2)),
        "WkT": _c(g("L0_attn_b_Wk").T), "WvT": _c(g("L0_attn_b_Wv").T),
        "WmbT": _c(g("L0_attn_b_Wmb").T),
        "bk_r": _c(g("L0_attn_b_bk").reshape(1, -1)),
        "bv_r": _c(g("L0_attn_b_bv").reshape(1, -1)),
        "bmb_r": _c(g("L0_attn_b_bmb").reshape(1, -1)),
        "memory": g("L0_attn_b_memory"), "normT": _c(g("L0_attn_b_norm").T),
        "kbeT": kb_expandT, "kab_full": kab_full, "kb_e": kb_e,
        "qnw_col": _c(qnw.reshape(Dh, 1)),
        "knw_t4": _c(np.tile(knw, KVH).reshape(1, -1)),
        "ss_c": _c(np.asarray(inputs["L0_attn_a_sketch_scale"]).reshape(1, 1)),
        "ident_b": ident, "ones_row": ones_row, "ones_col": ones_col, "one16": one16,
        "iota_row": iota_row, "iota_r1": iota_r1, "ltri": ltri, "utri64": utri64,
        "sel8": sel8, "rep8": rep8,
    }
    for c in range(NCN):
        m = dict(base)
        m["xin"] = _c(x[c * TL:(c + 1) * TL])
        m["lgT_loc"] = _c(lgT[:, c * NT:(c + 1) * NT])
        m["qrT"] = np.asarray(resA[c]["o_qrT"])
        m["ka_eL"] = _c(ka_expand[:32].T[:, 4 * c:4 * c + 4])
        maps.append(m)
    return maps


def run_B(inputs, resA, debug=False):
    nc = build_launch_B(debug=debug)
    maps = host_prep_B(inputs, resA)
    res = run_bass_kernel_spmd(nc, maps, core_ids=list(range(NCN)))
    return res.results


def emit_phase_C(p, AF_, AB_, big6, psT, C, debug,
                 x2in, h3Tin, memL, nrmL, WqT, WgT, bq_r, bg_r,
                 gate_wT, gateT, upT, downT, o_out,
                 dmp=None, dnp=None, r_dm=None, r_dn=None, O=None):
    nc = p.nc
    ps1 = big6[:, 3072:3584]
    cst = C["cst"]; ident_bs = C["ident_bs"]; onesr_sb = C["onesr_sb"]
    stat = C["stat"]; dump = C["dump"]; tw_all = C["tw_all"]; rl_f = C["rl_f"]
    mh_dram = nc.dram_tensor("mh_dram", [TL, DM], F32).ap()
    if debug:
        d_nm = O("d_nm", [128, H * 6 * 128])
        d_nnT = O("d_nnT", [128, 6 * H])
        d_mh = O("d_mh", [TL, DM])
        d_x3 = O("d_x3", [TL, DM])
        d_tw2 = O("d_tw2", [TL, NE])

    s1 = stat[:, 0:1]; s2 = stat[:, 1:2]; m_t = stat[:, 2:3]; r_t = stat[:, 3:4]

    def ln_tile_ops(xap, bx, tag):
        p.act(lambda e: e.activation(dump, xap, AF.Square, accum_out=s2, bias=cst[:, 0:1]),
              reads=[bx, "cst"], writes=["dump", "stat"], tag=f"{tag}sq")
        p.dve(lambda e: e.reduce_sum(s1, xap, axis=AX), reads=[bx], writes=["stat"], tag=f"{tag}s1")
        p.dve(lambda e: e.tensor_scalar_mul(m_t, s1, 1.0 / DM), reads=["stat"], writes=["stat"], tag=f"{tag}m")
        p.dve(lambda e: e.tensor_scalar_mul(s2, s2, 1.0 / DM), reads=["stat"], writes=["stat"], tag=f"{tag}v1")
        p.dve(lambda e: e.tensor_tensor(s1, m_t, m_t, ALU.mult), reads=["stat"], writes=["stat"], tag=f"{tag}v2")
        p.dve(lambda e: e.tensor_sub(s2, s2, s1), reads=["stat"], writes=["stat"], tag=f"{tag}v3")
        p.act(lambda e: e.activation(s2, s2, AF.Sqrt, bias=cst[:, 1:2]),
              reads=["stat", "cst"], writes=["stat"], tag=f"{tag}sd")
        p.dve(lambda e: e.reciprocal(r_t, s2), reads=["stat"], writes=["stat"], tag=f"{tag}r")

    m1f = AF_.mark()
    nm_acc = AF_.alloc("nm_acc", H * 6 * 128)
    nn_acc = AF_.alloc("nn_acc", 6 * H)
    stg = AF_.alloc("stg", 6144)
    nnstg = AF_.alloc("nnstg", 6 * H)
    nnT_f = AF_.alloc("nnT_f", 6 * H)
    p.dma("sp", nm_acc, memL, writes=["nm_acc"], tag="ldmem")
    p.dma("sp", nn_acc, nrmL, writes=["nn_acc"], tag="ldnrm")
    if dmp is not None:
        srcs = [(dmp[c], dnp[c], f"{c}", [], []) for c in range(NCN)]
    else:
        srcs = [(r_dm, r_dn, "r", ["rdm"], ["rdn"])]
    for dmsrc, dnsrc, ctag, rdm_nm, rdn_nm in srcs:
        for hh in range(2):
            p.dma("sp", stg, dmsrc[:, hh * 6144:(hh + 1) * 6144],
                  reads=rdm_nm, writes=["stg"], tag=f"lddm{ctag}_{hh}")
            p.dve(lambda e, o=nm_acc[:, hh * 6144:(hh + 1) * 6144]:
                  e.tensor_add(o, o, stg), reads=["stg", "nm_acc"], writes=["nm_acc"],
                  tag=f"adm{ctag}_{hh}")
        p.dma("sp", nnstg, dnsrc, reads=rdn_nm, writes=["nnstg"], tag=f"lddn{ctag}")
        p.dve(lambda e: e.tensor_add(nn_acc, nn_acc, nnstg),
              reads=["nnstg", "nn_acc"], writes=["nn_acc"], tag=f"adn{ctag}")
    for kv in range(4):
        p.dve(lambda e, o=nnT_f.rearrange("p (a h) -> p a h", a=6)[:, :, kv * 4:(kv + 1) * 4],
              i_=nn_acc.rearrange("p (k f j) -> p k f j", k=4, f=6)[:, kv]:
              e.tensor_copy(o, i_), reads=["nn_acc"], writes=["nnT_f"], tag=f"nnp{kv}")
    m1b = AB_.mark()
    nm16 = AB_.alloc("nm16", H * 6 * 128)
    nnT16 = AB_.alloc("nnT16", 6 * H)
    p.dve(lambda e: e.tensor_copy(nnT16, nnT_f), reads=["nnT_f"], writes=["nnT16"], tag="nnc")
    p.dve(lambda e: e.tensor_copy(nm16, nm_acc), reads=["nm_acc"], writes=["nm16"], tag="nmc")
    if debug:
        p.dma("sp", d_nm, nm_acc, reads=["nm_acc"], tag="dnm")
        p.dma("sp", d_nnT, nnT_f, reads=["nnT_f"], tag="dnn")
    AF_.pop(m1f)

    x1 = AF_.alloc("x1", NT * DM)
    p.dma("sp", x1.rearrange("p (a d) -> p a d", a=NT),
          x2in.rearrange("(a p) d -> p a d", p=128),
          reads=["x2_dram"], writes=["x1"], tag="ldx2")

    m2b = AB_.mark()
    h3c = AB_.alloc("h3c", 16 * 128)
    wqh = AB_.alloc("wqh", 16 * 1024)
    bias_r = AB_.alloc("biasr", 1024)
    mk16c = AB_.alloc("mk16", KVH * FD)
    mqT16 = AB_.alloc("mqT", KVH * FD)
    m2f = AF_.mark()
    xc_f = AF_.alloc("xc", KVH * 256)
    mq_f = AF_.alloc("mq", KVH * FD)
    mkst = AF_.alloc("mkst", 16)
    den_f = AF_.alloc("den", 16)
    mh_half = AF_.alloc("mhh", 1024)
    for hf in range(2):
        p.dma("pool", wqh.rearrange("p (a o) -> p a o", a=16),
              WqT.rearrange("(a p) o -> p a o", p=128)[:, :, hf * 1024:(hf + 1) * 1024],
              writes=["wqh"], tag=f"ldwq{hf}")
        p.dma("pool", bias_r[0:1, :], bq_r[:, hf * 1024:(hf + 1) * 1024],
              writes=["biasr"], tag=f"ldbq{hf}")
        for tt in range(NT):
            p.dma("sp", h3c.rearrange("p (a t) -> p a t", a=16),
                  h3Tin.rearrange("(a p) t -> p a t", p=128)[:, :, tt * 128:(tt + 1) * 128],
                  reads=["h3T_dram"], writes=["h3c"], tag=f"h3l{hf}_{tt}")
            for chm in range(2):
                for dt_i in range(16):
                    p.mm(big6[:, chm * 512:(chm + 1) * 512],
                         h3c[:, dt_i * 128:(dt_i + 1) * 128],
                         wqh[:, dt_i * 1024 + chm * 512: dt_i * 1024 + (chm + 1) * 512],
                         start=(dt_i == 0), stop=False,
                         reads=["h3c", "wqh"], writes=["big6"], tag=f"qm{hf}_{tt}_{chm}")
                p.mm(big6[:, chm * 512:(chm + 1) * 512], onesr_sb[0:1, :],
                     bias_r[0:1, chm * 512:(chm + 1) * 512], start=False, stop=True,
                     reads=["onesr", "biasr"], writes=["big6"], tag=f"qb{hf}_{tt}_{chm}")
            for qq in range(2):
                xcv = xc_f.rearrange("p (kv c) -> p kv c", kv=KVH)
                k3 = big6[:, qq * 512:(qq + 1) * 512].rearrange("p (kv d) -> p kv d", kv=KVH)
                p.act(lambda e, k=k3: e.activation(xcv[:, :, 0:128], k, AF.Relu, bias=cst[:, 0:1]),
                      reads=["big6", "cst"], writes=["xc"], tag=f"rp{hf}_{tt}_{qq}")
                p.act(lambda e, k=k3: e.activation(xcv[:, :, 128:256], k, AF.Relu,
                                                   bias=cst[:, 0:1], scale=-1.0),
                      reads=["big6", "cst"], writes=["xc"], tag=f"rn{hf}_{tt}_{qq}")
                mm3 = mq_f.rearrange("p (kv f) -> p kv f", kv=KVH)
                for c_i in range(3):
                    sh = c_i + 1
                    p.dve(lambda e, o=mm3[:, :, c_i * 256 + sh: c_i * 256 + 256],
                          a=xcv[:, :, sh:256], b=xcv[:, :, 0:256 - sh]:
                          e.tensor_tensor(o, a, b, ALU.mult),
                          reads=["xc"], writes=["mq"], tag=f"mka{hf}{tt}{qq}_{c_i}")
                    p.dve(lambda e, o=mm3[:, :, c_i * 256: c_i * 256 + sh],
                          a=xcv[:, :, 0:sh], b=xcv[:, :, 256 - sh:256]:
                          e.tensor_tensor(o, a, b, ALU.mult),
                          reads=["xc"], writes=["mq"], tag=f"mkb{hf}{tt}{qq}_{c_i}")
                for kv in range(KVH):
                    p.act(lambda e, i_=mm3[:, kv, :], a=mkst[:, kv:kv + 1]:
                          e.activation(dump[:, 0:FD], i_, AF.Square, accum_out=a, bias=cst[:, 0:1]),
                          reads=["mq", "cst"], writes=["dump", "mkst"], tag=f"mkq{hf}{tt}{qq}_{kv}")
                p.act(lambda e: e.activation(mkst[:, 4:8], mkst[:, 0:4], AF.Sqrt, bias=cst[:, 0:1]),
                      reads=["mkst", "cst"], writes=["mkst"], tag=f"mks{hf}{tt}{qq}")
                p.dve(lambda e: e.tensor_scalar_max(mkst[:, 4:8], mkst[:, 4:8], 1e-12),
                      reads=["mkst"], writes=["mkst"], tag=f"mkm{hf}{tt}{qq}")
                p.dve(lambda e: e.reciprocal(mkst[:, 8:12], mkst[:, 4:8]),
                      reads=["mkst"], writes=["mkst"], tag=f"mkr{hf}{tt}{qq}")
                for kv in range(KVH):
                    p.dve(lambda e, o=mm3[:, kv, :], sci=mkst[:, 8 + kv:9 + kv]:
                          e.tensor_scalar_mul(o, o, sci),
                          reads=["mq", "mkst"], writes=["mq"], tag=f"mkn{hf}{tt}{qq}_{kv}")
                p.dve(lambda e: e.tensor_copy(mk16c, mq_f), reads=["mq"], writes=["mk16"],
                      tag=f"mkc{hf}{tt}{qq}")
                for grp in range(3):
                    for j in range(8):
                        idx = grp * 8 + j
                        p.tr(psT[:, j * 128:(j + 1) * 128],
                             mk16c[:, idx * 128:(idx + 1) * 128], ident_bs,
                             reads=["mk16", "identb"], writes=["psT"], tag=f"mkt{hf}{tt}{qq}_{grp}_{j}")
                    p.dve(lambda e, o=mqT16[:, grp * 1024:(grp + 1) * 1024], i_=psT[:, 0:1024]:
                          e.tensor_copy(o, i_), reads=["psT"], writes=["mqT"], tag=f"mke{hf}{tt}{qq}_{grp}")
                for j in range(4):
                    hglob = hf * 8 + qq * 4 + j
                    for fc in range(6):
                        p.mm(big6[:, 1024 + j * 128: 1024 + (j + 1) * 128],
                             mqT16[:, j * FD + fc * 128: j * FD + (fc + 1) * 128],
                             nm16[:, (hglob * 6 + fc) * 128:(hglob * 6 + fc + 1) * 128],
                             start=(fc == 0), stop=(fc == 5),
                             reads=["mqT", "nm16"], writes=["big6"], tag=f"num{hf}{tt}{qq}_{j}")
                    for fc in range(6):
                        p.mm(ps1[:, j:j + 1],
                             mqT16[:, j * FD + fc * 128: j * FD + (fc + 1) * 128],
                             nnT16[:, fc * H + hglob: fc * H + hglob + 1],
                             start=(fc == 0), stop=(fc == 5),
                             reads=["mqT", "nnT16"], writes=["big6"], tag=f"dnc{hf}{tt}{qq}_{j}")
                p.dve(lambda e: e.tensor_scalar(den_f[:, 0:4], ps1[:, 0:4], 1e-8, None, ALU.add),
                      reads=["big6"], writes=["den"], tag=f"dne{hf}{tt}{qq}")
                p.dve(lambda e: e.reciprocal(den_f[:, 4:8], den_f[:, 0:4]),
                      reads=["den"], writes=["den"], tag=f"dnr{hf}{tt}{qq}")
                for j in range(4):
                    hloc = qq * 4 + j
                    p.dve(lambda e, o=mh_half[:, hloc * 128:(hloc + 1) * 128],
                          i_=big6[:, 1024 + j * 128:1024 + (j + 1) * 128]:
                          e.tensor_copy(o, i_), reads=["big6"], writes=["mhh"],
                          tag=f"mhe{hf}{tt}{qq}_{j}")
                    p.dve(lambda e, o=mh_half[:, hloc * 128:(hloc + 1) * 128],
                          sci=den_f[:, 4 + j:5 + j]:
                          e.tensor_scalar_mul(o, o, sci), reads=["mhh", "den"], writes=["mhh"],
                          tag=f"mhd{hf}{tt}{qq}_{j}")
            p.dma("sp", mh_dram[tt * 128:(tt + 1) * 128, hf * 1024:(hf + 1) * 1024],
                  mh_half, reads=["mhh"], writes=["mh_dram"], tag=f"smh{hf}_{tt}")
    AB_.pop(m1b)
    AF_.pop(m2f)

    m3b = AB_.mark()
    h3c = AB_.alloc("h3c", 16 * 128)
    wgh = AB_.alloc("wgh", 16 * 1024)
    bias_r = AB_.alloc("biasr2", 1024)
    m3f = AF_.mark()
    gate_f = AF_.alloc("gatef", 1024)
    mh_st = AF_.alloc("mhst", 1024)
    tmp_h = AF_.alloc("tmph", DM)
    for hf in range(2):
        p.dma("pool", wgh.rearrange("p (a o) -> p a o", a=16),
              WgT.rearrange("(a p) o -> p a o", p=128)[:, :, hf * 1024:(hf + 1) * 1024],
              writes=["wgh"], tag=f"ldwg{hf}")
        p.dma("pool", bias_r[0:1, :], bg_r[:, hf * 1024:(hf + 1) * 1024],
              writes=["biasr2"], tag=f"ldbg{hf}")
        for tt in range(NT):
            xap = x1[:, tt * DM:(tt + 1) * DM]
            p.dma("sp", h3c.rearrange("p (a t) -> p a t", a=16),
                  h3Tin.rearrange("(a p) t -> p a t", p=128)[:, :, tt * 128:(tt + 1) * 128],
                  reads=["h3T_dram"], writes=["h3c"], tag=f"h3g{hf}_{tt}")
            if hf == 0:
                ln_tile_ops(xap, "x1", f"lnc{tt}")
                p.dve(lambda e, x=xap: e.tensor_scalar(tmp_h, x, m_t, r_t, ALU.subtract, ALU.mult),
                      reads=["x1", "stat"], writes=["tmph"], tag=f"lncn{tt}")
            for chm in range(2):
                for dt_i in range(16):
                    p.mm(big6[:, chm * 512:(chm + 1) * 512],
                         h3c[:, dt_i * 128:(dt_i + 1) * 128],
                         wgh[:, dt_i * 1024 + chm * 512: dt_i * 1024 + (chm + 1) * 512],
                         start=(dt_i == 0), stop=False,
                         reads=["h3c", "wgh"], writes=["big6"], tag=f"gm{hf}_{tt}_{chm}")
                p.mm(big6[:, chm * 512:(chm + 1) * 512], onesr_sb[0:1, :],
                     bias_r[0:1, chm * 512:(chm + 1) * 512], start=False, stop=True,
                     reads=["onesr", "biasr2"], writes=["big6"], tag=f"gb{hf}_{tt}_{chm}")
            p.act(lambda e: e.activation(gate_f, big6[:, 0:1024], AF.Sigmoid, bias=cst[:, 0:1]),
                  reads=["big6", "cst"], writes=["gatef"], tag=f"gs{hf}_{tt}")
            p.dma("sp", mh_st, mh_dram[tt * 128:(tt + 1) * 128, hf * 1024:(hf + 1) * 1024],
                  reads=["mh_dram"], writes=["mhst"], tag=f"lmh{hf}_{tt}")
            p.dve(lambda e: e.tensor_tensor(gate_f, gate_f, mh_st, ALU.mult),
                  reads=["gatef", "mhst"], writes=["gatef"], tag=f"gmm{hf}_{tt}")
            if hf == 0:
                p.dve(lambda e, x=xap: e.tensor_add(x, x, tmp_h),
                      reads=["x1", "tmph"], writes=["x1"], tag=f"xh{tt}")
            p.dve(lambda e, o=x1[:, tt * DM + hf * 1024: tt * DM + (hf + 1) * 1024]:
                  e.tensor_add(o, o, gate_f), reads=["x1", "gatef"], writes=["x1"],
                  tag=f"xg{hf}_{tt}")
    if debug:
        p.dma("sp", d_mh, mh_dram, reads=["mh_dram"], tag="dmh")
        for tt in range(NT):
            p.dma("sp", d_x3[tt * 128:(tt + 1) * 128, :], x1[:, tt * DM:(tt + 1) * DM],
                  reads=["x1"], tag=f"dx3{tt}")
    AB_.pop(m3b)
    AF_.pop(m3f)

    h2t = AB_.alloc("h2t", DM)
    gwT_sb = AB_.alloc("gwT", 16 * NE)
    hT2 = AB_.alloc("hT2", 16 * TL)
    p.dma("pool", gwT_sb.rearrange("p (a o) -> p a o", a=16),
          gate_wT.rearrange("(a p) o -> p a o", p=128), writes=["gwT"], tag="ldgw")
    for tt in range(NT):
        xap = x1[:, tt * DM:(tt + 1) * DM]
        ln_tile_ops(xap, "x1", f"ln4_{tt}")
        p.dve(lambda e, x=xap: e.tensor_scalar(h2t, x, m_t, r_t, ALU.subtract, ALU.mult),
              reads=["x1", "stat"], writes=["h2t"], tag=f"ln4n{tt}")
        for g in range(2):
            for d8 in range(8):
                dt_i = g * 8 + d8
                p.tr(psT[:, d8 * 128:(d8 + 1) * 128], h2t[:, dt_i * 128:(dt_i + 1) * 128],
                     ident_bs, reads=["h2t", "identb"], writes=["psT"], tag=f"h4tr{tt}")
            dst = hT2.rearrange("p (a t) -> p a t", a=16)[:, g * 8:(g + 1) * 8,
                                                         tt * 128:(tt + 1) * 128]
            p.dve(lambda e, o=dst, i_=psT[:, 0:1024].rearrange("p (a t) -> p a t", a=8):
                  e.tensor_copy(o, i_), reads=["psT"], writes=["hT2"], tag=f"h4ev{tt}")
        for dt_i in range(16):
            p.mm(ps1[:, 0:NE], hT2.rearrange("p (a t) -> p a t", a=16)[:, dt_i,
                                                                      tt * 128:(tt + 1) * 128],
                 gwT_sb[:, dt_i * NE:(dt_i + 1) * NE],
                 start=(dt_i == 0), stop=(dt_i == 15),
                 reads=["hT2", "gwT"], writes=["big6"], tag=f"rt{tt}")
        p.dve(lambda e: e.tensor_copy(rl_f[:, 0:NE], ps1[:, 0:NE]),
              reads=["big6"], writes=["rl"], tag=f"rte{tt}")
        p.dve(lambda e: e.reduce_max(rl_f[:, 8:9], rl_f[:, 0:NE], axis=AX),
              reads=["rl"], writes=["rl"], tag=f"rm1{tt}")
        p.dve(lambda e: e.tensor_scalar(dump[:, 0:NE], rl_f[:, 0:NE], rl_f[:, 8:9], None,
                                        ALU.is_equal),
              reads=["rl"], writes=["dump"], tag=f"re1{tt}")
        p.dve(lambda e: e.scalar_tensor_tensor(dump[:, 8:16], dump[:, 0:NE], -1e9,
                                               rl_f[:, 0:NE], ALU.mult, ALU.add),
              reads=["dump", "rl"], writes=["dump"], tag=f"rl2{tt}")
        p.dve(lambda e: e.reduce_max(rl_f[:, 9:10], dump[:, 8:16], axis=AX),
              reads=["dump"], writes=["rl"], tag=f"rm2{tt}")
        p.dve(lambda e: e.tensor_scalar(dump[:, 16:24], dump[:, 8:16], rl_f[:, 9:10], None,
                                        ALU.is_equal),
              reads=["dump", "rl"], writes=["dump"], tag=f"re2{tt}")
        p.dve(lambda e: e.tensor_tensor(rl_f[:, 10:11], rl_f[:, 8:9], rl_f[:, 9:10], ALU.subtract),
              reads=["rl"], writes=["rl"], tag=f"rd{tt}")
        p.act(lambda e: e.activation(rl_f[:, 11:12], rl_f[:, 10:11], AF.Sigmoid, bias=cst[:, 0:1]),
              reads=["rl", "cst"], writes=["rl"], tag=f"rp1{tt}")
        p.dve(lambda e: e.tensor_scalar(rl_f[:, 12:13], rl_f[:, 11:12], -1.0, 1.0,
                                        ALU.mult, ALU.add),
              reads=["rl"], writes=["rl"], tag=f"rp2{tt}")
        p.dve(lambda e, o=tw_all[:, tt * NE:(tt + 1) * NE]:
              e.tensor_scalar_mul(o, dump[:, 0:NE], rl_f[:, 11:12]),
              reads=["dump", "rl"], writes=["tw"], tag=f"tw1{tt}")
        p.dve(lambda e: e.tensor_scalar_mul(dump[:, 16:24], dump[:, 16:24], rl_f[:, 12:13]),
              reads=["dump", "rl"], writes=["dump"], tag=f"tw2{tt}")
        p.dve(lambda e, o=tw_all[:, tt * NE:(tt + 1) * NE]:
              e.tensor_add(o, o, dump[:, 16:24]),
              reads=["tw", "dump"], writes=["tw"], tag=f"tw3{tt}")
    if debug:
        p.dma("sp", d_tw2.rearrange("(t p) e -> p t e", p=128),
              tw_all.rearrange("p (t e) -> p t e", t=NT), reads=["tw"], tag="dtw")

    m4b = AB_.mark()
    wsl = [AB_.alloc(f"wsl{i}", 2 * 16 * 128) for i in range(2)]
    hTf = AB_.alloc("hTf", 2 * 11 * 512)
    sil = AB_.alloc("sil", 512)
    for e_i in range(NE):
        for ft in range(11):
            sl = wsl[ft % 2]
            sn = f"wsl{ft % 2}"
            p.dma("pool", sl[:, 0:2048].rearrange("p (a f) -> p a f", a=16),
                  gateT[e_i].rearrange("(a p) f -> p a f", p=128)[:, :, ft * 128:(ft + 1) * 128],
                  writes=[sn], tag=f"fg{e_i}_{ft}")
            p.dma("pool", sl[:, 2048:4096].rearrange("p (a f) -> p a f", a=16),
                  upT[e_i].rearrange("(a p) f -> p a f", p=128)[:, :, ft * 128:(ft + 1) * 128],
                  writes=[sn], tag=f"fu{e_i}_{ft}")
            for ch in range(2):
                for dt_i in range(16):
                    p.mm(big6[:, ch * 1024: ch * 1024 + 512],
                         sl[:, dt_i * 128:(dt_i + 1) * 128],
                         hT2[:, dt_i * TL + ch * 512: dt_i * TL + (ch + 1) * 512],
                         start=(dt_i == 0), stop=(dt_i == 15),
                         reads=[sn, "hT2"], writes=["big6"], tag=f"gm{e_i}_{ch}_{ft}")
                for dt_i in range(16):
                    p.mm(big6[:, ch * 1024 + 512: ch * 1024 + 1024],
                         sl[:, 2048 + dt_i * 128: 2048 + (dt_i + 1) * 128],
                         hT2[:, dt_i * TL + ch * 512: dt_i * TL + (ch + 1) * 512],
                         start=(dt_i == 0), stop=(dt_i == 15),
                         reads=[sn, "hT2"], writes=["big6"], tag=f"um{e_i}_{ch}_{ft}")
                p.act(lambda e, ch=ch: e.activation(sil, big6[:, ch * 1024: ch * 1024 + 512],
                                                    AF.Silu, bias=cst[:, 0:1]),
                      reads=["big6", "cst"], writes=["sil"], tag=f"si{e_i}_{ch}_{ft}")
                p.dve(lambda e, o=hTf[:, ch * 5632 + ft * 512: ch * 5632 + (ft + 1) * 512],
                      i_=big6[:, ch * 1024 + 512: ch * 1024 + 1024]:
                      e.tensor_tensor(o, i_, sil, ALU.mult),
                      reads=["big6", "sil"], writes=["hTf"], tag=f"hm{e_i}_{ch}_{ft}")
        for dp in range(8):
            sl = wsl[dp % 2]
            sn = f"wsl{dp % 2}"
            p.dma("pool", sl[:, 0:11 * 256].rearrange("p (a f) -> p a f", a=11),
                  downT[e_i].rearrange("(a p) o -> p a o", p=128)[:, :, dp * 256:(dp + 1) * 256],
                  writes=[sn], tag=f"fd{e_i}_{dp}")
            for ch in range(2):
                for do in range(2):
                    for ft in range(11):
                        p.mm(big6[:, (ch * 2 + do) * 512:(ch * 2 + do + 1) * 512],
                             sl[:, ft * 256 + do * 128: ft * 256 + (do + 1) * 128],
                             hTf[:, ch * 5632 + ft * 512: ch * 5632 + (ft + 1) * 512],
                             start=(ft == 0), stop=(ft == 10),
                             reads=[sn, "hTf"], writes=["big6"], tag=f"dm{e_i}_{ch}_{dp}_{do}")
            for ch in range(2):
                for j in range(4):
                    tt = ch * 4 + j
                    for do in range(2):
                        p.dve(lambda e, o=sil[:, 0:128],
                              i_=big6[:, (ch * 2 + do) * 512 + j * 128:
                                      (ch * 2 + do) * 512 + (j + 1) * 128]:
                              e.tensor_copy(o, i_), reads=["big6"], writes=["sil"],
                              tag=f"de{e_i}_{ch}_{dp}_{do}_{j}")
                        p.tr(psT[:, do * 128:(do + 1) * 128], sil[:, 0:128], ident_bs,
                             reads=["sil", "identb"], writes=["psT"], tag=f"dt{e_i}_{ch}_{dp}_{do}_{j}")
                    p.dve(lambda e, o=x1[:, tt * DM + dp * 256: tt * DM + (dp + 1) * 256],
                          i_=psT[:, 0:256],
                          sci=tw_all[:, tt * NE + e_i: tt * NE + e_i + 1]:
                          e.scalar_tensor_tensor(o, i_, sci, o, ALU.mult, ALU.add),
                          reads=["psT", "tw", "x1"], writes=["x1"], tag=f"da{e_i}_{ch}_{dp}_{j}")
    for tt in range(NT):
        p.dma("sp", o_out[tt * 128:(tt + 1) * 128, :], x1[:, tt * DM:(tt + 1) * DM],
              reads=["x1"], tag=f"sx4{tt}")
    AB_.pop(m4b)


def build_launch_C(debug=False):
    nc = bass.Bass("TRN2", target_bir_lowering=False, debug=False)
    I = lambda n, s, d=F32: nc.dram_tensor(n, s, d, kind="ExternalInput").ap()
    O = lambda n, s, d=F32: nc.dram_tensor(n, s, d, kind="ExternalOutput").ap()

    x2in = I("x2in", [TL, DM])
    h3Tin = I("h3Tin", [DM, TL], BF16)
    dmp = I("dmp", [NCN, 128, H * 6 * 128])
    dnp = I("dnp", [NCN, 128, 6 * H])
    memL = I("memL", [128, H * 6 * 128])
    nrmL = I("nrmL", [128, 6 * H])
    WqT = I("WqT", [DM, DM])
    WgT = I("WgT", [DM, DM])
    bq_r = I("bq_r", [1, DM])
    bg_r = I("bg_r", [1, DM])
    gate_wT = I("gate_wT", [DM, NE])
    gateT = I("gateT", [NE, DM, FF])
    upT = I("upT", [NE, DM, FF])
    downT = I("downT", [NE, FF, DM])
    ident_b = I("ident_b", [128, 128], BF16)
    ones_row = I("ones_row", [1, 128], BF16)

    o_out = O("o_out", [TL, DM])

    p = Prog(nc)
    with ExitStack() as st:
        big6 = st.enter_context(nc.psum_tensor("big6", [128, 3584], F32))
        psT = st.enter_context(nc.psum_tensor("psT", [128, 1024], BF16))
        af_t = st.enter_context(nc.sbuf_tensor("af", [128, 26000], F32))
        ab_t = st.enter_context(nc.sbuf_tensor("ab", [128, 49500], BF16))
        AF_ = Arena(p, af_t, 26000)
        AB_ = Arena(p, ab_t, 49500)
        C = make_shared_consts(p, AF_, AB_, ident_b=ident_b, ones_row=ones_row)
        emit_phase_C(p, AF_, AB_, big6, psT, C, debug,
                     x2in, h3Tin, memL, nrmL, WqT, WgT, bq_r, bg_r,
                     gate_wT, gateT, upT, downT, o_out,
                     dmp=dmp, dnp=dnp, O=O)
        p.emit()
    return nc


def host_prep_C(inputs, resB):
    g = lambda k: np.asarray(inputs[k])
    mem = g("L0_attn_b_memory")
    memL = _c(mem.reshape(H, 6, 128, Dh).transpose(2, 0, 1, 3).reshape(128, H * 6 * 128))
    nrm = g("L0_attn_b_norm")
    nrmL = _c(nrm.reshape(4, 4, 6, 128).transpose(3, 0, 2, 1).reshape(128, 6 * H))
    dmp = np.stack([np.asarray(resB[c]["o_dmem"]) for c in range(NCN)])
    dnp = np.stack([np.asarray(resB[c]["o_dnorm"]) for c in range(NCN)])
    base = {
        "dmp": dmp, "dnp": dnp, "memL": memL, "nrmL": nrmL,
        "WqT": _c(g("L0_attn_b_Wq").T), "bq_r": _c(g("L0_attn_b_bq").reshape(1, -1)),
        "WgT": _c(g("L0_attn_b_Wg").T), "bg_r": _c(g("L0_attn_b_bg").reshape(1, -1)),
        "gate_wT": _c(g("L0_ffn_gate_w").T),
        "gateT": _c(np.swapaxes(g("L0_ffn_gate_proj"), 1, 2)),
        "upT": _c(np.swapaxes(g("L0_ffn_up_proj"), 1, 2)),
        "downT": _c(np.swapaxes(g("L0_ffn_down_proj"), 1, 2)),
        "ident_b": np.eye(128, dtype=BF), "ones_row": np.ones((1, 128), BF),
    }
    maps = []
    for c in range(NCN):
        m = dict(base)
        m["x2in"] = np.asarray(resB[c]["o_x2"])
        m["h3Tin"] = np.asarray(resB[c]["o_h3T"])
        maps.append(m)
    return maps


def run_C(inputs, resB, debug=False):
    nc = build_launch_C(debug=debug)
    maps = host_prep_C(inputs, resB)
    res = run_bass_kernel_spmd(nc, maps, core_ids=list(range(NCN)))
    return res.results



def build_fused(debug=False):
    nc = bass.Bass("TRN2", target_bir_lowering=False, debug=False)
    I = lambda n, s, d=F32: nc.dram_tensor(n, s, d, kind="ExternalInput").ap()
    O = lambda n, s, d=F32: nc.dram_tensor(n, s, d, kind="ExternalOutput").ap()
    D = lambda n, s, d=F32: nc.dram_tensor(n, s, d).ap()

    xin = I("xin", [TL, DM])
    wq = I("wq", [DM, DM])
    wk = I("wk", [DM, KVH * Dh])
    wv = I("wv", [DM, KVH * Dh])
    impw1 = I("impw1", [DM, 64])
    impb1 = I("impb1", [64, 1])
    impw2 = I("impw2", [64, 1])
    impb2 = I("impb2", [1, 1])
    qnw = I("qnw", [1, Dh])
    knw = I("knw", [1, Dh])
    qnw_roll = I("qnw_roll", [1, Dh])
    knw_roll = I("knw_roll", [1, Dh])
    sgnv = I("sgnv", [1, Dh])
    cos_l = I("cos_l", [TL, Dh])
    sin_l = I("sin_l", [TL, Dh])
    kbeT = I("kbeT", [256, SK])
    kab = I("kab", [4, 128, SK])
    wo = I("wo", [DM, DM])
    gate_wT = I("gate_wT", [DM, NE])
    gateT = I("gateT", [NE, DM, FF])
    upT = I("upT", [NE, DM, FF])
    downT = I("downT", [NE, FF, DM])
    WkT = I("WkT", [DM, KVH * Dh])
    WvT = I("WvT", [DM, KVH * Dh])
    WmbT = I("WmbT", [DM, DM])
    bk_r = I("bk_r", [1, KVH * Dh])
    bv_r = I("bv_r", [1, KVH * Dh])
    bmb_r = I("bmb_r", [1, DM])
    memory = I("memory", [H, FD, Dh])
    normT = I("normT", [FD, H])
    kab_full = I("kab_full", [32, 128, SK])
    ka_eL = I("ka_eL", [SK, 4])
    kb_e = I("kb_e", [SK, 256])
    qnw_col = I("qnw_col", [Dh, 1])
    knw_t4 = I("knw_t4", [1, KVH * Dh])
    ss_c = I("ss_c", [1, 1])
    memL = I("memL", [128, H * 6 * 128])
    nrmL = I("nrmL", [128, 6 * H])
    WqT = I("WqT", [DM, DM])
    WgT = I("WgT", [DM, DM])
    bq_r = I("bq_r", [1, DM])
    bg_r = I("bg_r", [1, DM])
    ident_b = I("ident_b", [128, 128], BF16)
    ones_row = I("ones_row", [1, 128], BF16)
    ones_col = I("ones_col", [128, 1], BF16)
    one16 = I("one16", [1, 1], BF16)
    iota_row = I("iota_row", [128, 128])
    iota_r1 = I("iota_r1", [1, 512])
    ltri = I("ltri", [128, 128], BF16)
    utri64 = I("utri64", [64, 64], BF16)
    sel8 = I("sel8", [64, NSB], BF16)
    rep8 = I("rep8", [64, GT], BF16)

    o_out = O("o_out", [TL, DM])

    b_lg = D("b_lg", [1, TL])
    g_lg = D("g_lg", [NCN, TL])
    b_kh = D("b_kh", [TL, KVH * Dh], BF16)
    g_kh = D("g_kh", [S, KVH * Dh], BF16)
    b_kr = D("b_kr", [TL, KVH * Dh], BF16)
    g_kr = D("g_kr", [S, KVH * Dh], BF16)
    b_v = D("b_v", [TL, KVH * Dh], BF16)
    g_v = D("g_v", [S, KVH * Dh], BF16)
    b_qsp = D("b_qsp", [SK, DM])
    g_qsp = D("g_qsp", [SK, DM])
    qrT_dram = D("qrT_f", [DM, TL], BF16)
    x2_dram = D("x2_dram", [TL, DM])
    h3T_dram = D("h3T_dram_f", [DM, TL], BF16)
    b_dm = D("b_dm", [128, H * 6 * 128])
    r_dm = D("r_dm", [128, H * 6 * 128])
    b_dn = D("b_dn", [128, 6 * H])
    r_dn = D("r_dn", [128, 6 * H])

    p = Prog(nc)
    with ExitStack() as st:
        big6 = st.enter_context(nc.psum_tensor("big6", [128, 3584], F32))
        psT = st.enter_context(nc.psum_tensor("psT", [128, 1024], BF16))
        af_t = st.enter_context(nc.sbuf_tensor("af", [128, 26000], F32))
        ab_t = st.enter_context(nc.sbuf_tensor("ab", [128, 49500], BF16))
        AF_ = Arena(p, af_t, 26000)
        AB_ = Arena(p, ab_t, 49500)
        C_ = make_shared_consts(p, AF_, AB_, ident_b=ident_b, ones_row=ones_row,
                                ones_col=ones_col, one16=one16, ltri=ltri,
                                utri64=utri64, sel8=sel8, rep8=rep8,
                                iota_row=iota_row, iota_r1=iota_r1)
        emit_phase_A(p, AF_, AB_, big6, psT, C_["cst"], C_["ident_bs"],
                     C_["onesr_sb"], C_["one16_sb"], C_["stat"], C_["dump"],
                     C_["lg_col"],
                     xin, wq, wk, wv, impw1, impb1, impw2, impb2,
                     qnw, knw, qnw_roll, knw_roll, sgnv, cos_l, sin_l, kbeT, kab,
                     b_lg, b_kh, b_kr, b_v, b_qsp, qrT_dram)
        p.cc("AllGather", ALU.bypass, b_lg, g_lg, reads=["blg"], writes=["glg"], tag="cclg")
        p.cc("AllGather", ALU.bypass, b_kr, g_kr, reads=["bkr"], writes=["gkr"], tag="cckr")
        p.cc("AllGather", ALU.bypass, b_kh, g_kh, reads=["bkh"], writes=["gkh"], tag="cckh")
        p.cc("AllGather", ALU.bypass, b_v, g_v, reads=["bv"], writes=["gv"], tag="ccv")
        p.cc("AllReduce", ALU.add, b_qsp, g_qsp, reads=["bqsp"], writes=["gqsp"], tag="ccqsp")

        def lgT_load(dst):
            p.dma("sp", dst.rearrange("p (r a) -> p r a", r=NCN),
                  g_lg.rearrange("r (a p) -> p r a", p=128),
                  reads=["glg"], writes=["lg"], tag="ldlg",
                  allow_slow_non_contiguous=True)

        def lgl_load(dst):
            p.dve(lambda e: e.tensor_copy(dst, C_["lg_col"]),
                  reads=["lgcol"], writes=["lgl"], tag="lglc")

        emit_phase_B(p, AF_, AB_, big6, psT, C_, debug,
                     xin, g_kr, g_kh, g_v, qrT_dram,
                     wo, gate_wT, gateT, upT, downT, WkT, WvT, WmbT, bk_r, bv_r, bmb_r,
                     memory, normT, kbeT, kab_full, ka_eL, kb_e, qnw_col, knw_t4, ss_c,
                     x2_dram, h3T_dram, b_dm, b_dn,
                     lgT_load, lgl_load, qspT=None, g_qsp=g_qsp, O=O)
        p.cc("AllReduce", ALU.add, b_dm, r_dm, reads=["bdm"], writes=["rdm"], tag="ccdm")
        p.cc("AllReduce", ALU.add, b_dn, r_dn, reads=["bdn"], writes=["rdn"], tag="ccdn")
        emit_phase_C(p, AF_, AB_, big6, psT, C_, debug,
                     x2_dram, h3T_dram, memL, nrmL, WqT, WgT, bq_r, bg_r,
                     gate_wT, gateT, upT, downT, o_out,
                     dmp=None, dnp=None, r_dm=r_dm, r_dn=r_dn, O=O)
        p.emit()
    return nc


def host_prep_fused(inputs):
    mapsA = host_prep_A(inputs)
    g = lambda k: np.asarray(inputs[k])
    mem = g("L0_attn_b_memory")
    memL = _c(mem.reshape(H, 6, 128, Dh).transpose(2, 0, 1, 3).reshape(128, H * 6 * 128))
    nrm = g("L0_attn_b_norm")
    nrmL = _c(nrm.reshape(4, 4, 6, 128).transpose(3, 0, 2, 1).reshape(128, 6 * H))
    ka = g("L0_attn_a_kron_a")
    kb = g("L0_attn_a_kron_b")
    ika = np.arange(SK) % 20
    ikb = np.arange(SK) // 20
    ka_expand = _c(ka.T[:, ika])
    kab_full = _c(np.broadcast_to(ka_expand[:32, None, :], (32, 128, SK)).astype(np.float32))
    kb_e = _c(kb[ikb])
    qnw = g("L0_attn_a_q_norm_w")
    knw = g("L0_attn_a_k_norm_w")
    k_i = np.arange(128)
    ltri = _c((k_i[:, None] <= k_i[None, :]).astype(BF))
    k64 = np.arange(64)
    utri64 = _c((k64[:, None] < k64[None, :]).astype(BF))
    tile_super = np.zeros(GT, np.int64)
    for sb, (st0, ntl, cap) in enumerate(SUP):
        tile_super[st0:st0 + ntl] = sb
    sel8 = np.zeros((64, NSB), BF)
    sel8[np.arange(GT), tile_super] = 1
    rep8 = np.zeros((64, GT), BF)
    first_tile = np.array([SUP[s][0] for s in tile_super])
    rep8[first_tile, np.arange(GT)] = 1
    shared = {
        "wo": g("L0_attn_a_wo"),
        "gate_wT": _c(g("L0_ffn_gate_w").T),
        "gateT": _c(np.swapaxes(g("L0_ffn_gate_proj"), 1, 2)),
        "upT": _c(np.swapaxes(g("L0_ffn_up_proj"), 1, 2)),
        "downT": _c(np.swapaxes(g("L0_ffn_down_proj"), 1, 2)),
        "WkT": _c(g("L0_attn_b_Wk").T), "WvT": _c(g("L0_attn_b_Wv").T),
        "WmbT": _c(g("L0_attn_b_Wmb").T),
        "bk_r": _c(g("L0_attn_b_bk").reshape(1, -1)),
        "bv_r": _c(g("L0_attn_b_bv").reshape(1, -1)),
        "bmb_r": _c(g("L0_attn_b_bmb").reshape(1, -1)),
        "memory": g("L0_attn_b_memory"), "normT": _c(g("L0_attn_b_norm").T),
        "kab_full": kab_full, "kb_e": kb_e,
        "qnw_col": _c(qnw.reshape(Dh, 1)),
        "knw_t4": _c(np.tile(knw, KVH).reshape(1, -1)),
        "ss_c": _c(np.asarray(inputs["L0_attn_a_sketch_scale"]).reshape(1, 1)),
        "memL": memL, "nrmL": nrmL,
        "WqT": _c(g("L0_attn_b_Wq").T), "bq_r": _c(g("L0_attn_b_bq").reshape(1, -1)),
        "WgT": _c(g("L0_attn_b_Wg").T), "bg_r": _c(g("L0_attn_b_bg").reshape(1, -1)),
        "ones_col": np.ones((128, 1), BF), "one16": np.ones((1, 1), BF),
        "iota_row": _c(np.broadcast_to(np.arange(128, dtype=np.float32)[None, :],
                                       (128, 128))),
        "iota_r1": _c(np.arange(512, dtype=np.float32).reshape(1, 512)),
        "ltri": ltri, "utri64": utri64, "sel8": sel8, "rep8": rep8,
    }
    maps = []
    for c in range(NCN):
        m = dict(mapsA[c])
        m.update(shared)
        m["ka_eL"] = _c(ka_expand[:32].T[:, 4 * c:4 * c + 4])
        maps.append(m)
    return maps


def run_fused(inputs, debug=False):
    nc = build_fused(debug=debug)
    maps = host_prep_fused(inputs)
    res = run_bass_kernel_spmd(nc, maps, core_ids=list(range(NCN)))
    return res.results


def kernel(**inputs):
    res = run_fused(inputs, debug=False)
    out = np.concatenate([np.asarray(res[c]["o_out"], np.float32) for c in range(NCN)], 0)
    return out.reshape(1, S, DM)


def kernel_3launch(**inputs):
    resA = run_A(inputs)
    resB = run_B(inputs, resA, debug=False)
    resC = run_C(inputs, resB, debug=False)
    out = np.concatenate([np.asarray(resC[c]["o_out"], np.float32) for c in range(NCN)], 0)
    return out.reshape(1, S, DM)
